# version 14
# baseline (speedup 1.0000x reference)
import contextlib
from functools import partial

import numpy as np

import concourse.bass as bass
import concourse.mybir as mybir
from concourse.bass import AP
from concourse.bass_utils import run_bass_kernel_spmd

F32 = mybir.dt.float32
BF16 = mybir.dt.bfloat16
AF = mybir.ActivationFunctionType
ALU = mybir.AluOpType
AXX = mybir.AxisListType.X

NCORES = 8
SEQS = 2
S = 2048
D = 1024
H = 16
NG = 4
GT = 512
NQC = 16
NE = 8
DFF = 4096
NB = 4
KSLOT = 6
EPS = 1e-5


def I_mm(eng, out=None, lhsT=None, rhs=None, start=True, stop=True):
    return eng.matmul(out, lhsT, rhs, start=start, stop=stop)

def I_act(eng, out=None, in_=None, func=None, bias=0.0, scale=1.0):
    return eng.activation(out, in_, func, bias=bias, scale=scale)

def I_dma(eng, out=None, in_=None):
    return eng.dma_start(out, in_)

def I_dmat(eng, out=None, in_=None):
    return eng.dma_start_transpose(out, in_)

def I_copy(eng, out=None, in_=None):
    return eng.tensor_copy(out, in_)

def I_tt(eng, out=None, a=None, b=None, op=None):
    return eng.tensor_tensor(out, a, b, op)

def I_ts(eng, out=None, in_=None, s1=None, s2=None, op0=None, op1=None):
    if op1 is None:
        return eng.tensor_single_scalar(out, in_, s1, op0)
    return eng.tensor_scalar(out, in_, s1, s2, op0, op1)

def I_stt(eng, out=None, in0=None, scal=None, in1=None, op0=None, op1=None):
    return eng.scalar_tensor_tensor(out, in0, scal, in1, op0, op1)

def I_red(eng, out=None, in_=None, op=None, negate=False):
    return eng.tensor_reduce(out, in_, AXX, op, negate=negate)

def I_recip(eng, out=None, in_=None):
    return eng.reciprocal(out, in_)

def I_memset(eng, ap=None, c=0.0):
    return eng.memset(ap, c)

def I_iota(eng, ap=None, pat=None, cm=0):
    return eng.iota(ap, pat, channel_multiplier=cm,
                    allow_small_or_imprecise_dtypes=True)


def dmax(*ds):
    out = {}
    for d in ds:
        if d:
            for k, v in d.items():
                if out.get(k, -1) < v:
                    out[k] = v
    return out


class Prog:

    ENGINES = ("tensor", "scalar", "vector", "gpsimd", "sync")

    def __init__(self, nc, sems, cnt):
        self.nc = nc
        self.sems = sems
        self.cnt = cnt
        self.q = {e: [] for e in self.ENGINES}

    COMPUTE_SEMS = ("pe", "act", "dve", "pool")

    def op(self, eng, fn, wait=None, inc=None):
        incs = inc or []
        w = dict(wait or {})
        for sname in list(w):
            if sname not in self.COMPUTE_SEMS:
                w[sname] = max(w[sname], self.cnt[sname])
        for sname, n in incs:
            self.cnt[sname] += n
        post = {sname: self.cnt[sname] for sname, _ in incs}
        self.q[eng].append((fn, w, incs))
        return post

    def drain(self):
        w = {s: self.cnt[s] for s in self.cnt
             if s not in self.COMPUTE_SEMS and self.cnt[s] > 0}
        self.op("gpsimd", None, wait=w)

    AUDIT = []

    def emit(self):
        Prog.AUDIT.append({e: list(self.q[e]) for e in self.ENGINES})
        with self.nc.Block() as block:
            for eng_name in self.ENGINES:
                ops = self.q[eng_name]
                if not ops:
                    continue

                def make_stream(ops=ops):
                    def stream(eng):
                        last_wait = {}
                        for fn, waits, incs in ops:
                            for sname, v in waits.items():
                                if last_wait.get(sname, -1) < v:
                                    eng.wait_ge(self.sems[sname], v)
                                    last_wait[sname] = v
                            ins = fn(eng) if fn is not None else None
                            for sname, n in incs:
                                ins.then_inc(self.sems[sname], n)
                    return stream

                getattr(block, eng_name)(make_stream())


def build_program():
    nc = bass.Bass("TRN2", target_bir_lowering=False)

    x_d = nc.dram_tensor("x", [SEQS * S, D], F32, kind="ExternalInput")
    ipw_d = nc.dram_tensor("in_proj_w", [3 * D, D], F32, kind="ExternalInput")
    ipb_d = nc.dram_tensor("in_proj_b", [3 * D], F32, kind="ExternalInput")
    opw_d = nc.dram_tensor("out_proj_w", [D, D], F32, kind="ExternalInput")
    opb_d = nc.dram_tensor("out_proj_b", [D], F32, kind="ExternalInput")
    rw_d = nc.dram_tensor("router_W", [D, NE], F32, kind="ExternalInput")
    rb_d = nc.dram_tensor("router_b", [NE], F32, kind="ExternalInput")
    w1_d = nc.dram_tensor("W1", [NE, D, DFF], F32, kind="ExternalInput")
    b1_d = nc.dram_tensor("b1", [NE, DFF], F32, kind="ExternalInput")
    w2_d = nc.dram_tensor("W2", [NE, DFF, D], F32, kind="ExternalInput")
    b2_d = nc.dram_tensor("b2", [NE, D], F32, kind="ExternalInput")
    y_d = nc.dram_tensor("y", [SEQS * S, D], F32, kind="ExternalOutput")

    wt_d = nc.dram_tensor("wt_scratch", [128, 8 * 4 * D], BF16, kind="Internal")
    x1_d = nc.dram_tensor("x1_scratch", [SEQS * S, D], F32, kind="Internal")
    beff_d = nc.dram_tensor("beff_scratch", [D], F32, kind="Internal")

    sem_names = ("pe", "act", "dve", "pool",
                 "su_b2", "su_rw", "su_rb", "su_bt", "su_btt", "su_b1",
                 "su_b1t", "su_bo", "su_be", "su_wts",
                 "ws0", "ws1", "wt0", "wt1",
                 "wq0", "wq1", "xg0", "xg1", "xt0", "xt1", "wot",
                 "of0", "of1", "of2", "x1t0", "x1t1", "gf0", "gf1",
                 "xf0", "xf1", "park", "accl", "wa0", "wa1", "wb0", "wb1",
                 "out")
    cnt = {sname: 0 for sname in sem_names}
    fr = {}
    rd = {}
    for i in range(8):
        fr[f"ps{i}"] = {}

    with contextlib.ExitStack() as top:
        E = top.enter_context
        sems = {sname: E(nc.semaphore(sname)) for sname in sem_names}

        ps = [E(nc.psum_tensor(f"ps{i}", [128, 512], F32)) for i in range(8)]
        beff_rep = E(nc.sbuf_tensor("beff_rep", [128, D], F32))
        rb_rep = E(nc.sbuf_tensor("rb_rep", [128, NE], F32))
        rw_sb = E(nc.sbuf_tensor("rw_sb", [128, 8 * NE], BF16))
        b2pad = E(nc.sbuf_tensor("b2pad", [128, D], BF16))
        bTf = E(nc.sbuf_tensor("bTf", [128, 32], F32))
        b1Tf = E(nc.sbuf_tensor("b1Tf", [128, 32 * NE], F32))
        maskL = E(nc.sbuf_tensor("maskL", [128, 128], BF16))
        maskR = E(nc.sbuf_tensor("maskR", [128, 128], BF16))
        ones_col = E(nc.sbuf_tensor("ones_col", [128, 1], BF16))
        eps_sb = E(nc.sbuf_tensor("eps_sb", [128, 1], F32))
        gates_tm = E(nc.sbuf_tensor("gates_tm", [128, NQC * NE], F32))
        gfm = E(nc.sbuf_tensor("gfm", [128, NQC * 128], BF16))
        statsb = [E(nc.sbuf_tensor(f"stats{i}", [128, 16], F32)) for i in range(2)]
        gscr = [E(nc.sbuf_tensor(f"gscr{i}", [128, 40], F32)) for i in range(2)]
        gpad = [E(nc.sbuf_tensor(f"gpad{i}", [128, 128], BF16)) for i in range(2)]

        with contextlib.ExitStack() as sc0:
            E0 = sc0.enter_context
            wT_full = E0(nc.sbuf_tensor("wT_full", [128, 8 * 4 * D], BF16))
            wstag = [E0(nc.sbuf_tensor(f"wstag{i}", [128, D], BF16)) for i in range(2)]
            btmp = E0(nc.sbuf_tensor("btmp", [32, 128], BF16))
            b1tmp = E0(nc.sbuf_tensor("b1tmp", [16, DFF], BF16))
            b1T_bf = E0(nc.sbuf_tensor("b1T_bf", [128, 32 * 16], BF16))
            bT_bf = E0(nc.sbuf_tensor("bT_bf", [128, 32], BF16))
            iop = E0(nc.sbuf_tensor("iop", [128, 128], F32))
            ioq = E0(nc.sbuf_tensor("ioq", [128, 128], F32))
            mtmp = E0(nc.sbuf_tensor("mtmp", [128, 128], F32))
            bout_row = E0(nc.sbuf_tensor("bout_row", [1, D], F32))
            beff_row = E0(nc.sbuf_tensor("beff_row", [1, D], F32))

            p = Prog(nc, sems, cnt)

            p.op("gpsimd", partial(I_iota, ap=iop[:, :], pat=[[0, 128]], cm=1))
            t_io = p.op("gpsimd", partial(I_iota, ap=ioq[:, :], pat=[[1, 128]], cm=0),
                        inc=[("pool", 1)])
            t = p.op("vector", partial(I_tt, out=mtmp[:, :], a=iop[:, :], b=ioq[:, :],
                                       op=ALU.is_ge), wait=t_io, inc=[("dve", 1)])
            t = p.op("vector", partial(I_copy, out=maskL[:, :], in_=mtmp[:, :]),
                     wait=t, inc=[("dve", 1)])
            t = p.op("vector", partial(I_tt, out=mtmp[:, :], a=iop[:, :], b=ioq[:, :],
                                       op=ALU.is_le), wait=dmax(t_io, t), inc=[("dve", 1)])
            t = p.op("vector", partial(I_copy, out=maskR[:, :], in_=mtmp[:, :]),
                     wait=t, inc=[("dve", 1)])
            rd["masks"] = t
            rd["ones"] = p.op("vector", partial(I_memset, ap=ones_col[:, :], c=1.0),
                              inc=[("dve", 1)])
            rd["eps"] = p.op("vector", partial(I_memset, ap=eps_sb[:, :], c=float(EPS)),
                             inc=[("dve", 1)])
            t_b2z = p.op("vector", partial(I_memset, ap=b2pad[:, :], c=0.0), inc=[("dve", 1)])
            t_btz = p.op("vector", partial(I_memset, ap=btmp[:, :], c=0.0), inc=[("dve", 1)])
            t_b1z = p.op("vector", partial(I_memset, ap=b1tmp[:, :], c=0.0), inc=[("dve", 1)])
            t_gp = {}
            for gi in range(2):
                t_gp = p.op("vector", partial(I_memset, ap=gpad[gi][:, :], c=0.0),
                            inc=[("dve", 1)])
                fr[f"gpad{gi}"] = {}
            rd["gpad_z"] = t_gp

            rd["b2pad"] = p.op("gpsimd", partial(I_dma, out=b2pad[0:NE, :], in_=b2_d.ap()),
                               wait=t_b2z, inc=[("su_b2", 16)])
            rd["rw"] = p.op("gpsimd", partial(
                I_dma, out=rw_sb[:, :].rearrange("p (c e) -> p c e", c=8),
                in_=rw_d.ap().rearrange("(c p) e -> p c e", p=128)), inc=[("su_rw", 16)])
            rd["rb"] = p.op("sync", partial(
                I_dma, out=rb_rep[:, :], in_=AP(rb_d, 0, [[0, 128], [1, NE]])),
                inc=[("su_rb", 16)])
            t = p.op("gpsimd", partial(
                I_dma, out=btmp[0:24, :],
                in_=ipb_d.ap().rearrange("(c p) -> c p", p=128)),
                wait=t_btz, inc=[("su_bt", 16)])
            t = p.op("sync", partial(I_dmat, out=bT_bf[:, :], in_=btmp[:, :]),
                     wait=t, inc=[("su_btt", 16)])
            rd["bT"] = p.op("vector", partial(I_copy, out=bTf[:, :], in_=bT_bf[:, :]),
                            wait=t, inc=[("dve", 1)])
            t_b1 = p.op("gpsimd", partial(I_dma, out=b1tmp[0:NE, :], in_=b1_d.ap()),
                        wait=t_b1z, inc=[("su_b1", 16)])
            t = p.op("sync", partial(
                I_dmat,
                out=b1T_bf[:, :].rearrange("p (c e) -> p c e", c=32),
                in_=b1tmp[:, :].rearrange("p (c f) -> p c f", c=32)),
                wait=t_b1, inc=[("su_b1t", 16)])
            rd["b1T"] = p.op("vector", partial(
                I_copy,
                out=b1Tf[:, :].rearrange("p (c e) -> p c e", e=NE),
                in_=b1T_bf[:, :].rearrange("p (c e) -> p c e", e=16)[:, :, 0:NE]),
                wait=t, inc=[("dve", 1)])
            t_bout = p.op("sync", partial(I_dma, out=bout_row[:, :], in_=opb_d.ap()[None, :]),
                          inc=[("su_bo", 16)])

            wst_fr = [{}, {}]
            t_tr = {}
            for oc in range(32):
                sl = oc % 2
                src = (ipw_d.ap()[oc * 128:(oc + 1) * 128, :] if oc < 24 else
                       opw_d.ap()[(oc - 24) * 128:(oc - 24) * 128 + 128, :])
                t_ld = p.op("gpsimd", partial(I_dma, out=wstag[sl][:, :], in_=src),
                            wait=wst_fr[sl], inc=[(f"ws{sl}", 16)])
                t_tr = p.op("sync", partial(
                    I_dmat,
                    out=wT_full[:, oc * 1024:(oc + 1) * 1024].rearrange(
                        "p (ic f) -> p ic f", ic=8),
                    in_=wstag[sl][:, :].rearrange("p (ic f) -> p ic f", ic=8)),
                    wait=t_ld, inc=[(f"wt{oc % 2}", 16)])
                wst_fr[sl] = t_tr
            rd["wT_full"] = {"wt0": cnt["wt0"], "wt1": cnt["wt1"]}
            rd["wt_dram"] = p.op("sync", partial(
                I_dma, out=wt_d.ap(), in_=wT_full[:, :]),
                wait=rd["wT_full"], inc=[("su_wts", 16)])

            t_dve = {}
            for half in range(2):
                bank = 6 + half
                t_pe = {}
                for j in range(4):
                    oc = 24 + half * 4 + j
                    for ic in range(8):
                        w = dmax(rd["wT_full"], rd["bT"]) if (j == 0 and ic == 0) else {}
                        t_pe = p.op("tensor", partial(
                            I_mm, out=ps[bank][0:1, j * 128:(j + 1) * 128],
                            lhsT=bT_bf[:, 16 + ic:17 + ic],
                            rhs=wT_full[:, oc * 1024 + ic * 128:
                                        oc * 1024 + ic * 128 + 128],
                            start=(ic == 0), stop=(ic == 7)),
                            wait=w,
                            inc=[("pe", 1)] if (j == 3 and ic == 7) else None)
                t_dve = p.op("vector", partial(
                    I_tt, out=beff_row[:, half * 512:(half + 1) * 512],
                    a=ps[bank][0:1, 0:512],
                    b=bout_row[:, half * 512:(half + 1) * 512], op=ALU.add),
                    wait=dmax(t_pe, t_bout), inc=[("dve", 1)])
                fr[f"ps{bank}"] = t_dve
            t = p.op("sync", partial(I_dma, out=beff_d.ap()[None, :], in_=beff_row[:, :]),
                     wait=t_dve, inc=[("su_be", 16)])
            rd["beff"] = p.op("sync", partial(
                I_dma, out=beff_rep[:, :], in_=AP(beff_d, 0, [[0, 128], [1, D]])),
                wait=t, inc=[("su_be", 16)])

            p.drain()
            p.emit()
        nc.all_engine_barrier()

        fr["gates"] = {}
        fr["gfm"] = {}
        for si in range(SEQS):
            with contextlib.ExitStack() as scs:
                Es = scs.enter_context
                X1b = Es(nc.sbuf_tensor(f"X1b_{si}", [128, 8 * S], BF16))

                sc_att = contextlib.ExitStack()
                with sc_att:
                    kfm = sc_att.enter_context(
                        nc.sbuf_tensor(f"kfm_{si}", [128, 8 * S], BF16))
                    vsb = sc_att.enter_context(
                        nc.sbuf_tensor(f"vsb_{si}", [128, NQC * 1040], BF16))
                    qg = sc_att.enter_context(
                        nc.sbuf_tensor(f"qg_{si}", [128, 8 * S], BF16))

                    with contextlib.ExitStack() as scq:
                        Eq = scq.enter_context
                        wqb = [Eq(nc.sbuf_tensor(f"wqb{i}_{si}", [128, 8 * 512], BF16))
                               for i in range(2)]
                        xgbf = [Eq(nc.sbuf_tensor(f"xgbf{i}_{si}", [128, 4 * D], BF16))
                                for i in range(2)]
                        xfm = [Eq(nc.sbuf_tensor(f"xfm{i}_{si}", [128, 8 * GT], BF16))
                               for i in range(2)]

                        p = Prog(nc, sems, cnt)
                        for i in range(2):
                            fr[f"wqb{i}"] = {}
                            fr[f"xgbf{i}"] = {}
                            fr[f"xfm{i}"] = {}

                        def sched_group_loads(g):
                            sl = g % 2
                            row0 = si * S + g * GT
                            t_ld = {}
                            for tc in range(4):
                                t_ld = p.op("gpsimd", partial(
                                    I_dma,
                                    out=AP(xgbf[sl], tc * 128,
                                           [[4096, 128], [512, 8], [1, 128]]),
                                    in_=AP(x_d, (row0 + tc * 128) * D,
                                           [[D, 128], [128, 8], [1, 128]])),
                                    wait=fr[f"xgbf{sl}"] if tc == 0 else {},
                                    inc=[(f"xg{sl}", 16)])
                            t_tr = p.op("sync", partial(
                                I_dmat,
                                out=xfm[sl][:, :].rearrange("p (b f) -> p b f", b=32),
                                in_=xgbf[sl][:, :].rearrange("p (b f) -> p b f", b=32)),
                                wait=dmax(t_ld, fr[f"xfm{sl}"]), inc=[(f"xt{sl}", 16)])
                            fr[f"xgbf{sl}"] = t_tr
                            rd[f"xfm{sl}"] = t_tr

                        t_vi = p.op("vector", partial(I_memset, ap=vsb[:, :], c=1.0),
                                    inc=[("dve", 1)])
                        rd["vsb_ones"] = t_vi

                        sched_group_loads(0)

                        for g in range(NG):
                            sl = g % 2
                            xfm_rd = rd[f"xfm{sl}"]
                            for og in range(6):
                                wsl = og % 2
                                t_w = p.op("gpsimd", partial(
                                    I_dma, out=wqb[wsl][:, :],
                                    in_=wt_d.ap()[:, og * 4096:(og + 1) * 4096]),
                                    wait=dmax(rd["wt_dram"], fr[f"wqb{wsl}"]),
                                    inc=[(f"wq{wsl}", 16)])
                                if og < 4:
                                    for j in range(4):
                                        oc = og * 4 + j
                                        bank = 6 + (j % 2)
                                        t_pe = {}
                                        for ic in range(8):
                                            w = dmax(t_w, xfm_rd, fr[f"ps{bank}"]) if ic == 0 else {}
                                            t_pe = p.op("tensor", partial(
                                                I_mm, out=ps[bank][:, :],
                                                lhsT=wqb[wsl][:, j * 1024 + ic * 128:
                                                              j * 1024 + ic * 128 + 128],
                                                rhs=xfm[sl][:, ic * GT:(ic + 1) * GT],
                                                start=(ic == 0), stop=(ic == 7)),
                                                wait=w, inc=[("pe", 1)] if ic == 7 else None)
                                        fr[f"wqb{wsl}"] = dmax(fr[f"wqb{wsl}"], t_pe)
                                        if oc < 8:
                                            t = p.op("scalar", partial(
                                                I_act,
                                                out=qg[:, oc * S + g * GT:
                                                        oc * S + g * GT + GT],
                                                in_=ps[bank][:, :], func=AF.Identity,
                                                bias=bTf[:, oc:oc + 1], scale=1.0),
                                                wait=dmax(t_pe, rd["bT"]),
                                                inc=[("act", 1)])
                                            fr[f"ps{bank}"] = t
                                        else:
                                            koc = oc - 8
                                            t = p.op("scalar", partial(
                                                I_act,
                                                out=kfm[:, koc * S + g * GT:
                                                        koc * S + g * GT + GT],
                                                in_=ps[bank][:, :], func=AF.Identity,
                                                bias=bTf[:, 8 + koc:9 + koc], scale=1.0),
                                                wait=dmax(t_pe, rd["bT"]),
                                                inc=[("act", 1)])
                                            fr[f"ps{bank}"] = t
                                else:
                                    vh = og - 4
                                    for tc in range(4):
                                        slot = g * 4 + tc
                                        bank = 6 + (tc % 2)
                                        t_pe = {}
                                        for jv in range(4):
                                            for ic in range(8):
                                                w = dmax(t_w, xfm_rd, fr[f"ps{bank}"]) \
                                                    if (jv == 0 and ic == 0) else {}
                                                t_pe = p.op("tensor", partial(
                                                    I_mm,
                                                    out=ps[bank][:, jv * 128:(jv + 1) * 128],
                                                    lhsT=xfm[sl][:, ic * GT + tc * 128:
                                                                 ic * GT + tc * 128 + 128],
                                                    rhs=wqb[wsl][:, jv * 1024 + ic * 128:
                                                                 jv * 1024 + ic * 128 + 128],
                                                    start=(ic == 0), stop=(ic == 7)),
                                                    wait=w,
                                                    inc=[("pe", 1)]
                                                    if (jv == 3 and ic == 7) else None)
                                        fr[f"wqb{wsl}"] = dmax(fr[f"wqb{wsl}"], t_pe)
                                        t = p.op("scalar", partial(
                                            I_act,
                                            out=AP(vsb, slot * 1040 + vh * 520,
                                                   [[1040 * NQC, 128], [65, 8],
                                                    [1, 64]]),
                                            in_=ps[bank][:, :].rearrange(
                                                "p (h c) -> p h c", h=8),
                                            func=AF.Identity,
                                            bias=0.0, scale=1.0),
                                            wait=dmax(t_pe, rd["vsb_ones"]),
                                            inc=[("act", 1)])
                                        fr[f"ps{bank}"] = t

                            fr[f"xfm{sl}"] = {"pe": cnt["pe"]}
                            if g + 1 < NG:
                                sched_group_loads(g + 1)

                        p.drain()
                        p.emit()
                    nc.all_engine_barrier()

                    with contextlib.ExitStack() as sca:
                        Ea = sca.enter_context
                        woT = Ea(nc.sbuf_tensor(f"woT_{si}", [128, 8 * D], BF16))
                        qpadb = [Ea(nc.sbuf_tensor(f"qpad{i}_{si}", [128, H * 128], BF16))
                                 for i in range(2)]
                        atb = [Ea(nc.sbuf_tensor(f"atb{i}_{si}", [128, 512], BF16))
                               for i in range(4)]
                        fcols = Ea(nc.sbuf_tensor(f"fcols_{si}", [128, 16], F32))
                        otm = [Ea(nc.sbuf_tensor(f"otm{i}_{si}", [128, D], BF16))
                               for i in range(2)]
                        ofm = [Ea(nc.sbuf_tensor(f"ofm{i}_{si}", [128, D], BF16))
                               for i in range(3)]
                        xbf_ = [Ea(nc.sbuf_tensor(f"xb{i}_{si}", [128, D], F32))
                                for i in range(3)]
                        x1pre = Ea(nc.sbuf_tensor(f"x1pre_{si}", [128, D], F32))
                        sqv = Ea(nc.sbuf_tensor(f"sqv_{si}", [128, D], F32))
                        x1f = [Ea(nc.sbuf_tensor(f"x1f{i}_{si}", [128, D], F32))
                               for i in range(1)]
                        x1btm = [Ea(nc.sbuf_tensor(f"x1btm{i}_{si}", [128, D], BF16))
                                 for i in range(2)]

                        p = Prog(nc, sems, cnt)
                        for i in range(3):
                            for nm in ("ofm", "xb"):
                                fr.setdefault(f"{nm}{i}", {})
                        for i in range(2):
                            for nm in ("otm", "x1f", "x1btm", "qpadb",
                                       "stats", "gscr"):
                                fr.setdefault(f"{nm}{i}", {})
                        for i in range(4):
                            fr[f"atb{i}"] = {}
                        fr["x1pre"] = {}
                        fr["sqv"] = {}

                        for icw in range(8):
                            rd["woT"] = p.op("gpsimd", partial(
                                I_dma,
                                out=woT[:, icw * 1024:(icw + 1) * 1024],
                                in_=wt_d.ap()[:, 24 * 1024 + icw * 128:32 * 1024]
                                .rearrange("p (oc f) -> p oc f", f=128)[:, ::8, :]
                                if False else
                                AP(wt_d, 24 * 1024 + icw * 128,
                                   [[32768, 128], [1024, 8], [1, 128]])),
                                wait=rd["wt_dram"], inc=[("wot", 16)])
                        t_qz = {}
                        for i in range(2):
                            t_qz = p.op("vector", partial(I_memset, ap=qpadb[i][:, :], c=0.0),
                                        inc=[("dve", 1)])
                        rd["qpad_z"] = t_qz

                        ctx = {}

                        qprep_t = {}
                        ast = {}
                        bctx = {}
                        b2ctx = {}

                        def sched_qprep(qc):
                            if qc >= NQC:
                                return
                            qsl = qc % 2
                            t_qp = {}
                            for h in range(H):
                                r0 = 0 if h % 2 == 0 else 64
                                t_qp = p.op("vector", partial(
                                    I_copy,
                                    out=qpadb[qsl][r0:r0 + 64, h * 128:(h + 1) * 128],
                                    in_=qg[r0:r0 + 64,
                                           (h // 2) * S + qc * 128:
                                           (h // 2) * S + qc * 128 + 128]),
                                    wait=dmax(rd["qpad_z"], fr[f"qpadb{qsl}"])
                                    if h == 0 else {},
                                    inc=[("dve", 1)])
                            qprep_t[qc] = t_qp

                        def sched_xload(qc):
                            qs4 = qc % 3
                            row0 = si * S + qc * 128
                            ctx[qc] = p.op("scalar", partial(
                                I_dma, out=xbf_[qs4][:, :],
                                in_=x_d.ap()[row0:row0 + 128, :]),
                                wait=fr[f"xb{qs4}"], inc=[(f"xf{qc % 2}", 16)])

                        LAG = 3

                        def stage_a1(qc):
                            qsl = qc % 2
                            qprep = qprep_t.pop(qc)

                            nkc = min(qc + 3, NQC)
                            kcs = list(range(nkc))
                            blocks = [kcs[i:i + 4] for i in range(0, nkc, 4)]
                            nblk = len(blocks)
                            has_mask = (qc + 2 <= NQC - 1)

                            units = [(h, bi) for h in range(H) for bi in range(nblk)]
                            st_post = {}
                            msk_post = {}
                            av_post = [None] * H

                            def sched_st(u):
                                h, bi = units[u]
                                blk = blocks[bi]
                                bank = u % 2
                                t_pe = {}
                                for j, kc in enumerate(blk):
                                    w = dmax(fr[f"ps{bank}"], qprep) if j == 0 else {}
                                    t_pe = p.op("tensor", partial(
                                        I_mm, out=ps[bank][:, j * 128:(j + 1) * 128],
                                        lhsT=kfm[:, (h // 2) * S + kc * 128:
                                                  (h // 2) * S + kc * 128 + 128],
                                        rhs=qpadb[qsl][:, h * 128:(h + 1) * 128],
                                        start=True, stop=True),
                                        wait=w,
                                        inc=[("pe", 1)] if j == len(blk) - 1 else None)
                                st_post[u] = t_pe

                            def sched_exp_mask(u):
                                h, bi = units[u]
                                blk = blocks[bi]
                                bank = u % 2
                                asl = u % 4
                                t = p.op("scalar", partial(
                                    I_act, out=atb[asl][:, 0:len(blk) * 128],
                                    in_=ps[bank][:, 0:len(blk) * 128], func=AF.Exp,
                                    bias=0.0, scale=0.125),
                                    wait=dmax(st_post[u], fr[f"atb{asl}"]),
                                    inc=[("act", 1)])
                                fr[f"ps{bank}"] = t
                                if has_mask and bi == nblk - 1:
                                    j = len(blk) - 1
                                    t = p.op("vector", partial(
                                        I_tt, out=atb[asl][:, j * 128:(j + 1) * 128],
                                        a=atb[asl][:, j * 128:(j + 1) * 128],
                                        b=maskR[:, :], op=ALU.mult),
                                        wait=dmax(t, rd["masks"]), inc=[("dve", 1)])
                                msk_post[u] = t

                            def sched_av(u):
                                h, bi = units[u]
                                blk = blocks[bi]
                                asl = u % 4
                                obank = 2 + (h // 7 if h < 14 else 2)
                                idx = h - (0, 7, 14)[obank - 2]
                                first = (bi == 0)
                                last = (bi == nblk - 1)
                                t_pe = {}
                                for j, kc in enumerate(blk):
                                    w = {}
                                    if j == 0:
                                        w = dmax(msk_post[u],
                                                 (fr[f"ps{obank}"] if (idx == 0 and first)
                                                  else {}))
                                    t_pe = p.op("tensor", partial(
                                        I_mm,
                                        out=ps[obank][:, idx * 65:idx * 65 + 65],
                                        lhsT=atb[asl][:, j * 128:(j + 1) * 128],
                                        rhs=vsb[:, kc * 1040 + h * 65:
                                                kc * 1040 + h * 65 + 65],
                                        start=(first and j == 0),
                                        stop=(last and j == len(blk) - 1)),
                                        wait=w,
                                        inc=[("pe", 1)] if j == len(blk) - 1 else None)
                                fr[f"atb{asl}"] = t_pe
                                if last:
                                    av_post[h] = t_pe

                            S_ = dict(units=units, st=sched_st, em=sched_exp_mask,
                                      av=sched_av, av_post=av_post)
                            ast[qc] = S_
                            K = min(16, len(units))
                            for u in range(K):
                                sched_st(u)
                                sched_exp_mask(u)
                                if u >= LAG:
                                    sched_av(u - LAG)
                            S_["next"] = K

                        def stage_a2(qc):
                            qsl = qc % 2
                            S_ = ast.pop(qc)
                            units = S_["units"]
                            sched_st, sched_exp_mask, sched_av = \
                                S_["st"], S_["em"], S_["av"]
                            av_post = S_["av_post"]
                            for u in range(S_["next"], len(units)):
                                sched_st(u)
                                sched_exp_mask(u)
                                if u >= LAG:
                                    sched_av(u - LAG)
                            for u in range(max(0, len(units) - LAG), len(units)):
                                sched_av(u)
                            fr[f"qpadb{qsl}"] = {"pe": cnt["pe"]}

                            rr = [None] * 3
                            for b in range(3):
                                nh = (7, 7, 2)[b]
                                h0 = (0, 7, 14)[b]
                                rr[b] = p.op("vector", partial(
                                    I_recip, out=fcols[:, h0:h0 + nh],
                                    in_=AP(ps[2 + b], 64, [[512, 128], [65, nh]])),
                                    wait=av_post[h0 + nh - 1], inc=[("dve", 1)])
                            t_ev = {}
                            for h in range(H):
                                b = 0 if h < 7 else (1 if h < 14 else 2)
                                idx = h - (0, 7, 14)[b]
                                t_ev = p.op("vector", partial(
                                    I_ts, out=otm[qsl][:, h * 64:(h + 1) * 64],
                                    in_=ps[2 + b][:, idx * 65:idx * 65 + 64],
                                    s1=fcols[:, h:h + 1], op0=ALU.mult),
                                    wait=dmax(rr[b],
                                              fr[f"otm{qsl}"] if h == 0 else {}),
                                    inc=[("dve", 1)])
                            fr["ps2"] = t_ev
                            fr["ps3"] = t_ev
                            fr["ps4"] = t_ev
                            rd[f"otm{qsl}"] = t_ev

                        def stage_tr(qc):
                            qsl = qc % 2
                            qs4 = qc % 3
                            t_tr = p.op("sync", partial(
                                I_dmat,
                                out=ofm[qs4][:, :].rearrange("p (ic f) -> p ic f", ic=8),
                                in_=otm[qsl][:, :].rearrange("p (ic f) -> p ic f", ic=8)),
                                wait=dmax(rd[f"otm{qsl}"], fr[f"ofm{qs4}"]),
                                inc=[(f"of{qs4}", 16)])
                            fr[f"otm{qsl}"] = t_tr
                            rd[f"ofm{qs4}"] = t_tr

                        def stage_b1_pe(qc):
                            qs4 = qc % 3
                            bst = {}
                            for half in range(2):
                                bank = 6 + half
                                t_pe = {}
                                for ic in range(8):
                                    w = dmax(rd[f"ofm{qs4}"], rd["woT"],
                                             fr[f"ps{bank}"]) if ic == 0 else {}
                                    t_pe = p.op("tensor", partial(
                                        I_mm, out=ps[bank][:, :],
                                        lhsT=ofm[qs4][:, ic * 128:(ic + 1) * 128],
                                        rhs=woT[:, ic * D + half * 512:
                                                ic * D + half * 512 + 512],
                                        start=(ic == 0), stop=(ic == 7)),
                                        wait=w, inc=[("pe", 1)] if ic == 7 else None)
                                bst[half] = t_pe
                            fr[f"ofm{qs4}"] = {"pe": cnt["pe"]}
                            bctx[qc] = bst

                        def stage_b1_mid(qc):
                            qsl = qc % 2
                            qs4 = qc % 3
                            bst = bctx[qc]
                            t_xb = p.op("vector", partial(
                                I_tt, out=xbf_[qs4][:, :], a=xbf_[qs4][:, :],
                                b=beff_rep[:, :], op=ALU.add),
                                wait=dmax(ctx.pop(qc), rd["beff"]), inc=[("dve", 1)])
                            t_ln_in = {}
                            for half in range(2):
                                bank = 6 + half
                                t_ln_in = p.op("vector", partial(
                                    I_tt, out=x1pre[:, half * 512:(half + 1) * 512],
                                    a=ps[bank][:, :],
                                    b=xbf_[qs4][:, half * 512:(half + 1) * 512],
                                    op=ALU.add),
                                    wait=dmax(bst[half], t_xb,
                                              fr["x1pre"] if half == 0 else {}),
                                    inc=[("dve", 1)])
                                fr[f"ps{bank}"] = t_ln_in
                            fr[f"xb{qs4}"] = t_ln_in
                            rd["x1pre"] = t_ln_in

                            st_ = statsb[qsl]
                            t = p.op("vector", partial(
                                I_red, out=st_[:, 0:1], in_=x1pre[:, :],
                                op=ALU.add, negate=True),
                                wait=dmax(t_ln_in, fr[f"stats{qsl}"]), inc=[("dve", 1)])
                            t = p.op("vector", partial(
                                I_ts, out=st_[:, 1:2], in_=st_[:, 0:1],
                                s1=1.0 / D, op0=ALU.mult), wait=t, inc=[("dve", 1)])
                            t_sq = p.op("vector", partial(
                                I_tt, out=sqv[:, :], a=x1pre[:, :],
                                b=x1pre[:, :], op=ALU.mult),
                                wait=dmax(t_ln_in, fr["sqv"]), inc=[("dve", 1)])
                            t_ss = p.op("vector", partial(
                                I_red, out=st_[:, 2:3], in_=sqv[:, :],
                                op=ALU.add), wait=t_sq, inc=[("dve", 1)])
                            fr["sqv"] = t_ss
                            t_m2 = p.op("vector", partial(
                                I_tt, out=st_[:, 3:4], a=st_[:, 1:2], b=st_[:, 1:2],
                                op=ALU.mult), wait=t, inc=[("dve", 1)])
                            t_var = p.op("vector", partial(
                                I_stt, out=st_[:, 4:5], in0=st_[:, 2:3], scal=1.0 / D,
                                in1=st_[:, 3:4], op0=ALU.mult, op1=ALU.subtract),
                                wait=dmax(t_ss, t_m2), inc=[("dve", 1)])
                            bst["var"] = t_var

                        def stage_b1_tail(qc):
                            qsl = qc % 2
                            row0 = si * S + qc * 128
                            bst = bctx.pop(qc)
                            st_ = statsb[qsl]
                            t_ln1 = p.op("scalar", partial(
                                I_act, out=st_[:, 5:6], in_=st_[:, 4:5], func=AF.Ln,
                                bias=eps_sb[:, :], scale=1.0),
                                wait=dmax(bst["var"], rd["eps"]), inc=[("act", 1)])
                            t_rstd = p.op("scalar", partial(
                                I_act, out=st_[:, 6:7], in_=st_[:, 5:6], func=AF.Exp,
                                bias=0.0, scale=-0.5), wait=t_ln1, inc=[("act", 1)])
                            t_bln = p.op("vector", partial(
                                I_tt, out=st_[:, 7:8], a=st_[:, 1:2], b=st_[:, 6:7],
                                op=ALU.mult), wait=t_rstd, inc=[("dve", 1)])
                            t_x1 = p.op("scalar", partial(
                                I_act, out=x1f[0][:, :], in_=x1pre[:, :],
                                func=AF.Identity, bias=st_[:, 7:8],
                                scale=st_[:, 6:7]),
                                wait=dmax(t_bln, fr["x1f0"]), inc=[("act", 1)])
                            fr["x1pre"] = t_x1
                            fr[f"stats{qsl}"] = t_x1
                            t_x1b = p.op("vector", partial(
                                I_copy, out=x1btm[qsl][:, :], in_=x1f[0][:, :]),
                                wait=dmax(t_x1, fr[f"x1btm{qsl}"]), inc=[("dve", 1)])
                            t_park = p.op("scalar", partial(
                                I_dma, out=x1_d.ap()[row0:row0 + 128, :],
                                in_=x1f[0][:, :]), wait=t_x1, inc=[("park", 16)])
                            fr["x1f0"] = t_park
                            rd[f"x1park{qc}"] = t_park
                            t_tr = {}
                            for ic in range(8):
                                t_tr = p.op("sync", partial(
                                    I_dmat,
                                    out=X1b[:, ic * S + qc * 128: ic * S + qc * 128 + 128],
                                    in_=x1btm[qsl][:, ic * 128:(ic + 1) * 128]),
                                    wait=t_x1b, inc=[(f"x1t{qsl}", 16)])
                            fr[f"x1btm{qsl}"] = t_tr
                            rd[f"X1b{qc}"] = t_tr

                        def stage_b2_pe(qc):
                            t_pe = {}
                            for ic in range(8):
                                w = dmax(rd[f"X1b{qc}"], rd["rw"],
                                         fr["ps5"]) if ic == 0 else {}
                                t_pe = p.op("tensor", partial(
                                    I_mm, out=ps[5][:, 0:NE],
                                    lhsT=X1b[:, ic * S + qc * 128: ic * S + qc * 128 + 128],
                                    rhs=rw_sb[:, ic * NE:(ic + 1) * NE],
                                    start=(ic == 0), stop=(ic == 7)),
                                    wait=w, inc=[("pe", 1)] if ic == 7 else None)
                            b2ctx[qc] = t_pe

                        def stage_b2_rest(qc):
                            qsl = qc % 2
                            gs = gscr[qsl]
                            t_pe = b2ctx.pop(qc)
                            t = p.op("vector", partial(
                                I_tt, out=gs[:, 0:8], a=ps[5][:, 0:NE], b=rb_rep[:, :],
                                op=ALU.add),
                                wait=dmax(t_pe, rd["rb"], fr[f"gscr{qsl}"]),
                                inc=[("dve", 1)])
                            fr["ps5"] = t
                            t_nm = p.op("vector", partial(
                                I_red, out=gs[:, 8:9], in_=gs[:, 0:8], op=ALU.max,
                                negate=True), wait=t, inc=[("dve", 1)])
                            t_mk = p.op("vector", partial(
                                I_ts, out=gs[:, 16:24], in_=gs[:, 0:8],
                                s1=gs[:, 8:9], s2=0.0, op0=ALU.add, op1=ALU.is_ge),
                                wait=t_nm, inc=[("dve", 1)])
                            t_e = p.op("scalar", partial(
                                I_act, out=gs[:, 24:32], in_=gs[:, 0:8], func=AF.Exp,
                                bias=gs[:, 8:9], scale=1.0), wait=t_nm, inc=[("act", 1)])
                            t_se = p.op("vector", partial(
                                I_red, out=gs[:, 9:10], in_=gs[:, 24:32], op=ALU.add),
                                wait=t_e, inc=[("dve", 1)])
                            t_rse = p.op("vector", partial(
                                I_recip, out=gs[:, 10:11], in_=gs[:, 9:10]),
                                wait=t_se, inc=[("dve", 1)])
                            t_gm = p.op("vector", partial(
                                I_tt, out=gs[:, 32:40], a=gs[:, 24:32], b=gs[:, 16:24],
                                op=ALU.mult), wait=dmax(t_mk, t_e), inc=[("dve", 1)])
                            t_g = p.op("vector", partial(
                                I_ts, out=gates_tm[:, qc * NE:(qc + 1) * NE],
                                in_=gs[:, 32:40], s1=gs[:, 10:11], op0=ALU.mult),
                                wait=dmax(t_gm, t_rse), inc=[("dve", 1)])
                            rd[f"gates{qc}"] = t_g
                            gsl = qc % 2
                            t_gb = p.op("vector", partial(
                                I_copy, out=gpad[gsl][:, 0:NE],
                                in_=gates_tm[:, qc * NE:(qc + 1) * NE]),
                                wait=dmax(t_g, rd["gpad_z"], fr[f"gpad{gsl}"]),
                                inc=[("dve", 1)])
                            t_gt = p.op("sync", partial(
                                I_dmat, out=gfm[:, qc * 128:(qc + 1) * 128],
                                in_=gpad[gsl][:, :]),
                                wait=t_gb, inc=[(f"gf{gsl}", 16)])
                            fr[f"gpad{gsl}"] = t_gt
                            fr[f"gscr{qsl}"] = t_gt
                            rd[f"gfm{qc}"] = t_gt

                        sched_qprep(0)
                        sched_qprep(1)
                        sched_xload(0)
                        sched_xload(1)
                        stage_a1(0)
                        stage_a2(0)
                        stage_a1(1)
                        stage_a2(1)
                        sched_qprep(2)
                        stage_tr(0)
                        stage_tr(1)
                        for it in range(NQC + 2):
                            if 0 <= it - 1 < NQC:
                                stage_b1_pe(it - 1)
                            if 0 <= it - 2 < NQC:
                                stage_b2_pe(it - 2)
                            if it + 2 < NQC:
                                stage_a1(it + 2)
                            if 0 <= it - 1 < NQC:
                                stage_b1_mid(it - 1)
                            if it + 3 < NQC:
                                sched_qprep(it + 3)
                            if 0 <= it - 1 < NQC:
                                stage_b1_tail(it - 1)
                            if 0 <= it - 2 < NQC:
                                stage_b2_rest(it - 2)
                            if it + 2 < NQC:
                                stage_a2(it + 2)
                                sched_xload(it + 2)
                                stage_tr(it + 2)

                        p.drain()
                        p.emit()
                    nc.all_engine_barrier()

                with contextlib.ExitStack() as scf:
                    Ef = scf.enter_context
                    acc = Ef(nc.sbuf_tensor(f"acc_{si}", [128, NQC * D], F32))
                    w1b = [Ef(nc.sbuf_tensor(f"w1b{i}_{si}", [128, 8 * 1024], BF16))
                           for i in range(2)]
                    w2b = [Ef(nc.sbuf_tensor(f"w2b{i}_{si}", [128, 8 * 1024], BF16))
                           for i in range(2)]
                    hsc = [Ef(nc.sbuf_tensor(f"hsc{i}_{si}", [128, 8 * 512], BF16))
                           for i in range(2)]
                    outsb = [Ef(nc.sbuf_tensor(f"outsb{i}_{si}", [128, D], F32))
                             for i in range(2)]

                    p = Prog(nc, sems, cnt)
                    for i in range(2):
                        fr[f"w1b{i}"] = {}
                        fr[f"w2b{i}"] = {}
                        fr[f"hsc{i}"] = {}
                        fr[f"outsb{i}"] = {}

                    t_l = {}
                    for qc in range(NQC):
                        row0 = si * S + qc * 128
                        t_l = p.op("scalar", partial(
                            I_dma, out=acc[:, qc * D:(qc + 1) * D],
                            in_=x1_d.ap()[row0:row0 + 128, :]),
                            wait=rd[f"x1park{qc}"], inc=[("accl", 16)])
                    for qc in range(NQC):
                        for half in range(2):
                            bank = 6 + half
                            t_pe = p.op("tensor", partial(
                                I_mm, out=ps[bank][:, :],
                                lhsT=gfm[:, qc * 128:(qc + 1) * 128],
                                rhs=b2pad[:, half * 512:(half + 1) * 512],
                                start=True, stop=True),
                                wait=dmax(rd[f"gfm{qc}"], rd["b2pad"], fr[f"ps{bank}"]),
                                inc=[("pe", 1)])
                            t = p.op("vector", partial(
                                I_stt, out=acc[:, qc * D + half * 512:
                                               qc * D + half * 512 + 512],
                                in0=ps[bank][:, :], scal=1.0,
                                in1=acc[:, qc * D + half * 512:
                                        qc * D + half * 512 + 512],
                                op0=ALU.mult, op1=ALU.add),
                                wait=dmax(t_pe, t_l), inc=[("dve", 1)])
                            fr[f"ps{bank}"] = t

                    def sched_wload(e, b, wsl):
                        t1 = p.op("gpsimd", partial(
                            I_dma, out=w1b[wsl][:, :].rearrange("p (c f) -> p c f", c=8),
                            in_=w1_d.ap()[e, :, b * 1024:(b + 1) * 1024].rearrange(
                                "(c p) f -> p c f", p=128)),
                            wait=fr[f"w1b{wsl}"], inc=[(f"wa{wsl}", 16)])
                        t2 = p.op("gpsimd", partial(
                            I_dma, out=w2b[wsl][:, :].rearrange("p (m o) -> p m o", m=8),
                            in_=w2_d.ap()[e, b * 1024:(b + 1) * 1024, :].rearrange(
                                "(m p) o -> p m o", p=128)),
                            wait=fr[f"w2b{wsl}"], inc=[(f"wb{wsl}", 16)])
                        return t1, t2

                    eb_list = [(e, b) for e in range(NE) for b in range(NB)]
                    w_rd = {}
                    w_rd[0] = sched_wload(*eb_list[0], 0)
                    for idx, (e, b) in enumerate(eb_list):
                        wsl = idx % 2
                        if idx + 1 < len(eb_list):
                            w_rd[idx + 1] = sched_wload(*eb_list[idx + 1], (idx + 1) % 2)
                        t_w1, t_w2 = w_rd[idx]
                        for tt in range(4):
                            hsl = tt % 2
                            h_done = {}
                            for m in range(8):
                                bank = m % 2
                                t_pe = {}
                                for dc in range(8):
                                    w = dmax(t_w1, fr[f"ps{bank}"],
                                             rd[f"X1b{tt * 4}"], rd[f"X1b{tt * 4 + 1}"],
                                             rd[f"X1b{tt * 4 + 2}"], rd[f"X1b{tt * 4 + 3}"],
                                             fr[f"hsc{hsl}"] if m == 0 else {}) \
                                        if dc == 0 else {}
                                    t_pe = p.op("tensor", partial(
                                        I_mm, out=ps[bank][:, :],
                                        lhsT=w1b[wsl][:, dc * 1024 + m * 128:
                                                      dc * 1024 + m * 128 + 128],
                                        rhs=X1b[:, dc * S + tt * 512:
                                                dc * S + tt * 512 + 512],
                                        start=(dc == 0), stop=(dc == 7)),
                                        wait=w, inc=[("pe", 1)] if dc == 7 else None)
                                t_h = p.op("scalar", partial(
                                    I_act, out=hsc[hsl][:, m * 512:(m + 1) * 512],
                                    in_=ps[bank][:, :], func=AF.Relu,
                                    bias=b1Tf[:, (b * 8 + m) * NE + e:
                                              (b * 8 + m) * NE + e + 1],
                                    scale=1.0),
                                    wait=dmax(t_pe, rd["b1T"]), inc=[("act", 1)])
                                fr[f"ps{bank}"] = t_h
                                h_done = t_h
                            rd[f"hsc{hsl}"] = h_done
                            last_y = {}
                            for ts_ in range(4):
                                qc = tt * 4 + ts_
                                for half in range(2):
                                    bank = 2 + (ts_ * 2 + half) % 3
                                    t_pe = {}
                                    for m in range(8):
                                        w = dmax(rd[f"hsc{hsl}"], t_w2,
                                                 fr[f"ps{bank}"]) if m == 0 else {}
                                        t_pe = p.op("tensor", partial(
                                            I_mm, out=ps[bank][:, :],
                                            lhsT=hsc[hsl][:, m * 512 + ts_ * 128:
                                                          m * 512 + ts_ * 128 + 128],
                                            rhs=w2b[wsl][:, m * 1024 + half * 512:
                                                         m * 1024 + half * 512 + 512],
                                            start=(m == 0), stop=(m == 7)),
                                            wait=w, inc=[("pe", 1)] if m == 7 else None)
                                    t_y = p.op("vector", partial(
                                        I_stt,
                                        out=acc[:, qc * D + half * 512:
                                                qc * D + half * 512 + 512],
                                        in0=ps[bank][:, :],
                                        scal=gates_tm[:, qc * NE + e:qc * NE + e + 1],
                                        in1=acc[:, qc * D + half * 512:
                                                qc * D + half * 512 + 512],
                                        op0=ALU.mult, op1=ALU.add),
                                        wait=dmax(t_pe, rd[f"gates{qc}"]),
                                        inc=[("dve", 1)])
                                    fr[f"ps{bank}"] = t_y
                                    last_y = t_y
                            fr[f"hsc{hsl}"] = last_y
                        fr[f"w1b{wsl}"] = last_y
                        fr[f"w2b{wsl}"] = last_y

                    acc_done = last_y
                    t_out = {}
                    for qc in range(NQC):
                        qsl = qc % 2
                        st_ = statsb[qsl]
                        a_ap = acc[:, qc * D:(qc + 1) * D]
                        t = p.op("vector", partial(
                            I_red, out=st_[:, 0:1], in_=a_ap, op=ALU.add, negate=True),
                            wait=dmax(acc_done, fr[f"stats{qsl}"]), inc=[("dve", 1)])
                        t = p.op("vector", partial(
                            I_ts, out=st_[:, 1:2], in_=st_[:, 0:1], s1=1.0 / D,
                            op0=ALU.mult), wait=t, inc=[("dve", 1)])
                        t_sq = p.op("vector", partial(
                            I_tt, out=outsb[qsl][:, :], a=a_ap, b=a_ap, op=ALU.mult),
                            wait=dmax(t, fr[f"outsb{qsl}"]), inc=[("dve", 1)])
                        t_ss = p.op("vector", partial(
                            I_red, out=st_[:, 2:3], in_=outsb[qsl][:, :], op=ALU.add),
                            wait=t_sq, inc=[("dve", 1)])
                        t_m2 = p.op("vector", partial(
                            I_tt, out=st_[:, 3:4], a=st_[:, 1:2], b=st_[:, 1:2],
                            op=ALU.mult), wait=t, inc=[("dve", 1)])
                        t_var = p.op("vector", partial(
                            I_stt, out=st_[:, 4:5], in0=st_[:, 2:3], scal=1.0 / D,
                            in1=st_[:, 3:4], op0=ALU.mult, op1=ALU.subtract),
                            wait=dmax(t_ss, t_m2), inc=[("dve", 1)])
                        t_ln = p.op("scalar", partial(
                            I_act, out=st_[:, 5:6], in_=st_[:, 4:5], func=AF.Ln,
                            bias=eps_sb[:, :], scale=1.0),
                            wait=dmax(t_var, rd["eps"]), inc=[("act", 1)])
                        t_rstd = p.op("scalar", partial(
                            I_act, out=st_[:, 6:7], in_=st_[:, 5:6], func=AF.Exp,
                            bias=0.0, scale=-0.5), wait=t_ln, inc=[("act", 1)])
                        t_bln = p.op("vector", partial(
                            I_tt, out=st_[:, 7:8], a=st_[:, 1:2], b=st_[:, 6:7],
                            op=ALU.mult), wait=t_rstd, inc=[("dve", 1)])
                        t_x2 = p.op("scalar", partial(
                            I_act, out=outsb[qsl][:, :], in_=a_ap, func=AF.Identity,
                            bias=st_[:, 7:8], scale=st_[:, 6:7]),
                            wait=t_bln, inc=[("act", 1)])
                        fr[f"stats{qsl}"] = t_x2
                        row0 = si * S + qc * 128
                        t_out = p.op("sync", partial(
                            I_dma, out=y_d.ap()[row0:row0 + 128, :],
                            in_=outsb[qsl][:, :]), wait=t_x2, inc=[("out", 16)])
                        fr[f"outsb{qsl}"] = t_out

                    if si == SEQS - 1:
                        p.op("gpsimd", None, wait={"out": cnt["out"]})
                    p.drain()
                    p.emit()
                nc.all_engine_barrier()

    return nc


_NC_CACHE = []


def kernel(**inputs):
    x = np.ascontiguousarray(np.asarray(inputs["L0_attn_x"], dtype=np.float32))
    full = {
        "in_proj_w": np.ascontiguousarray(np.asarray(inputs["L0_attn_in_proj_w"], np.float32)),
        "in_proj_b": np.ascontiguousarray(np.asarray(inputs["L0_attn_in_proj_b"], np.float32)),
        "out_proj_w": np.ascontiguousarray(np.asarray(inputs["L0_attn_out_proj_w"], np.float32)),
        "out_proj_b": np.ascontiguousarray(np.asarray(inputs["L0_attn_out_proj_b"], np.float32)),
        "router_W": np.ascontiguousarray(np.asarray(inputs["L0_ffn_router_W"], np.float32)),
        "router_b": np.ascontiguousarray(np.asarray(inputs["L0_ffn_router_b"], np.float32)),
        "W1": np.ascontiguousarray(np.asarray(inputs["L0_ffn_W1"], np.float32)),
        "b1": np.ascontiguousarray(np.asarray(inputs["L0_ffn_b1"], np.float32)),
        "W2": np.ascontiguousarray(np.asarray(inputs["L0_ffn_W2"], np.float32)),
        "b2": np.ascontiguousarray(np.asarray(inputs["L0_ffn_b2"], np.float32)),
    }
    if not _NC_CACHE:
        _NC_CACHE.append(build_program())
    nc = _NC_CACHE[0]
    in_maps = []
    for c in range(NCORES):
        m = {"x": x[SEQS * c:SEQS * (c + 1)].reshape(SEQS * S, D)}
        m.update(full)
        in_maps.append(m)
    res = run_bass_kernel_spmd(nc, in_maps, core_ids=list(range(NCORES)))
    out = np.concatenate(
        [res.results[c]["y"].reshape(SEQS, S, D) for c in range(NCORES)], axis=0)
    return out


if __name__ == "__main__":
    nc = build_program()
    print("instructions:", nc.n_instructions())


# version 15
# speedup vs baseline: 1.0842x; 1.0842x over previous
import contextlib
from functools import partial

import numpy as np

import concourse.bass as bass
import concourse.mybir as mybir
from concourse.bass import AP
from concourse.bass_utils import run_bass_kernel_spmd

F32 = mybir.dt.float32
BF16 = mybir.dt.bfloat16
AF = mybir.ActivationFunctionType
ALU = mybir.AluOpType
AXX = mybir.AxisListType.X

NCORES = 8
SEQS = 2
S = 2048
D = 1024
H = 16
NG = 4
GT = 512
NQC = 16
NE = 8
DFF = 4096
NB = 4
KSLOT = 6
EPS = 1e-5


def I_mm(eng, out=None, lhsT=None, rhs=None, start=True, stop=True):
    return eng.matmul(out, lhsT, rhs, start=start, stop=stop)

def I_act(eng, out=None, in_=None, func=None, bias=0.0, scale=1.0):
    return eng.activation(out, in_, func, bias=bias, scale=scale)

def I_dma(eng, out=None, in_=None):
    return eng.dma_start(out, in_)

def I_dmat(eng, out=None, in_=None):
    return eng.dma_start_transpose(out, in_)

def I_copy(eng, out=None, in_=None):
    return eng.tensor_copy(out, in_)

def I_tt(eng, out=None, a=None, b=None, op=None):
    return eng.tensor_tensor(out, a, b, op)

def I_ts(eng, out=None, in_=None, s1=None, s2=None, op0=None, op1=None):
    if op1 is None:
        return eng.tensor_single_scalar(out, in_, s1, op0)
    return eng.tensor_scalar(out, in_, s1, s2, op0, op1)

def I_stt(eng, out=None, in0=None, scal=None, in1=None, op0=None, op1=None):
    return eng.scalar_tensor_tensor(out, in0, scal, in1, op0, op1)

def I_red(eng, out=None, in_=None, op=None, negate=False):
    return eng.tensor_reduce(out, in_, AXX, op, negate=negate)

def I_recip(eng, out=None, in_=None):
    return eng.reciprocal(out, in_)

def I_memset(eng, ap=None, c=0.0):
    return eng.memset(ap, c)

def I_iota(eng, ap=None, pat=None, cm=0):
    return eng.iota(ap, pat, channel_multiplier=cm,
                    allow_small_or_imprecise_dtypes=True)


def dmax(*ds):
    out = {}
    for d in ds:
        if d:
            for k, v in d.items():
                if out.get(k, -1) < v:
                    out[k] = v
    return out


class Prog:

    ENGINES = ("tensor", "scalar", "vector", "gpsimd", "sync")

    def __init__(self, nc, sems, cnt):
        self.nc = nc
        self.sems = sems
        self.cnt = cnt
        self.q = {e: [] for e in self.ENGINES}

    COMPUTE_SEMS = ("pe", "act", "dve", "pool")

    def op(self, eng, fn, wait=None, inc=None):
        incs = inc or []
        w = dict(wait or {})
        for sname in list(w):
            if sname not in self.COMPUTE_SEMS:
                w[sname] = max(w[sname], self.cnt[sname])
        for sname, n in incs:
            self.cnt[sname] += n
        post = {sname: self.cnt[sname] for sname, _ in incs}
        self.q[eng].append((fn, w, incs))
        return post

    def drain(self):
        w = {s: self.cnt[s] for s in self.cnt
             if s not in self.COMPUTE_SEMS and self.cnt[s] > 0}
        self.op("gpsimd", None, wait=w)

    AUDIT = []

    def emit(self):
        Prog.AUDIT.append({e: list(self.q[e]) for e in self.ENGINES})
        with self.nc.Block() as block:
            for eng_name in self.ENGINES:
                ops = self.q[eng_name]
                if not ops:
                    continue

                def make_stream(ops=ops):
                    def stream(eng):
                        last_wait = {}
                        for fn, waits, incs in ops:
                            for sname, v in waits.items():
                                if last_wait.get(sname, -1) < v:
                                    eng.wait_ge(self.sems[sname], v)
                                    last_wait[sname] = v
                            ins = fn(eng) if fn is not None else None
                            for sname, n in incs:
                                ins.then_inc(self.sems[sname], n)
                    return stream

                getattr(block, eng_name)(make_stream())


def build_program():
    nc = bass.Bass("TRN2", target_bir_lowering=False)

    x_d = nc.dram_tensor("x", [SEQS * S, D], F32, kind="ExternalInput")
    ipw_d = nc.dram_tensor("in_proj_w", [3 * D, D], F32, kind="ExternalInput")
    ipb_d = nc.dram_tensor("in_proj_b", [3 * D], F32, kind="ExternalInput")
    opw_d = nc.dram_tensor("out_proj_w", [D, D], F32, kind="ExternalInput")
    opb_d = nc.dram_tensor("out_proj_b", [D], F32, kind="ExternalInput")
    rw_d = nc.dram_tensor("router_W", [D, NE], F32, kind="ExternalInput")
    rb_d = nc.dram_tensor("router_b", [NE], F32, kind="ExternalInput")
    w1_d = nc.dram_tensor("W1", [NE, D, DFF], F32, kind="ExternalInput")
    b1_d = nc.dram_tensor("b1", [NE, DFF], F32, kind="ExternalInput")
    w2_d = nc.dram_tensor("W2", [NE, DFF, D], F32, kind="ExternalInput")
    b2_d = nc.dram_tensor("b2", [NE, D], F32, kind="ExternalInput")
    y_d = nc.dram_tensor("y", [SEQS * S, D], F32, kind="ExternalOutput")

    wt_d = nc.dram_tensor("wt_scratch", [128, 8 * 4 * D], BF16, kind="Internal")
    x1_d = nc.dram_tensor("x1_scratch", [SEQS * S, D], F32, kind="Internal")
    beff_d = nc.dram_tensor("beff_scratch", [D], F32, kind="Internal")

    sem_names = ("pe", "act", "dve", "pool",
                 "su_b2", "su_rw", "su_rb", "su_bt", "su_btt", "su_b1",
                 "su_b1t", "su_bo", "su_be", "su_wts",
                 "ws0", "ws1", "wt0", "wt1",
                 "wq0", "wq1", "xg0", "xg1", "xt0", "xt1", "wot",
                 "of0", "of1", "of2", "x1t0", "x1t1", "gf0", "gf1",
                 "xf0", "xf1", "park0", "park1", "accl", "wa0", "wa1", "wb0", "wb1",
                 "out")
    cnt = {sname: 0 for sname in sem_names}
    fr = {}
    rd = {}
    for i in range(8):
        fr[f"ps{i}"] = {}

    with contextlib.ExitStack() as top:
        E = top.enter_context
        sems = {sname: E(nc.semaphore(sname)) for sname in sem_names}

        ps = [E(nc.psum_tensor(f"ps{i}", [128, 512], F32)) for i in range(8)]
        beff_rep = E(nc.sbuf_tensor("beff_rep", [128, D], F32))
        rb_rep = E(nc.sbuf_tensor("rb_rep", [128, NE], F32))
        rw_sb = E(nc.sbuf_tensor("rw_sb", [128, 8 * NE], BF16))
        b2pad = E(nc.sbuf_tensor("b2pad", [128, D], BF16))
        bTf = E(nc.sbuf_tensor("bTf", [128, 32], F32))
        b1Tf = E(nc.sbuf_tensor("b1Tf", [128, 32 * NE], F32))
        maskL = E(nc.sbuf_tensor("maskL", [128, 128], BF16))
        maskR = E(nc.sbuf_tensor("maskR", [128, 128], BF16))
        ones_col = E(nc.sbuf_tensor("ones_col", [128, 1], BF16))
        eps_sb = E(nc.sbuf_tensor("eps_sb", [128, 1], F32))
        gates_tm = E(nc.sbuf_tensor("gates_tm", [128, NQC * NE], F32))
        gfm = E(nc.sbuf_tensor("gfm", [128, NQC * 128], BF16))
        statsb = [E(nc.sbuf_tensor(f"stats{i}", [128, 16], F32)) for i in range(2)]
        gscr = [E(nc.sbuf_tensor(f"gscr{i}", [128, 40], F32)) for i in range(2)]
        gpad = [E(nc.sbuf_tensor(f"gpad{i}", [128, 128], BF16)) for i in range(2)]

        with contextlib.ExitStack() as sc0:
            E0 = sc0.enter_context
            wT_full = E0(nc.sbuf_tensor("wT_full", [128, 8 * 4 * D], BF16))
            wstag = [E0(nc.sbuf_tensor(f"wstag{i}", [128, D], BF16)) for i in range(2)]
            btmp = E0(nc.sbuf_tensor("btmp", [32, 128], BF16))
            b1tmp = E0(nc.sbuf_tensor("b1tmp", [16, DFF], BF16))
            b1T_bf = E0(nc.sbuf_tensor("b1T_bf", [128, 32 * 16], BF16))
            bT_bf = E0(nc.sbuf_tensor("bT_bf", [128, 32], BF16))
            iop = E0(nc.sbuf_tensor("iop", [128, 128], F32))
            ioq = E0(nc.sbuf_tensor("ioq", [128, 128], F32))
            mtmp = E0(nc.sbuf_tensor("mtmp", [128, 128], F32))
            bout_row = E0(nc.sbuf_tensor("bout_row", [1, D], F32))
            beff_row = E0(nc.sbuf_tensor("beff_row", [1, D], F32))

            p = Prog(nc, sems, cnt)

            p.op("gpsimd", partial(I_iota, ap=iop[:, :], pat=[[0, 128]], cm=1))
            t_io = p.op("gpsimd", partial(I_iota, ap=ioq[:, :], pat=[[1, 128]], cm=0),
                        inc=[("pool", 1)])
            t = p.op("vector", partial(I_tt, out=mtmp[:, :], a=iop[:, :], b=ioq[:, :],
                                       op=ALU.is_ge), wait=t_io, inc=[("dve", 1)])
            t = p.op("vector", partial(I_copy, out=maskL[:, :], in_=mtmp[:, :]),
                     wait=t, inc=[("dve", 1)])
            t = p.op("vector", partial(I_tt, out=mtmp[:, :], a=iop[:, :], b=ioq[:, :],
                                       op=ALU.is_le), wait=dmax(t_io, t), inc=[("dve", 1)])
            t = p.op("vector", partial(I_copy, out=maskR[:, :], in_=mtmp[:, :]),
                     wait=t, inc=[("dve", 1)])
            rd["masks"] = t
            rd["ones"] = p.op("vector", partial(I_memset, ap=ones_col[:, :], c=1.0),
                              inc=[("dve", 1)])
            rd["eps"] = p.op("vector", partial(I_memset, ap=eps_sb[:, :], c=float(EPS)),
                             inc=[("dve", 1)])
            t_b2z = p.op("vector", partial(I_memset, ap=b2pad[:, :], c=0.0), inc=[("dve", 1)])
            t_btz = p.op("vector", partial(I_memset, ap=btmp[:, :], c=0.0), inc=[("dve", 1)])
            t_b1z = p.op("vector", partial(I_memset, ap=b1tmp[:, :], c=0.0), inc=[("dve", 1)])
            t_gp = {}
            for gi in range(2):
                t_gp = p.op("vector", partial(I_memset, ap=gpad[gi][:, :], c=0.0),
                            inc=[("dve", 1)])
                fr[f"gpad{gi}"] = {}
            rd["gpad_z"] = t_gp

            rd["b2pad"] = p.op("gpsimd", partial(I_dma, out=b2pad[0:NE, :], in_=b2_d.ap()),
                               wait=t_b2z, inc=[("su_b2", 16)])
            rd["rw"] = p.op("gpsimd", partial(
                I_dma, out=rw_sb[:, :].rearrange("p (c e) -> p c e", c=8),
                in_=rw_d.ap().rearrange("(c p) e -> p c e", p=128)), inc=[("su_rw", 16)])
            rd["rb"] = p.op("sync", partial(
                I_dma, out=rb_rep[:, :], in_=AP(rb_d, 0, [[0, 128], [1, NE]])),
                inc=[("su_rb", 16)])
            t = p.op("gpsimd", partial(
                I_dma, out=btmp[0:24, :],
                in_=ipb_d.ap().rearrange("(c p) -> c p", p=128)),
                wait=t_btz, inc=[("su_bt", 16)])
            t = p.op("sync", partial(I_dmat, out=bT_bf[:, :], in_=btmp[:, :]),
                     wait=t, inc=[("su_btt", 16)])
            rd["bT"] = p.op("vector", partial(I_copy, out=bTf[:, :], in_=bT_bf[:, :]),
                            wait=t, inc=[("dve", 1)])
            t_b1 = p.op("gpsimd", partial(I_dma, out=b1tmp[0:NE, :], in_=b1_d.ap()),
                        wait=t_b1z, inc=[("su_b1", 16)])
            t = p.op("sync", partial(
                I_dmat,
                out=b1T_bf[:, :].rearrange("p (c e) -> p c e", c=32),
                in_=b1tmp[:, :].rearrange("p (c f) -> p c f", c=32)),
                wait=t_b1, inc=[("su_b1t", 16)])
            rd["b1T"] = p.op("vector", partial(
                I_copy,
                out=b1Tf[:, :].rearrange("p (c e) -> p c e", e=NE),
                in_=b1T_bf[:, :].rearrange("p (c e) -> p c e", e=16)[:, :, 0:NE]),
                wait=t, inc=[("dve", 1)])
            t_bout = p.op("sync", partial(I_dma, out=bout_row[:, :], in_=opb_d.ap()[None, :]),
                          inc=[("su_bo", 16)])

            wst_fr = [{}, {}]
            t_tr = {}
            for oc in range(32):
                sl = oc % 2
                src = (ipw_d.ap()[oc * 128:(oc + 1) * 128, :] if oc < 24 else
                       opw_d.ap()[(oc - 24) * 128:(oc - 24) * 128 + 128, :])
                t_ld = p.op("gpsimd", partial(I_dma, out=wstag[sl][:, :], in_=src),
                            wait=wst_fr[sl], inc=[(f"ws{sl}", 16)])
                t_tr = p.op("sync", partial(
                    I_dmat,
                    out=wT_full[:, oc * 1024:(oc + 1) * 1024].rearrange(
                        "p (ic f) -> p ic f", ic=8),
                    in_=wstag[sl][:, :].rearrange("p (ic f) -> p ic f", ic=8)),
                    wait=t_ld, inc=[(f"wt{oc % 2}", 16)])
                wst_fr[sl] = t_tr
            rd["wT_full"] = {"wt0": cnt["wt0"], "wt1": cnt["wt1"]}
            rd["wt_dram"] = p.op("sync", partial(
                I_dma, out=wt_d.ap(), in_=wT_full[:, :]),
                wait=rd["wT_full"], inc=[("su_wts", 16)])

            t_dve = {}
            for half in range(2):
                bank = 6 + half
                t_pe = {}
                for j in range(4):
                    oc = 24 + half * 4 + j
                    for ic in range(8):
                        w = dmax(rd["wT_full"], rd["bT"]) if (j == 0 and ic == 0) else {}
                        t_pe = p.op("tensor", partial(
                            I_mm, out=ps[bank][0:1, j * 128:(j + 1) * 128],
                            lhsT=bT_bf[:, 16 + ic:17 + ic],
                            rhs=wT_full[:, oc * 1024 + ic * 128:
                                        oc * 1024 + ic * 128 + 128],
                            start=(ic == 0), stop=(ic == 7)),
                            wait=w,
                            inc=[("pe", 1)] if (j == 3 and ic == 7) else None)
                t_dve = p.op("vector", partial(
                    I_tt, out=beff_row[:, half * 512:(half + 1) * 512],
                    a=ps[bank][0:1, 0:512],
                    b=bout_row[:, half * 512:(half + 1) * 512], op=ALU.add),
                    wait=dmax(t_pe, t_bout), inc=[("dve", 1)])
                fr[f"ps{bank}"] = t_dve
            t = p.op("sync", partial(I_dma, out=beff_d.ap()[None, :], in_=beff_row[:, :]),
                     wait=t_dve, inc=[("su_be", 16)])
            rd["beff"] = p.op("sync", partial(
                I_dma, out=beff_rep[:, :], in_=AP(beff_d, 0, [[0, 128], [1, D]])),
                wait=t, inc=[("su_be", 16)])

            p.drain()
            p.emit()
        nc.all_engine_barrier()

        fr["gates"] = {}
        fr["gfm"] = {}
        for si in range(SEQS):
            with contextlib.ExitStack() as scs:
                Es = scs.enter_context
                X1b = Es(nc.sbuf_tensor(f"X1b_{si}", [128, 8 * S], BF16))

                sc_att = contextlib.ExitStack()
                with sc_att:
                    kfm = sc_att.enter_context(
                        nc.sbuf_tensor(f"kfm_{si}", [128, 8 * S], BF16))
                    vsb = sc_att.enter_context(
                        nc.sbuf_tensor(f"vsb_{si}", [128, NQC * 1040], BF16))
                    qg = sc_att.enter_context(
                        nc.sbuf_tensor(f"qg_{si}", [128, 8 * S], BF16))

                    with contextlib.ExitStack() as scq:
                        Eq = scq.enter_context
                        wqb = [Eq(nc.sbuf_tensor(f"wqb{i}_{si}", [128, 8 * 512], BF16))
                               for i in range(2)]
                        xgbf = [Eq(nc.sbuf_tensor(f"xgbf{i}_{si}", [128, 4 * D], BF16))
                                for i in range(2)]
                        xfm = [Eq(nc.sbuf_tensor(f"xfm{i}_{si}", [128, 8 * GT], BF16))
                               for i in range(2)]

                        p = Prog(nc, sems, cnt)
                        for i in range(2):
                            fr[f"wqb{i}"] = {}
                            fr[f"xgbf{i}"] = {}
                            fr[f"xfm{i}"] = {}

                        def sched_group_loads(g):
                            sl = g % 2
                            row0 = si * S + g * GT
                            t_ld = {}
                            for tc in range(4):
                                t_ld = p.op("gpsimd", partial(
                                    I_dma,
                                    out=AP(xgbf[sl], tc * 128,
                                           [[4096, 128], [512, 8], [1, 128]]),
                                    in_=AP(x_d, (row0 + tc * 128) * D,
                                           [[D, 128], [128, 8], [1, 128]])),
                                    wait=fr[f"xgbf{sl}"] if tc == 0 else {},
                                    inc=[(f"xg{sl}", 16)])
                            t_tr = p.op("sync", partial(
                                I_dmat,
                                out=xfm[sl][:, :].rearrange("p (b f) -> p b f", b=32),
                                in_=xgbf[sl][:, :].rearrange("p (b f) -> p b f", b=32)),
                                wait=dmax(t_ld, fr[f"xfm{sl}"]), inc=[(f"xt{sl}", 16)])
                            fr[f"xgbf{sl}"] = t_tr
                            rd[f"xfm{sl}"] = t_tr

                        t_vi = p.op("vector", partial(I_memset, ap=vsb[:, :], c=1.0),
                                    inc=[("dve", 1)])
                        rd["vsb_ones"] = t_vi

                        sched_group_loads(0)

                        for g in range(NG):
                            sl = g % 2
                            xfm_rd = rd[f"xfm{sl}"]
                            for og in range(6):
                                wsl = og % 2
                                t_w = p.op("gpsimd", partial(
                                    I_dma, out=wqb[wsl][:, :],
                                    in_=wt_d.ap()[:, og * 4096:(og + 1) * 4096]),
                                    wait=dmax(rd["wt_dram"], fr[f"wqb{wsl}"]),
                                    inc=[(f"wq{wsl}", 16)])
                                if og < 4:
                                    for j in range(4):
                                        oc = og * 4 + j
                                        bank = 6 + (j % 2)
                                        t_pe = {}
                                        for ic in range(8):
                                            w = dmax(t_w, xfm_rd, fr[f"ps{bank}"]) if ic == 0 else {}
                                            t_pe = p.op("tensor", partial(
                                                I_mm, out=ps[bank][:, :],
                                                lhsT=wqb[wsl][:, j * 1024 + ic * 128:
                                                              j * 1024 + ic * 128 + 128],
                                                rhs=xfm[sl][:, ic * GT:(ic + 1) * GT],
                                                start=(ic == 0), stop=(ic == 7)),
                                                wait=w, inc=[("pe", 1)] if ic == 7 else None)
                                        fr[f"wqb{wsl}"] = dmax(fr[f"wqb{wsl}"], t_pe)
                                        if oc < 8:
                                            t = p.op("scalar", partial(
                                                I_act,
                                                out=qg[:, oc * S + g * GT:
                                                        oc * S + g * GT + GT],
                                                in_=ps[bank][:, :], func=AF.Identity,
                                                bias=bTf[:, oc:oc + 1], scale=1.0),
                                                wait=dmax(t_pe, rd["bT"]),
                                                inc=[("act", 1)])
                                            fr[f"ps{bank}"] = t
                                        else:
                                            koc = oc - 8
                                            t = p.op("scalar", partial(
                                                I_act,
                                                out=kfm[:, koc * S + g * GT:
                                                        koc * S + g * GT + GT],
                                                in_=ps[bank][:, :], func=AF.Identity,
                                                bias=bTf[:, 8 + koc:9 + koc], scale=1.0),
                                                wait=dmax(t_pe, rd["bT"]),
                                                inc=[("act", 1)])
                                            fr[f"ps{bank}"] = t
                                else:
                                    vh = og - 4
                                    for tc in range(4):
                                        slot = g * 4 + tc
                                        bank = 6 + (tc % 2)
                                        t_pe = {}
                                        for jv in range(4):
                                            for ic in range(8):
                                                w = dmax(t_w, xfm_rd, fr[f"ps{bank}"]) \
                                                    if (jv == 0 and ic == 0) else {}
                                                t_pe = p.op("tensor", partial(
                                                    I_mm,
                                                    out=ps[bank][:, jv * 128:(jv + 1) * 128],
                                                    lhsT=xfm[sl][:, ic * GT + tc * 128:
                                                                 ic * GT + tc * 128 + 128],
                                                    rhs=wqb[wsl][:, jv * 1024 + ic * 128:
                                                                 jv * 1024 + ic * 128 + 128],
                                                    start=(ic == 0), stop=(ic == 7)),
                                                    wait=w,
                                                    inc=[("pe", 1)]
                                                    if (jv == 3 and ic == 7) else None)
                                        fr[f"wqb{wsl}"] = dmax(fr[f"wqb{wsl}"], t_pe)
                                        t = p.op("scalar", partial(
                                            I_act,
                                            out=AP(vsb, slot * 1040 + vh * 520,
                                                   [[1040 * NQC, 128], [65, 8],
                                                    [1, 64]]),
                                            in_=ps[bank][:, :].rearrange(
                                                "p (h c) -> p h c", h=8),
                                            func=AF.Identity,
                                            bias=0.0, scale=1.0),
                                            wait=dmax(t_pe, rd["vsb_ones"]),
                                            inc=[("act", 1)])
                                        fr[f"ps{bank}"] = t

                            fr[f"xfm{sl}"] = {"pe": cnt["pe"]}
                            if g + 1 < NG:
                                sched_group_loads(g + 1)

                        p.drain()
                        p.emit()
                    nc.all_engine_barrier()

                    with contextlib.ExitStack() as sca:
                        Ea = sca.enter_context
                        woT = Ea(nc.sbuf_tensor(f"woT_{si}", [128, 8 * D], BF16))
                        qpadb = [Ea(nc.sbuf_tensor(f"qpad{i}_{si}", [128, H * 128], BF16))
                                 for i in range(2)]
                        atb = [Ea(nc.sbuf_tensor(f"atb{i}_{si}", [128, 512], BF16))
                               for i in range(4)]
                        fcols = Ea(nc.sbuf_tensor(f"fcols_{si}", [128, 16], F32))
                        otm = [Ea(nc.sbuf_tensor(f"otm{i}_{si}", [128, D], BF16))
                               for i in range(2)]
                        ofm = [Ea(nc.sbuf_tensor(f"ofm{i}_{si}", [128, D], BF16))
                               for i in range(3)]
                        xbf_ = [Ea(nc.sbuf_tensor(f"xb{i}_{si}", [128, D], F32))
                                for i in range(3)]
                        x1pre = [Ea(nc.sbuf_tensor(f"x1pre{i}_{si}", [128, D], F32))
                                 for i in range(2)]
                        sqv = Ea(nc.sbuf_tensor(f"sqv_{si}", [128, D], F32))
                        x1btm = [Ea(nc.sbuf_tensor(f"x1btm{i}_{si}", [128, D], BF16))
                                 for i in range(2)]

                        p = Prog(nc, sems, cnt)
                        for i in range(3):
                            for nm in ("ofm", "xb"):
                                fr.setdefault(f"{nm}{i}", {})
                        for i in range(2):
                            for nm in ("otm", "x1pre", "x1btm", "qpadb",
                                       "stats", "gscr"):
                                fr.setdefault(f"{nm}{i}", {})
                        for i in range(4):
                            fr[f"atb{i}"] = {}
                        fr["sqv"] = {}

                        for icw in range(8):
                            rd["woT"] = p.op("gpsimd", partial(
                                I_dma,
                                out=woT[:, icw * 1024:(icw + 1) * 1024],
                                in_=wt_d.ap()[:, 24 * 1024 + icw * 128:32 * 1024]
                                .rearrange("p (oc f) -> p oc f", f=128)[:, ::8, :]
                                if False else
                                AP(wt_d, 24 * 1024 + icw * 128,
                                   [[32768, 128], [1024, 8], [1, 128]])),
                                wait=rd["wt_dram"], inc=[("wot", 16)])
                        t_qz = {}
                        for i in range(2):
                            t_qz = p.op("vector", partial(I_memset, ap=qpadb[i][:, :], c=0.0),
                                        inc=[("dve", 1)])
                        rd["qpad_z"] = t_qz

                        ctx = {}

                        qprep_t = {}
                        ast = {}
                        bctx = {}
                        b2ctx = {}

                        def sched_qprep(qc):
                            if qc >= NQC:
                                return
                            qsl = qc % 2
                            t_qp = {}
                            for h in range(H):
                                r0 = 0 if h % 2 == 0 else 64
                                t_qp = p.op("vector", partial(
                                    I_copy,
                                    out=qpadb[qsl][r0:r0 + 64, h * 128:(h + 1) * 128],
                                    in_=qg[r0:r0 + 64,
                                           (h // 2) * S + qc * 128:
                                           (h // 2) * S + qc * 128 + 128]),
                                    wait=dmax(rd["qpad_z"], fr[f"qpadb{qsl}"])
                                    if h == 0 else {},
                                    inc=[("dve", 1)])
                            qprep_t[qc] = t_qp

                        def sched_xload(qc):
                            qs4 = qc % 3
                            row0 = si * S + qc * 128
                            ctx[qc] = p.op("scalar", partial(
                                I_dma, out=xbf_[qs4][:, :],
                                in_=x_d.ap()[row0:row0 + 128, :]),
                                wait=fr[f"xb{qs4}"], inc=[(f"xf{qc % 2}", 16)])

                        LAG = 3

                        def stage_a1(qc):
                            qsl = qc % 2
                            qprep = qprep_t.pop(qc)

                            nkc = min(qc + 3, NQC)
                            kcs = list(range(nkc))
                            blocks = [kcs[i:i + 4] for i in range(0, nkc, 4)]
                            nblk = len(blocks)
                            has_mask = (qc + 2 <= NQC - 1)

                            units = [(h, bi) for h in range(H) for bi in range(nblk)]
                            st_post = {}
                            msk_post = {}
                            av_post = [None] * H

                            def sched_st(u):
                                h, bi = units[u]
                                blk = blocks[bi]
                                bank = u % 2
                                t_pe = {}
                                for j, kc in enumerate(blk):
                                    w = dmax(fr[f"ps{bank}"], qprep) if j == 0 else {}
                                    t_pe = p.op("tensor", partial(
                                        I_mm, out=ps[bank][:, j * 128:(j + 1) * 128],
                                        lhsT=kfm[:, (h // 2) * S + kc * 128:
                                                  (h // 2) * S + kc * 128 + 128],
                                        rhs=qpadb[qsl][:, h * 128:(h + 1) * 128],
                                        start=True, stop=True),
                                        wait=w,
                                        inc=[("pe", 1)] if j == len(blk) - 1 else None)
                                st_post[u] = t_pe

                            def sched_exp_mask(u):
                                h, bi = units[u]
                                blk = blocks[bi]
                                bank = u % 2
                                asl = u % 4
                                t = p.op("scalar", partial(
                                    I_act, out=atb[asl][:, 0:len(blk) * 128],
                                    in_=ps[bank][:, 0:len(blk) * 128], func=AF.Exp,
                                    bias=0.0, scale=0.125),
                                    wait=dmax(st_post[u], fr[f"atb{asl}"]),
                                    inc=[("act", 1)])
                                fr[f"ps{bank}"] = t
                                if has_mask and bi == nblk - 1:
                                    j = len(blk) - 1
                                    t = p.op("vector", partial(
                                        I_tt, out=atb[asl][:, j * 128:(j + 1) * 128],
                                        a=atb[asl][:, j * 128:(j + 1) * 128],
                                        b=maskR[:, :], op=ALU.mult),
                                        wait=dmax(t, rd["masks"]), inc=[("dve", 1)])
                                msk_post[u] = t

                            def sched_av(u):
                                h, bi = units[u]
                                blk = blocks[bi]
                                asl = u % 4
                                obank = 2 + (h // 7 if h < 14 else 2)
                                idx = h - (0, 7, 14)[obank - 2]
                                first = (bi == 0)
                                last = (bi == nblk - 1)
                                t_pe = {}
                                for j, kc in enumerate(blk):
                                    w = {}
                                    if j == 0:
                                        w = dmax(msk_post[u],
                                                 (fr[f"ps{obank}"] if (idx == 0 and first)
                                                  else {}))
                                    t_pe = p.op("tensor", partial(
                                        I_mm,
                                        out=ps[obank][:, idx * 65:idx * 65 + 65],
                                        lhsT=atb[asl][:, j * 128:(j + 1) * 128],
                                        rhs=vsb[:, kc * 1040 + h * 65:
                                                kc * 1040 + h * 65 + 65],
                                        start=(first and j == 0),
                                        stop=(last and j == len(blk) - 1)),
                                        wait=w,
                                        inc=[("pe", 1)] if j == len(blk) - 1 else None)
                                fr[f"atb{asl}"] = t_pe
                                if last:
                                    av_post[h] = t_pe

                            S_ = dict(units=units, st=sched_st, em=sched_exp_mask,
                                      av=sched_av, av_post=av_post)
                            ast[qc] = S_
                            K = min(8, len(units))
                            for u in range(K):
                                sched_st(u)
                                sched_exp_mask(u)
                                if u >= LAG:
                                    sched_av(u - LAG)
                            S_["next"] = K

                        def stage_a2_units(qc):
                            qsl = qc % 2
                            S_ = ast[qc]
                            units = S_["units"]
                            sched_st, sched_exp_mask, sched_av = \
                                S_["st"], S_["em"], S_["av"]
                            for u in range(S_["next"], len(units)):
                                sched_st(u)
                                sched_exp_mask(u)
                                if u >= LAG:
                                    sched_av(u - LAG)
                            for u in range(max(0, len(units) - LAG), len(units)):
                                sched_av(u)
                            fr[f"qpadb{qsl}"] = {"pe": cnt["pe"]}

                        def stage_a2_evac(qc):
                            qsl = qc % 2
                            S_ = ast.pop(qc)
                            av_post = S_["av_post"]
                            rr = [None] * 3
                            for b in range(3):
                                nh = (7, 7, 2)[b]
                                h0 = (0, 7, 14)[b]
                                rr[b] = p.op("vector", partial(
                                    I_recip, out=fcols[:, h0:h0 + nh],
                                    in_=AP(ps[2 + b], 64, [[512, 128], [65, nh]])),
                                    wait=av_post[h0 + nh - 1], inc=[("dve", 1)])
                            t_ev = {}
                            for h in range(H):
                                b = 0 if h < 7 else (1 if h < 14 else 2)
                                idx = h - (0, 7, 14)[b]
                                t_ev = p.op("vector", partial(
                                    I_ts, out=otm[qsl][:, h * 64:(h + 1) * 64],
                                    in_=ps[2 + b][:, idx * 65:idx * 65 + 64],
                                    s1=fcols[:, h:h + 1], op0=ALU.mult),
                                    wait=dmax(rr[b],
                                              fr[f"otm{qsl}"] if h == 0 else {}),
                                    inc=[("dve", 1)])
                            fr["ps2"] = t_ev
                            fr["ps3"] = t_ev
                            fr["ps4"] = t_ev
                            rd[f"otm{qsl}"] = t_ev

                        def stage_tr(qc):
                            qsl = qc % 2
                            qs4 = qc % 3
                            t_tr = p.op("sync", partial(
                                I_dmat,
                                out=ofm[qs4][:, :].rearrange("p (ic f) -> p ic f", ic=8),
                                in_=otm[qsl][:, :].rearrange("p (ic f) -> p ic f", ic=8)),
                                wait=dmax(rd[f"otm{qsl}"], fr[f"ofm{qs4}"]),
                                inc=[(f"of{qs4}", 16)])
                            fr[f"otm{qsl}"] = t_tr
                            rd[f"ofm{qs4}"] = t_tr

                        def stage_b1_pe(qc):
                            qs4 = qc % 3
                            bst = {}
                            for half in range(2):
                                bank = 6 + half
                                t_pe = {}
                                for ic in range(8):
                                    w = dmax(rd[f"ofm{qs4}"], rd["woT"],
                                             fr[f"ps{bank}"]) if ic == 0 else {}
                                    t_pe = p.op("tensor", partial(
                                        I_mm, out=ps[bank][:, :],
                                        lhsT=ofm[qs4][:, ic * 128:(ic + 1) * 128],
                                        rhs=woT[:, ic * D + half * 512:
                                                ic * D + half * 512 + 512],
                                        start=(ic == 0), stop=(ic == 7)),
                                        wait=w, inc=[("pe", 1)] if ic == 7 else None)
                                bst[half] = t_pe
                            fr[f"ofm{qs4}"] = {"pe": cnt["pe"]}
                            bctx[qc] = bst

                        def stage_b1_stt(qc):
                            sl = qc % 2
                            qs4 = qc % 3
                            bst = bctx[qc]
                            t_xb = p.op("vector", partial(
                                I_tt, out=xbf_[qs4][:, :], a=xbf_[qs4][:, :],
                                b=beff_rep[:, :], op=ALU.add),
                                wait=dmax(ctx.pop(qc), rd["beff"]), inc=[("dve", 1)])
                            t_stt = {}
                            for half in range(2):
                                bank = 6 + half
                                t_stt = p.op("vector", partial(
                                    I_tt, out=x1pre[sl][:, half * 512:(half + 1) * 512],
                                    a=ps[bank][:, :],
                                    b=xbf_[qs4][:, half * 512:(half + 1) * 512],
                                    op=ALU.add),
                                    wait=dmax(bst[half], t_xb,
                                              fr[f"x1pre{sl}"] if half == 0 else {}),
                                    inc=[("dve", 1)])
                                fr[f"ps{bank}"] = t_stt
                            fr[f"xb{qs4}"] = t_stt
                            bst["stt"] = t_stt

                        def stage_b1_stats(qc):
                            sl = qc % 2
                            bst = bctx[qc]
                            st_ = statsb[sl]
                            t = p.op("vector", partial(
                                I_red, out=st_[:, 0:1], in_=x1pre[sl][:, :],
                                op=ALU.add, negate=True),
                                wait=dmax(bst["stt"], fr[f"stats{sl}"]), inc=[("dve", 1)])
                            t = p.op("vector", partial(
                                I_ts, out=st_[:, 1:2], in_=st_[:, 0:1],
                                s1=1.0 / D, op0=ALU.mult), wait=t, inc=[("dve", 1)])
                            t_sq = p.op("vector", partial(
                                I_tt, out=sqv[:, :], a=x1pre[sl][:, :],
                                b=x1pre[sl][:, :], op=ALU.mult),
                                wait=dmax(bst["stt"], fr["sqv"]), inc=[("dve", 1)])
                            t_ss = p.op("vector", partial(
                                I_red, out=st_[:, 2:3], in_=sqv[:, :],
                                op=ALU.add), wait=t_sq, inc=[("dve", 1)])
                            fr["sqv"] = t_ss
                            t_m2 = p.op("vector", partial(
                                I_tt, out=st_[:, 3:4], a=st_[:, 1:2], b=st_[:, 1:2],
                                op=ALU.mult), wait=t, inc=[("dve", 1)])
                            t_var = p.op("vector", partial(
                                I_stt, out=st_[:, 4:5], in0=st_[:, 2:3], scal=1.0 / D,
                                in1=st_[:, 3:4], op0=ALU.mult, op1=ALU.subtract),
                                wait=dmax(t_ss, t_m2), inc=[("dve", 1)])
                            bst["var"] = t_var

                        def stage_b1_tail(qc):
                            sl = qc % 2
                            row0 = si * S + qc * 128
                            bst = bctx.pop(qc)
                            st_ = statsb[sl]
                            t_ln1 = p.op("scalar", partial(
                                I_act, out=st_[:, 5:6], in_=st_[:, 4:5], func=AF.Ln,
                                bias=eps_sb[:, :], scale=1.0),
                                wait=dmax(bst["var"], rd["eps"]), inc=[("act", 1)])
                            t_rstd = p.op("scalar", partial(
                                I_act, out=st_[:, 6:7], in_=st_[:, 5:6], func=AF.Exp,
                                bias=0.0, scale=-0.5), wait=t_ln1, inc=[("act", 1)])
                            t_bln = p.op("scalar", partial(
                                I_act, out=st_[:, 7:8], in_=st_[:, 1:2],
                                func=AF.Identity, bias=0.0,
                                scale=st_[:, 6:7]), wait=t_rstd, inc=[("act", 1)])
                            t_x1 = p.op("scalar", partial(
                                I_act, out=x1pre[sl][:, :], in_=x1pre[sl][:, :],
                                func=AF.Identity, bias=st_[:, 7:8],
                                scale=st_[:, 6:7]),
                                wait=t_bln, inc=[("act", 1)])
                            fr[f"stats{sl}"] = t_x1
                            t_x1b = p.op("vector", partial(
                                I_copy, out=x1btm[sl][:, :], in_=x1pre[sl][:, :]),
                                wait=dmax(t_x1, fr[f"x1btm{sl}"]), inc=[("dve", 1)])
                            t_park = p.op("scalar", partial(
                                I_dma, out=x1_d.ap()[row0:row0 + 128, :],
                                in_=x1pre[sl][:, :]), wait=t_x1,
                                inc=[(f"park{qc % 2}", 16)])
                            rd[f"x1park{qc}"] = t_park
                            fr[f"x1pre{sl}"] = dmax(t_x1b, t_park)
                            t_tr = {}
                            for ic in range(8):
                                t_tr = p.op("sync", partial(
                                    I_dmat,
                                    out=X1b[:, ic * S + qc * 128: ic * S + qc * 128 + 128],
                                    in_=x1btm[sl][:, ic * 128:(ic + 1) * 128]),
                                    wait=t_x1b, inc=[(f"x1t{sl}", 16)])
                            fr[f"x1btm{sl}"] = t_tr
                            rd[f"X1b{qc}"] = t_tr

                        def stage_b2_pe(qc):
                            t_pe = {}
                            for ic in range(8):
                                w = dmax(rd[f"X1b{qc}"], rd["rw"],
                                         fr["ps5"]) if ic == 0 else {}
                                t_pe = p.op("tensor", partial(
                                    I_mm, out=ps[5][:, 0:NE],
                                    lhsT=X1b[:, ic * S + qc * 128: ic * S + qc * 128 + 128],
                                    rhs=rw_sb[:, ic * NE:(ic + 1) * NE],
                                    start=(ic == 0), stop=(ic == 7)),
                                    wait=w, inc=[("pe", 1)] if ic == 7 else None)
                            b2ctx[qc] = t_pe

                        def stage_b2_rest(qc):
                            qsl = qc % 2
                            gs = gscr[qsl]
                            t_pe = b2ctx.pop(qc)
                            t = p.op("vector", partial(
                                I_tt, out=gs[:, 0:8], a=ps[5][:, 0:NE], b=rb_rep[:, :],
                                op=ALU.add),
                                wait=dmax(t_pe, rd["rb"], fr[f"gscr{qsl}"]),
                                inc=[("dve", 1)])
                            fr["ps5"] = t
                            t_nm = p.op("vector", partial(
                                I_red, out=gs[:, 8:9], in_=gs[:, 0:8], op=ALU.max,
                                negate=True), wait=t, inc=[("dve", 1)])
                            t_mk = p.op("vector", partial(
                                I_ts, out=gs[:, 16:24], in_=gs[:, 0:8],
                                s1=gs[:, 8:9], s2=0.0, op0=ALU.add, op1=ALU.is_ge),
                                wait=t_nm, inc=[("dve", 1)])
                            t_e = p.op("scalar", partial(
                                I_act, out=gs[:, 24:32], in_=gs[:, 0:8], func=AF.Exp,
                                bias=gs[:, 8:9], scale=1.0), wait=t_nm, inc=[("act", 1)])
                            t_se = p.op("vector", partial(
                                I_red, out=gs[:, 9:10], in_=gs[:, 24:32], op=ALU.add),
                                wait=t_e, inc=[("dve", 1)])
                            t_rse = p.op("vector", partial(
                                I_recip, out=gs[:, 10:11], in_=gs[:, 9:10]),
                                wait=t_se, inc=[("dve", 1)])
                            t_gm = p.op("vector", partial(
                                I_tt, out=gs[:, 32:40], a=gs[:, 24:32], b=gs[:, 16:24],
                                op=ALU.mult), wait=dmax(t_mk, t_e), inc=[("dve", 1)])
                            t_g = p.op("vector", partial(
                                I_ts, out=gates_tm[:, qc * NE:(qc + 1) * NE],
                                in_=gs[:, 32:40], s1=gs[:, 10:11], op0=ALU.mult),
                                wait=dmax(t_gm, t_rse), inc=[("dve", 1)])
                            rd[f"gates{qc}"] = t_g
                            gsl = qc % 2
                            t_gb = p.op("vector", partial(
                                I_copy, out=gpad[gsl][:, 0:NE],
                                in_=gates_tm[:, qc * NE:(qc + 1) * NE]),
                                wait=dmax(t_g, rd["gpad_z"], fr[f"gpad{gsl}"]),
                                inc=[("dve", 1)])
                            t_gt = p.op("sync", partial(
                                I_dmat, out=gfm[:, qc * 128:(qc + 1) * 128],
                                in_=gpad[gsl][:, :]),
                                wait=t_gb, inc=[(f"gf{gsl}", 16)])
                            fr[f"gpad{gsl}"] = t_gt
                            fr[f"gscr{qsl}"] = t_gt
                            rd[f"gfm{qc}"] = t_gt

                        sched_qprep(0)
                        sched_qprep(1)
                        sched_xload(0)
                        sched_xload(1)
                        stage_a1(0)
                        stage_a2_units(0)
                        stage_a2_evac(0)
                        stage_a1(1)
                        stage_a2_units(1)
                        stage_a2_evac(1)
                        sched_qprep(2)
                        stage_tr(0)
                        stage_tr(1)
                        for it in range(NQC + 3):
                            if 0 <= it - 1 < NQC:
                                stage_b1_pe(it - 1)
                            if 0 <= it - 2 < NQC:
                                stage_b1_stats(it - 2)
                            if it + 2 < NQC:
                                stage_a1(it + 2)
                            if 0 <= it - 2 < NQC:
                                stage_b1_tail(it - 2)
                            if 0 <= it - 3 < NQC:
                                stage_b2_pe(it - 3)
                                stage_b2_rest(it - 3)
                            if it + 2 < NQC:
                                stage_a2_units(it + 2)
                            if 0 <= it - 1 < NQC:
                                stage_b1_stt(it - 1)
                            if it + 3 < NQC:
                                sched_qprep(it + 3)
                            if it + 2 < NQC:
                                stage_a2_evac(it + 2)
                                sched_xload(it + 2)
                                stage_tr(it + 2)

                        p.drain()
                        p.emit()
                    nc.all_engine_barrier()

                with contextlib.ExitStack() as scf:
                    Ef = scf.enter_context
                    acc = Ef(nc.sbuf_tensor(f"acc_{si}", [128, NQC * D], F32))
                    w1b = [Ef(nc.sbuf_tensor(f"w1b{i}_{si}", [128, 8 * 1024], BF16))
                           for i in range(2)]
                    w2b = [Ef(nc.sbuf_tensor(f"w2b{i}_{si}", [128, 8 * 1024], BF16))
                           for i in range(2)]
                    hsc = [Ef(nc.sbuf_tensor(f"hsc{i}_{si}", [128, 8 * 512], BF16))
                           for i in range(2)]
                    outsb = [Ef(nc.sbuf_tensor(f"outsb{i}_{si}", [128, D], F32))
                             for i in range(2)]

                    p = Prog(nc, sems, cnt)
                    for i in range(2):
                        fr[f"w1b{i}"] = {}
                        fr[f"w2b{i}"] = {}
                        fr[f"hsc{i}"] = {}
                        fr[f"outsb{i}"] = {}

                    t_l = {}
                    for qc in range(NQC):
                        row0 = si * S + qc * 128
                        t_l = p.op("scalar", partial(
                            I_dma, out=acc[:, qc * D:(qc + 1) * D],
                            in_=x1_d.ap()[row0:row0 + 128, :]),
                            wait=rd[f"x1park{qc}"], inc=[("accl", 16)])
                    for qc in range(NQC):
                        for half in range(2):
                            bank = 6 + half
                            t_pe = p.op("tensor", partial(
                                I_mm, out=ps[bank][:, :],
                                lhsT=gfm[:, qc * 128:(qc + 1) * 128],
                                rhs=b2pad[:, half * 512:(half + 1) * 512],
                                start=True, stop=True),
                                wait=dmax(rd[f"gfm{qc}"], rd["b2pad"], fr[f"ps{bank}"]),
                                inc=[("pe", 1)])
                            t = p.op("vector", partial(
                                I_stt, out=acc[:, qc * D + half * 512:
                                               qc * D + half * 512 + 512],
                                in0=ps[bank][:, :], scal=1.0,
                                in1=acc[:, qc * D + half * 512:
                                        qc * D + half * 512 + 512],
                                op0=ALU.mult, op1=ALU.add),
                                wait=dmax(t_pe, t_l), inc=[("dve", 1)])
                            fr[f"ps{bank}"] = t

                    def sched_wload(e, b, wsl):
                        t1 = p.op("gpsimd", partial(
                            I_dma, out=w1b[wsl][:, :].rearrange("p (c f) -> p c f", c=8),
                            in_=w1_d.ap()[e, :, b * 1024:(b + 1) * 1024].rearrange(
                                "(c p) f -> p c f", p=128)),
                            wait=fr[f"w1b{wsl}"], inc=[(f"wa{wsl}", 16)])
                        t2 = p.op("gpsimd", partial(
                            I_dma, out=w2b[wsl][:, :].rearrange("p (m o) -> p m o", m=8),
                            in_=w2_d.ap()[e, b * 1024:(b + 1) * 1024, :].rearrange(
                                "(m p) o -> p m o", p=128)),
                            wait=fr[f"w2b{wsl}"], inc=[(f"wb{wsl}", 16)])
                        return t1, t2

                    eb_list = [(e, b) for e in range(NE) for b in range(NB)]
                    w_rd = {}
                    w_rd[0] = sched_wload(*eb_list[0], 0)
                    for idx, (e, b) in enumerate(eb_list):
                        wsl = idx % 2
                        if idx + 1 < len(eb_list):
                            w_rd[idx + 1] = sched_wload(*eb_list[idx + 1], (idx + 1) % 2)
                        t_w1, t_w2 = w_rd[idx]
                        for tt in range(4):
                            hsl = tt % 2
                            h_done = {}
                            for m in range(8):
                                bank = m % 2
                                t_pe = {}
                                for dc in range(8):
                                    w = dmax(t_w1, fr[f"ps{bank}"],
                                             rd[f"X1b{tt * 4}"], rd[f"X1b{tt * 4 + 1}"],
                                             rd[f"X1b{tt * 4 + 2}"], rd[f"X1b{tt * 4 + 3}"],
                                             fr[f"hsc{hsl}"] if m == 0 else {}) \
                                        if dc == 0 else {}
                                    t_pe = p.op("tensor", partial(
                                        I_mm, out=ps[bank][:, :],
                                        lhsT=w1b[wsl][:, dc * 1024 + m * 128:
                                                      dc * 1024 + m * 128 + 128],
                                        rhs=X1b[:, dc * S + tt * 512:
                                                dc * S + tt * 512 + 512],
                                        start=(dc == 0), stop=(dc == 7)),
                                        wait=w, inc=[("pe", 1)] if dc == 7 else None)
                                t_h = p.op("scalar", partial(
                                    I_act, out=hsc[hsl][:, m * 512:(m + 1) * 512],
                                    in_=ps[bank][:, :], func=AF.Relu,
                                    bias=b1Tf[:, (b * 8 + m) * NE + e:
                                              (b * 8 + m) * NE + e + 1],
                                    scale=1.0),
                                    wait=dmax(t_pe, rd["b1T"]), inc=[("act", 1)])
                                fr[f"ps{bank}"] = t_h
                                h_done = t_h
                            rd[f"hsc{hsl}"] = h_done
                            last_y = {}
                            for ts_ in range(4):
                                qc = tt * 4 + ts_
                                for half in range(2):
                                    bank = 2 + (ts_ * 2 + half) % 3
                                    t_pe = {}
                                    for m in range(8):
                                        w = dmax(rd[f"hsc{hsl}"], t_w2,
                                                 fr[f"ps{bank}"]) if m == 0 else {}
                                        t_pe = p.op("tensor", partial(
                                            I_mm, out=ps[bank][:, :],
                                            lhsT=hsc[hsl][:, m * 512 + ts_ * 128:
                                                          m * 512 + ts_ * 128 + 128],
                                            rhs=w2b[wsl][:, m * 1024 + half * 512:
                                                         m * 1024 + half * 512 + 512],
                                            start=(m == 0), stop=(m == 7)),
                                            wait=w, inc=[("pe", 1)] if m == 7 else None)
                                    t_y = p.op("vector", partial(
                                        I_stt,
                                        out=acc[:, qc * D + half * 512:
                                                qc * D + half * 512 + 512],
                                        in0=ps[bank][:, :],
                                        scal=gates_tm[:, qc * NE + e:qc * NE + e + 1],
                                        in1=acc[:, qc * D + half * 512:
                                                qc * D + half * 512 + 512],
                                        op0=ALU.mult, op1=ALU.add),
                                        wait=dmax(t_pe, rd[f"gates{qc}"]),
                                        inc=[("dve", 1)])
                                    fr[f"ps{bank}"] = t_y
                                    last_y = t_y
                            fr[f"hsc{hsl}"] = last_y
                        fr[f"w1b{wsl}"] = last_y
                        fr[f"w2b{wsl}"] = last_y

                    acc_done = last_y
                    t_out = {}
                    for qc in range(NQC):
                        qsl = qc % 2
                        st_ = statsb[qsl]
                        a_ap = acc[:, qc * D:(qc + 1) * D]
                        t = p.op("vector", partial(
                            I_red, out=st_[:, 0:1], in_=a_ap, op=ALU.add, negate=True),
                            wait=dmax(acc_done, fr[f"stats{qsl}"]), inc=[("dve", 1)])
                        t = p.op("vector", partial(
                            I_ts, out=st_[:, 1:2], in_=st_[:, 0:1], s1=1.0 / D,
                            op0=ALU.mult), wait=t, inc=[("dve", 1)])
                        t_sq = p.op("vector", partial(
                            I_tt, out=outsb[qsl][:, :], a=a_ap, b=a_ap, op=ALU.mult),
                            wait=dmax(t, fr[f"outsb{qsl}"]), inc=[("dve", 1)])
                        t_ss = p.op("vector", partial(
                            I_red, out=st_[:, 2:3], in_=outsb[qsl][:, :], op=ALU.add),
                            wait=t_sq, inc=[("dve", 1)])
                        t_m2 = p.op("vector", partial(
                            I_tt, out=st_[:, 3:4], a=st_[:, 1:2], b=st_[:, 1:2],
                            op=ALU.mult), wait=t, inc=[("dve", 1)])
                        t_var = p.op("vector", partial(
                            I_stt, out=st_[:, 4:5], in0=st_[:, 2:3], scal=1.0 / D,
                            in1=st_[:, 3:4], op0=ALU.mult, op1=ALU.subtract),
                            wait=dmax(t_ss, t_m2), inc=[("dve", 1)])
                        t_ln = p.op("scalar", partial(
                            I_act, out=st_[:, 5:6], in_=st_[:, 4:5], func=AF.Ln,
                            bias=eps_sb[:, :], scale=1.0),
                            wait=dmax(t_var, rd["eps"]), inc=[("act", 1)])
                        t_rstd = p.op("scalar", partial(
                            I_act, out=st_[:, 6:7], in_=st_[:, 5:6], func=AF.Exp,
                            bias=0.0, scale=-0.5), wait=t_ln, inc=[("act", 1)])
                        t_bln = p.op("vector", partial(
                            I_tt, out=st_[:, 7:8], a=st_[:, 1:2], b=st_[:, 6:7],
                            op=ALU.mult), wait=t_rstd, inc=[("dve", 1)])
                        t_x2 = p.op("scalar", partial(
                            I_act, out=outsb[qsl][:, :], in_=a_ap, func=AF.Identity,
                            bias=st_[:, 7:8], scale=st_[:, 6:7]),
                            wait=t_bln, inc=[("act", 1)])
                        fr[f"stats{qsl}"] = t_x2
                        row0 = si * S + qc * 128
                        t_out = p.op("sync", partial(
                            I_dma, out=y_d.ap()[row0:row0 + 128, :],
                            in_=outsb[qsl][:, :]), wait=t_x2, inc=[("out", 16)])
                        fr[f"outsb{qsl}"] = t_out

                    if si == SEQS - 1:
                        p.op("gpsimd", None, wait={"out": cnt["out"]})
                    p.drain()
                    p.emit()
                nc.all_engine_barrier()

    return nc


_NC_CACHE = []


def kernel(**inputs):
    x = np.ascontiguousarray(np.asarray(inputs["L0_attn_x"], dtype=np.float32))
    full = {
        "in_proj_w": np.ascontiguousarray(np.asarray(inputs["L0_attn_in_proj_w"], np.float32)),
        "in_proj_b": np.ascontiguousarray(np.asarray(inputs["L0_attn_in_proj_b"], np.float32)),
        "out_proj_w": np.ascontiguousarray(np.asarray(inputs["L0_attn_out_proj_w"], np.float32)),
        "out_proj_b": np.ascontiguousarray(np.asarray(inputs["L0_attn_out_proj_b"], np.float32)),
        "router_W": np.ascontiguousarray(np.asarray(inputs["L0_ffn_router_W"], np.float32)),
        "router_b": np.ascontiguousarray(np.asarray(inputs["L0_ffn_router_b"], np.float32)),
        "W1": np.ascontiguousarray(np.asarray(inputs["L0_ffn_W1"], np.float32)),
        "b1": np.ascontiguousarray(np.asarray(inputs["L0_ffn_b1"], np.float32)),
        "W2": np.ascontiguousarray(np.asarray(inputs["L0_ffn_W2"], np.float32)),
        "b2": np.ascontiguousarray(np.asarray(inputs["L0_ffn_b2"], np.float32)),
    }
    if not _NC_CACHE:
        _NC_CACHE.append(build_program())
    nc = _NC_CACHE[0]
    in_maps = []
    for c in range(NCORES):
        m = {"x": x[SEQS * c:SEQS * (c + 1)].reshape(SEQS * S, D)}
        m.update(full)
        in_maps.append(m)
    res = run_bass_kernel_spmd(nc, in_maps, core_ids=list(range(NCORES)))
    out = np.concatenate(
        [res.results[c]["y"].reshape(SEQS, S, D) for c in range(NCORES)], axis=0)
    return out


if __name__ == "__main__":
    nc = build_program()
    print("instructions:", nc.n_instructions())


# version 16 (final)
# speedup vs baseline: 1.1425x; 1.0538x over previous
import contextlib
from functools import partial

import numpy as np

import concourse.bass as bass
import concourse.mybir as mybir
from concourse.bass import AP
from concourse.bass_utils import run_bass_kernel_spmd

F32 = mybir.dt.float32
BF16 = mybir.dt.bfloat16
AF = mybir.ActivationFunctionType
ALU = mybir.AluOpType
AXX = mybir.AxisListType.X

NCORES = 8
SEQS = 2
S = 2048
D = 1024
H = 16
NG = 4
GT = 512
NQC = 16
NE = 8
DFF = 4096
NB = 4
KSLOT = 6
EPS = 1e-5


def I_mm(eng, out=None, lhsT=None, rhs=None, start=True, stop=True):
    return eng.matmul(out, lhsT, rhs, start=start, stop=stop)

def I_act(eng, out=None, in_=None, func=None, bias=0.0, scale=1.0):
    return eng.activation(out, in_, func, bias=bias, scale=scale)

def I_dma(eng, out=None, in_=None):
    return eng.dma_start(out, in_)

def I_dmat(eng, out=None, in_=None):
    return eng.dma_start_transpose(out, in_)

def I_copy(eng, out=None, in_=None):
    return eng.tensor_copy(out, in_)

def I_tt(eng, out=None, a=None, b=None, op=None):
    return eng.tensor_tensor(out, a, b, op)

def I_ts(eng, out=None, in_=None, s1=None, s2=None, op0=None, op1=None):
    if op1 is None:
        return eng.tensor_single_scalar(out, in_, s1, op0)
    return eng.tensor_scalar(out, in_, s1, s2, op0, op1)

def I_stt(eng, out=None, in0=None, scal=None, in1=None, op0=None, op1=None):
    return eng.scalar_tensor_tensor(out, in0, scal, in1, op0, op1)

def I_red(eng, out=None, in_=None, op=None, negate=False):
    return eng.tensor_reduce(out, in_, AXX, op, negate=negate)

def I_recip(eng, out=None, in_=None):
    return eng.reciprocal(out, in_)

def I_memset(eng, ap=None, c=0.0):
    return eng.memset(ap, c)

def I_iota(eng, ap=None, pat=None, cm=0):
    return eng.iota(ap, pat, channel_multiplier=cm,
                    allow_small_or_imprecise_dtypes=True)


def dmax(*ds):
    out = {}
    for d in ds:
        if d:
            for k, v in d.items():
                if out.get(k, -1) < v:
                    out[k] = v
    return out


class Prog:

    ENGINES = ("tensor", "scalar", "vector", "gpsimd", "sync")

    def __init__(self, nc, sems, cnt):
        self.nc = nc
        self.sems = sems
        self.cnt = cnt
        self.q = {e: [] for e in self.ENGINES}

    COMPUTE_SEMS = ("pe", "act", "dve", "pool")

    def op(self, eng, fn, wait=None, inc=None):
        incs = inc or []
        w = dict(wait or {})
        for sname in list(w):
            if sname not in self.COMPUTE_SEMS:
                w[sname] = max(w[sname], self.cnt[sname])
        for sname, n in incs:
            self.cnt[sname] += n
        post = {sname: self.cnt[sname] for sname, _ in incs}
        self.q[eng].append((fn, w, incs))
        return post

    def drain(self):
        w = {s: self.cnt[s] for s in self.cnt
             if s not in self.COMPUTE_SEMS and self.cnt[s] > 0}
        self.op("gpsimd", None, wait=w)

    AUDIT = []

    def emit(self):
        Prog.AUDIT.append({e: list(self.q[e]) for e in self.ENGINES})
        with self.nc.Block() as block:
            for eng_name in self.ENGINES:
                ops = self.q[eng_name]
                if not ops:
                    continue

                def make_stream(ops=ops):
                    def stream(eng):
                        last_wait = {}
                        for fn, waits, incs in ops:
                            for sname, v in waits.items():
                                if last_wait.get(sname, -1) < v:
                                    eng.wait_ge(self.sems[sname], v)
                                    last_wait[sname] = v
                            ins = fn(eng) if fn is not None else None
                            for sname, n in incs:
                                ins.then_inc(self.sems[sname], n)
                    return stream

                getattr(block, eng_name)(make_stream())


def build_program():
    nc = bass.Bass("TRN2", target_bir_lowering=False)

    x_d = nc.dram_tensor("x", [SEQS * S, D], F32, kind="ExternalInput")
    ipw_d = nc.dram_tensor("in_proj_w", [3 * D, D], F32, kind="ExternalInput")
    ipb_d = nc.dram_tensor("in_proj_b", [3 * D], F32, kind="ExternalInput")
    opw_d = nc.dram_tensor("out_proj_w", [D, D], F32, kind="ExternalInput")
    opb_d = nc.dram_tensor("out_proj_b", [D], F32, kind="ExternalInput")
    rw_d = nc.dram_tensor("router_W", [D, NE], F32, kind="ExternalInput")
    rb_d = nc.dram_tensor("router_b", [NE], F32, kind="ExternalInput")
    w1_d = nc.dram_tensor("W1", [NE, D, DFF], F32, kind="ExternalInput")
    b1_d = nc.dram_tensor("b1", [NE, DFF], F32, kind="ExternalInput")
    w2_d = nc.dram_tensor("W2", [NE, DFF, D], F32, kind="ExternalInput")
    b2_d = nc.dram_tensor("b2", [NE, D], F32, kind="ExternalInput")
    y_d = nc.dram_tensor("y", [SEQS * S, D], F32, kind="ExternalOutput")

    wt_d = nc.dram_tensor("wt_scratch", [128, 8 * 4 * D], BF16, kind="Internal")
    x1_d = nc.dram_tensor("x1_scratch", [SEQS * S, D], F32, kind="Internal")
    beff_d = nc.dram_tensor("beff_scratch", [D], F32, kind="Internal")

    sem_names = ("pe", "act", "dve", "pool",
                 "su_b2", "su_rw", "su_rb", "su_bt", "su_btt", "su_b1",
                 "su_b1t", "su_bo", "su_be", "su_wts",
                 "ws0", "ws1", "wt0", "wt1",
                 "wq0", "wq1", "xg0", "xg1", "xt0", "xt1", "wot",
                 "of0", "of1", "of2", "x1t0", "x1t1", "gf0", "gf1",
                 "xf0", "xf1", "park0", "park1", "accl", "wa0", "wa1", "wb0", "wb1",
                 "out")
    cnt = {sname: 0 for sname in sem_names}
    fr = {}
    rd = {}
    for i in range(8):
        fr[f"ps{i}"] = {}

    with contextlib.ExitStack() as top:
        E = top.enter_context
        sems = {sname: E(nc.semaphore(sname)) for sname in sem_names}

        ps = [E(nc.psum_tensor(f"ps{i}", [128, 512], F32)) for i in range(8)]
        beff_rep = E(nc.sbuf_tensor("beff_rep", [128, D], F32))
        rb_rep = E(nc.sbuf_tensor("rb_rep", [128, NE], F32))
        rw_sb = E(nc.sbuf_tensor("rw_sb", [128, 8 * NE], BF16))
        b2pad = E(nc.sbuf_tensor("b2pad", [128, D], BF16))
        bTf = E(nc.sbuf_tensor("bTf", [128, 32], F32))
        b1Tf = E(nc.sbuf_tensor("b1Tf", [128, 32 * NE], F32))
        maskL = E(nc.sbuf_tensor("maskL", [128, 128], BF16))
        maskR = E(nc.sbuf_tensor("maskR", [128, 128], BF16))
        ones_col = E(nc.sbuf_tensor("ones_col", [128, 1], BF16))
        eps_sb = E(nc.sbuf_tensor("eps_sb", [128, 1], F32))
        gates_tm = E(nc.sbuf_tensor("gates_tm", [128, NQC * NE], F32))
        gfm = E(nc.sbuf_tensor("gfm", [128, NQC * 128], BF16))
        statsb = [E(nc.sbuf_tensor(f"stats{i}", [128, 16], F32)) for i in range(2)]
        gscr = [E(nc.sbuf_tensor(f"gscr{i}", [128, 40], F32)) for i in range(2)]
        gpad = [E(nc.sbuf_tensor(f"gpad{i}", [128, 128], BF16)) for i in range(2)]

        with contextlib.ExitStack() as sc0:
            E0 = sc0.enter_context
            wT_full = E0(nc.sbuf_tensor("wT_full", [128, 8 * 4 * D], BF16))
            wstag = [E0(nc.sbuf_tensor(f"wstag{i}", [128, D], BF16)) for i in range(2)]
            btmp = E0(nc.sbuf_tensor("btmp", [32, 128], BF16))
            b1tmp = E0(nc.sbuf_tensor("b1tmp", [16, DFF], BF16))
            b1T_bf = E0(nc.sbuf_tensor("b1T_bf", [128, 32 * 16], BF16))
            bT_bf = E0(nc.sbuf_tensor("bT_bf", [128, 32], BF16))
            iop = E0(nc.sbuf_tensor("iop", [128, 128], F32))
            ioq = E0(nc.sbuf_tensor("ioq", [128, 128], F32))
            mtmp = E0(nc.sbuf_tensor("mtmp", [128, 128], F32))
            bout_row = E0(nc.sbuf_tensor("bout_row", [1, D], F32))
            beff_row = E0(nc.sbuf_tensor("beff_row", [1, D], F32))

            p = Prog(nc, sems, cnt)

            p.op("gpsimd", partial(I_iota, ap=iop[:, :], pat=[[0, 128]], cm=1))
            t_io = p.op("gpsimd", partial(I_iota, ap=ioq[:, :], pat=[[1, 128]], cm=0),
                        inc=[("pool", 1)])
            t = p.op("vector", partial(I_tt, out=mtmp[:, :], a=iop[:, :], b=ioq[:, :],
                                       op=ALU.is_ge), wait=t_io, inc=[("dve", 1)])
            t = p.op("vector", partial(I_copy, out=maskL[:, :], in_=mtmp[:, :]),
                     wait=t, inc=[("dve", 1)])
            t = p.op("vector", partial(I_tt, out=mtmp[:, :], a=iop[:, :], b=ioq[:, :],
                                       op=ALU.is_le), wait=dmax(t_io, t), inc=[("dve", 1)])
            t = p.op("vector", partial(I_copy, out=maskR[:, :], in_=mtmp[:, :]),
                     wait=t, inc=[("dve", 1)])
            rd["masks"] = t
            rd["ones"] = p.op("vector", partial(I_memset, ap=ones_col[:, :], c=1.0),
                              inc=[("dve", 1)])
            rd["eps"] = p.op("vector", partial(I_memset, ap=eps_sb[:, :], c=float(EPS)),
                             inc=[("dve", 1)])
            t_b2z = p.op("vector", partial(I_memset, ap=b2pad[:, :], c=0.0), inc=[("dve", 1)])
            t_btz = p.op("vector", partial(I_memset, ap=btmp[:, :], c=0.0), inc=[("dve", 1)])
            t_b1z = p.op("vector", partial(I_memset, ap=b1tmp[:, :], c=0.0), inc=[("dve", 1)])
            t_gp = {}
            for gi in range(2):
                t_gp = p.op("vector", partial(I_memset, ap=gpad[gi][:, :], c=0.0),
                            inc=[("dve", 1)])
                fr[f"gpad{gi}"] = {}
            rd["gpad_z"] = t_gp

            rd["b2pad"] = p.op("gpsimd", partial(I_dma, out=b2pad[0:NE, :], in_=b2_d.ap()),
                               wait=t_b2z, inc=[("su_b2", 16)])
            rd["rw"] = p.op("gpsimd", partial(
                I_dma, out=rw_sb[:, :].rearrange("p (c e) -> p c e", c=8),
                in_=rw_d.ap().rearrange("(c p) e -> p c e", p=128)), inc=[("su_rw", 16)])
            rd["rb"] = p.op("sync", partial(
                I_dma, out=rb_rep[:, :], in_=AP(rb_d, 0, [[0, 128], [1, NE]])),
                inc=[("su_rb", 16)])
            t = p.op("gpsimd", partial(
                I_dma, out=btmp[0:24, :],
                in_=ipb_d.ap().rearrange("(c p) -> c p", p=128)),
                wait=t_btz, inc=[("su_bt", 16)])
            t = p.op("sync", partial(I_dmat, out=bT_bf[:, :], in_=btmp[:, :]),
                     wait=t, inc=[("su_btt", 16)])
            rd["bT"] = p.op("vector", partial(I_copy, out=bTf[:, :], in_=bT_bf[:, :]),
                            wait=t, inc=[("dve", 1)])
            t_b1 = p.op("gpsimd", partial(I_dma, out=b1tmp[0:NE, :], in_=b1_d.ap()),
                        wait=t_b1z, inc=[("su_b1", 16)])
            t = p.op("sync", partial(
                I_dmat,
                out=b1T_bf[:, :].rearrange("p (c e) -> p c e", c=32),
                in_=b1tmp[:, :].rearrange("p (c f) -> p c f", c=32)),
                wait=t_b1, inc=[("su_b1t", 16)])
            rd["b1T"] = p.op("vector", partial(
                I_copy,
                out=b1Tf[:, :].rearrange("p (c e) -> p c e", e=NE),
                in_=b1T_bf[:, :].rearrange("p (c e) -> p c e", e=16)[:, :, 0:NE]),
                wait=t, inc=[("dve", 1)])
            t_bout = p.op("sync", partial(I_dma, out=bout_row[:, :], in_=opb_d.ap()[None, :]),
                          inc=[("su_bo", 16)])

            wst_fr = [{}, {}]
            t_tr = {}
            for oc in range(32):
                sl = oc % 2
                src = (ipw_d.ap()[oc * 128:(oc + 1) * 128, :] if oc < 24 else
                       opw_d.ap()[(oc - 24) * 128:(oc - 24) * 128 + 128, :])
                t_ld = p.op("gpsimd", partial(I_dma, out=wstag[sl][:, :], in_=src),
                            wait=wst_fr[sl], inc=[(f"ws{sl}", 16)])
                t_tr = p.op("sync", partial(
                    I_dmat,
                    out=wT_full[:, oc * 1024:(oc + 1) * 1024].rearrange(
                        "p (ic f) -> p ic f", ic=8),
                    in_=wstag[sl][:, :].rearrange("p (ic f) -> p ic f", ic=8)),
                    wait=t_ld, inc=[(f"wt{oc % 2}", 16)])
                wst_fr[sl] = t_tr
            rd["wT_full"] = {"wt0": cnt["wt0"], "wt1": cnt["wt1"]}
            rd["wt_dram"] = p.op("sync", partial(
                I_dma, out=wt_d.ap(), in_=wT_full[:, :]),
                wait=rd["wT_full"], inc=[("su_wts", 16)])

            t_dve = {}
            for half in range(2):
                bank = 6 + half
                t_pe = {}
                for j in range(4):
                    oc = 24 + half * 4 + j
                    for ic in range(8):
                        w = dmax(rd["wT_full"], rd["bT"]) if (j == 0 and ic == 0) else {}
                        t_pe = p.op("tensor", partial(
                            I_mm, out=ps[bank][0:1, j * 128:(j + 1) * 128],
                            lhsT=bT_bf[:, 16 + ic:17 + ic],
                            rhs=wT_full[:, oc * 1024 + ic * 128:
                                        oc * 1024 + ic * 128 + 128],
                            start=(ic == 0), stop=(ic == 7)),
                            wait=w,
                            inc=[("pe", 1)] if (j == 3 and ic == 7) else None)
                t_dve = p.op("vector", partial(
                    I_tt, out=beff_row[:, half * 512:(half + 1) * 512],
                    a=ps[bank][0:1, 0:512],
                    b=bout_row[:, half * 512:(half + 1) * 512], op=ALU.add),
                    wait=dmax(t_pe, t_bout), inc=[("dve", 1)])
                fr[f"ps{bank}"] = t_dve
            t = p.op("sync", partial(I_dma, out=beff_d.ap()[None, :], in_=beff_row[:, :]),
                     wait=t_dve, inc=[("su_be", 16)])
            rd["beff"] = p.op("sync", partial(
                I_dma, out=beff_rep[:, :], in_=AP(beff_d, 0, [[0, 128], [1, D]])),
                wait=t, inc=[("su_be", 16)])

            p.drain()
            p.emit()
        nc.all_engine_barrier()

        fr["gates"] = {}
        fr["gfm"] = {}
        for si in range(SEQS):
            with contextlib.ExitStack() as scs:
                Es = scs.enter_context
                X1b = Es(nc.sbuf_tensor(f"X1b_{si}", [128, 8 * S], BF16))

                sc_att = contextlib.ExitStack()
                with sc_att:
                    kfm = sc_att.enter_context(
                        nc.sbuf_tensor(f"kfm_{si}", [128, 8 * S], BF16))
                    vsb = sc_att.enter_context(
                        nc.sbuf_tensor(f"vsb_{si}", [128, NQC * 1040], BF16))
                    qg = sc_att.enter_context(
                        nc.sbuf_tensor(f"qg_{si}", [128, 8 * S], BF16))

                    with contextlib.ExitStack() as scq:
                        Eq = scq.enter_context
                        wqb = [Eq(nc.sbuf_tensor(f"wqb{i}_{si}", [128, 8 * 512], BF16))
                               for i in range(2)]
                        xgbf = [Eq(nc.sbuf_tensor(f"xgbf{i}_{si}", [128, 4 * D], BF16))
                                for i in range(2)]
                        xfm = [Eq(nc.sbuf_tensor(f"xfm{i}_{si}", [128, 8 * GT], BF16))
                               for i in range(2)]

                        p = Prog(nc, sems, cnt)
                        for i in range(2):
                            fr[f"wqb{i}"] = {}
                            fr[f"xgbf{i}"] = {}
                            fr[f"xfm{i}"] = {}

                        def sched_group_loads(g):
                            sl = g % 2
                            row0 = si * S + g * GT
                            t_ld = {}
                            for tc in range(4):
                                t_ld = p.op("gpsimd", partial(
                                    I_dma,
                                    out=AP(xgbf[sl], tc * 128,
                                           [[4096, 128], [512, 8], [1, 128]]),
                                    in_=AP(x_d, (row0 + tc * 128) * D,
                                           [[D, 128], [128, 8], [1, 128]])),
                                    wait=fr[f"xgbf{sl}"] if tc == 0 else {},
                                    inc=[(f"xg{sl}", 16)])
                            t_tr = p.op("sync", partial(
                                I_dmat,
                                out=xfm[sl][:, :].rearrange("p (b f) -> p b f", b=32),
                                in_=xgbf[sl][:, :].rearrange("p (b f) -> p b f", b=32)),
                                wait=dmax(t_ld, fr[f"xfm{sl}"]), inc=[(f"xt{sl}", 16)])
                            fr[f"xgbf{sl}"] = t_tr
                            rd[f"xfm{sl}"] = t_tr

                        t_vi = p.op("vector", partial(I_memset, ap=vsb[:, :], c=1.0),
                                    inc=[("dve", 1)])
                        rd["vsb_ones"] = t_vi

                        sched_group_loads(0)

                        for g in range(NG):
                            sl = g % 2
                            xfm_rd = rd[f"xfm{sl}"]
                            for og in range(6):
                                wsl = og % 2
                                t_w = p.op("gpsimd", partial(
                                    I_dma, out=wqb[wsl][:, :],
                                    in_=wt_d.ap()[:, og * 4096:(og + 1) * 4096]),
                                    wait=dmax(rd["wt_dram"], fr[f"wqb{wsl}"]),
                                    inc=[(f"wq{wsl}", 16)])
                                if og < 4:
                                    for j in range(4):
                                        oc = og * 4 + j
                                        bank = 6 + (j % 2)
                                        t_pe = {}
                                        for ic in range(8):
                                            w = dmax(t_w, xfm_rd, fr[f"ps{bank}"]) if ic == 0 else {}
                                            t_pe = p.op("tensor", partial(
                                                I_mm, out=ps[bank][:, :],
                                                lhsT=wqb[wsl][:, j * 1024 + ic * 128:
                                                              j * 1024 + ic * 128 + 128],
                                                rhs=xfm[sl][:, ic * GT:(ic + 1) * GT],
                                                start=(ic == 0), stop=(ic == 7)),
                                                wait=w, inc=[("pe", 1)] if ic == 7 else None)
                                        fr[f"wqb{wsl}"] = dmax(fr[f"wqb{wsl}"], t_pe)
                                        if oc < 8:
                                            t = p.op("scalar", partial(
                                                I_act,
                                                out=qg[:, oc * S + g * GT:
                                                        oc * S + g * GT + GT],
                                                in_=ps[bank][:, :], func=AF.Identity,
                                                bias=bTf[:, oc:oc + 1], scale=1.0),
                                                wait=dmax(t_pe, rd["bT"]),
                                                inc=[("act", 1)])
                                            fr[f"ps{bank}"] = t
                                        else:
                                            koc = oc - 8
                                            t = p.op("scalar", partial(
                                                I_act,
                                                out=kfm[:, koc * S + g * GT:
                                                        koc * S + g * GT + GT],
                                                in_=ps[bank][:, :], func=AF.Identity,
                                                bias=bTf[:, 8 + koc:9 + koc], scale=1.0),
                                                wait=dmax(t_pe, rd["bT"]),
                                                inc=[("act", 1)])
                                            fr[f"ps{bank}"] = t
                                else:
                                    vh = og - 4
                                    for tc in range(4):
                                        slot = g * 4 + tc
                                        bank = 6 + (tc % 2)
                                        t_pe = {}
                                        for jv in range(4):
                                            for ic in range(8):
                                                w = dmax(t_w, xfm_rd, fr[f"ps{bank}"]) \
                                                    if (jv == 0 and ic == 0) else {}
                                                t_pe = p.op("tensor", partial(
                                                    I_mm,
                                                    out=ps[bank][:, jv * 128:(jv + 1) * 128],
                                                    lhsT=xfm[sl][:, ic * GT + tc * 128:
                                                                 ic * GT + tc * 128 + 128],
                                                    rhs=wqb[wsl][:, jv * 1024 + ic * 128:
                                                                 jv * 1024 + ic * 128 + 128],
                                                    start=(ic == 0), stop=(ic == 7)),
                                                    wait=w,
                                                    inc=[("pe", 1)]
                                                    if (jv == 3 and ic == 7) else None)
                                        fr[f"wqb{wsl}"] = dmax(fr[f"wqb{wsl}"], t_pe)
                                        t = p.op("scalar", partial(
                                            I_act,
                                            out=AP(vsb, slot * 1040 + vh * 520,
                                                   [[1040 * NQC, 128], [65, 8],
                                                    [1, 64]]),
                                            in_=ps[bank][:, :].rearrange(
                                                "p (h c) -> p h c", h=8),
                                            func=AF.Identity,
                                            bias=0.0, scale=1.0),
                                            wait=dmax(t_pe, rd["vsb_ones"]),
                                            inc=[("act", 1)])
                                        fr[f"ps{bank}"] = t

                            fr[f"xfm{sl}"] = {"pe": cnt["pe"]}
                            if g + 1 < NG:
                                sched_group_loads(g + 1)

                        p.drain()
                        p.emit()
                    nc.all_engine_barrier()

                    with contextlib.ExitStack() as sca:
                        Ea = sca.enter_context
                        woT = Ea(nc.sbuf_tensor(f"woT_{si}", [128, 8 * D], BF16))
                        qpadb = [Ea(nc.sbuf_tensor(f"qpad{i}_{si}", [128, H * 128], BF16))
                                 for i in range(2)]
                        atb = [Ea(nc.sbuf_tensor(f"atb{i}_{si}", [128, 512], BF16))
                               for i in range(4)]
                        fcols = Ea(nc.sbuf_tensor(f"fcols_{si}", [128, 16], F32))
                        otm = [Ea(nc.sbuf_tensor(f"otm{i}_{si}", [128, D], BF16))
                               for i in range(2)]
                        ofm = [Ea(nc.sbuf_tensor(f"ofm{i}_{si}", [128, D], BF16))
                               for i in range(3)]
                        xbf_ = [Ea(nc.sbuf_tensor(f"xb{i}_{si}", [128, D], F32))
                                for i in range(3)]
                        x1pre = [Ea(nc.sbuf_tensor(f"x1pre{i}_{si}", [128, D], F32))
                                 for i in range(2)]
                        sqv = Ea(nc.sbuf_tensor(f"sqv_{si}", [128, D], F32))
                        x1btm = [Ea(nc.sbuf_tensor(f"x1btm{i}_{si}", [128, D], BF16))
                                 for i in range(2)]

                        p = Prog(nc, sems, cnt)
                        for i in range(3):
                            for nm in ("ofm", "xb"):
                                fr.setdefault(f"{nm}{i}", {})
                        for i in range(2):
                            for nm in ("otm", "x1pre", "x1btm", "qpadb",
                                       "stats", "gscr"):
                                fr.setdefault(f"{nm}{i}", {})
                        for i in range(4):
                            fr[f"atb{i}"] = {}
                        fr["sqv"] = {}

                        for icw in range(8):
                            rd["woT"] = p.op("gpsimd", partial(
                                I_dma,
                                out=woT[:, icw * 1024:(icw + 1) * 1024],
                                in_=wt_d.ap()[:, 24 * 1024 + icw * 128:32 * 1024]
                                .rearrange("p (oc f) -> p oc f", f=128)[:, ::8, :]
                                if False else
                                AP(wt_d, 24 * 1024 + icw * 128,
                                   [[32768, 128], [1024, 8], [1, 128]])),
                                wait=rd["wt_dram"], inc=[("wot", 16)])
                        t_qz = {}
                        for i in range(2):
                            t_qz = p.op("vector", partial(I_memset, ap=qpadb[i][:, :], c=0.0),
                                        inc=[("dve", 1)])
                        rd["qpad_z"] = t_qz

                        ctx = {}

                        qprep_t = {}
                        ast = {}
                        bctx = {}
                        b2ctx = {}

                        def sched_qprep(qc):
                            if qc >= NQC:
                                return
                            qsl = qc % 2
                            t_qp = {}
                            for h in range(H):
                                r0 = 0 if h % 2 == 0 else 64
                                t_qp = p.op("vector", partial(
                                    I_copy,
                                    out=qpadb[qsl][r0:r0 + 64, h * 128:(h + 1) * 128],
                                    in_=qg[r0:r0 + 64,
                                           (h // 2) * S + qc * 128:
                                           (h // 2) * S + qc * 128 + 128]),
                                    wait=dmax(rd["qpad_z"], fr[f"qpadb{qsl}"])
                                    if h == 0 else {},
                                    inc=[("dve", 1)])
                            qprep_t[qc] = t_qp

                        def sched_xload(qc):
                            qs4 = qc % 3
                            row0 = si * S + qc * 128
                            ctx[qc] = p.op("scalar", partial(
                                I_dma, out=xbf_[qs4][:, :],
                                in_=x_d.ap()[row0:row0 + 128, :]),
                                wait=fr[f"xb{qs4}"], inc=[(f"xf{qc % 2}", 16)])

                        LAG = 3

                        def stage_a1(qc):
                            qsl = qc % 2
                            qprep = qprep_t.pop(qc)

                            nkc = min(qc + 3, NQC)
                            kcs = list(range(nkc))
                            blocks = [kcs[i:i + 4] for i in range(0, nkc, 4)]
                            nblk = len(blocks)
                            has_mask = (qc + 2 <= NQC - 1)

                            units = [(h, bi) for h in range(H) for bi in range(nblk)]
                            st_post = {}
                            msk_post = {}
                            av_post = [None] * H

                            def sched_st(u):
                                h, bi = units[u]
                                blk = blocks[bi]
                                bank = u % 2
                                t_pe = {}
                                for j, kc in enumerate(blk):
                                    w = dmax(fr[f"ps{bank}"], qprep) if j == 0 else {}
                                    t_pe = p.op("tensor", partial(
                                        I_mm, out=ps[bank][:, j * 128:(j + 1) * 128],
                                        lhsT=kfm[:, (h // 2) * S + kc * 128:
                                                  (h // 2) * S + kc * 128 + 128],
                                        rhs=qpadb[qsl][:, h * 128:(h + 1) * 128],
                                        start=True, stop=True),
                                        wait=w,
                                        inc=[("pe", 1)] if j == len(blk) - 1 else None)
                                st_post[u] = t_pe

                            def sched_exp_mask(u):
                                h, bi = units[u]
                                blk = blocks[bi]
                                bank = u % 2
                                asl = u % 4
                                t = p.op("scalar", partial(
                                    I_act, out=atb[asl][:, 0:len(blk) * 128],
                                    in_=ps[bank][:, 0:len(blk) * 128], func=AF.Exp,
                                    bias=0.0, scale=0.125),
                                    wait=dmax(st_post[u], fr[f"atb{asl}"]),
                                    inc=[("act", 1)])
                                fr[f"ps{bank}"] = t
                                if has_mask and bi == nblk - 1:
                                    j = len(blk) - 1
                                    t = p.op("vector", partial(
                                        I_tt, out=atb[asl][:, j * 128:(j + 1) * 128],
                                        a=atb[asl][:, j * 128:(j + 1) * 128],
                                        b=maskR[:, :], op=ALU.mult),
                                        wait=dmax(t, rd["masks"]), inc=[("dve", 1)])
                                msk_post[u] = t

                            def sched_av(u):
                                h, bi = units[u]
                                blk = blocks[bi]
                                asl = u % 4
                                obank = 2 + (h // 7 if h < 14 else 2)
                                idx = h - (0, 7, 14)[obank - 2]
                                first = (bi == 0)
                                last = (bi == nblk - 1)
                                t_pe = {}
                                for j, kc in enumerate(blk):
                                    w = {}
                                    if j == 0:
                                        w = dmax(msk_post[u],
                                                 (fr[f"ps{obank}"] if (idx == 0 and first)
                                                  else {}))
                                    t_pe = p.op("tensor", partial(
                                        I_mm,
                                        out=ps[obank][:, idx * 65:idx * 65 + 65],
                                        lhsT=atb[asl][:, j * 128:(j + 1) * 128],
                                        rhs=vsb[:, kc * 1040 + h * 65:
                                                kc * 1040 + h * 65 + 65],
                                        start=(first and j == 0),
                                        stop=(last and j == len(blk) - 1)),
                                        wait=w,
                                        inc=[("pe", 1)] if j == len(blk) - 1 else None)
                                fr[f"atb{asl}"] = t_pe
                                if last:
                                    av_post[h] = t_pe

                            S_ = dict(units=units, st=sched_st, em=sched_exp_mask,
                                      av=sched_av, av_post=av_post)
                            ast[qc] = S_
                            K = min(8, len(units))
                            for u in range(K):
                                sched_st(u)
                                sched_exp_mask(u)
                                if u >= LAG:
                                    sched_av(u - LAG)
                            S_["next"] = K

                        def stage_a2_units(qc):
                            qsl = qc % 2
                            S_ = ast[qc]
                            units = S_["units"]
                            sched_st, sched_exp_mask, sched_av = \
                                S_["st"], S_["em"], S_["av"]
                            for u in range(S_["next"], len(units)):
                                sched_st(u)
                                sched_exp_mask(u)
                                if u >= LAG:
                                    sched_av(u - LAG)
                            for u in range(max(0, len(units) - LAG), len(units)):
                                sched_av(u)
                            fr[f"qpadb{qsl}"] = {"pe": cnt["pe"]}

                        def stage_a2_evac(qc):
                            qsl = qc % 2
                            S_ = ast.pop(qc)
                            av_post = S_["av_post"]
                            rr = [None] * 3
                            for b in range(3):
                                nh = (7, 7, 2)[b]
                                h0 = (0, 7, 14)[b]
                                rr[b] = p.op("vector", partial(
                                    I_recip, out=fcols[:, h0:h0 + nh],
                                    in_=AP(ps[2 + b], 64, [[512, 128], [65, nh]])),
                                    wait=av_post[h0 + nh - 1], inc=[("dve", 1)])
                            t_ev = {}
                            for h in range(H):
                                b = 0 if h < 7 else (1 if h < 14 else 2)
                                idx = h - (0, 7, 14)[b]
                                t_ev = p.op("vector", partial(
                                    I_ts, out=otm[qsl][:, h * 64:(h + 1) * 64],
                                    in_=ps[2 + b][:, idx * 65:idx * 65 + 64],
                                    s1=fcols[:, h:h + 1], op0=ALU.mult),
                                    wait=dmax(rr[b],
                                              fr[f"otm{qsl}"] if h == 0 else {}),
                                    inc=[("dve", 1)])
                            fr["ps2"] = t_ev
                            fr["ps3"] = t_ev
                            fr["ps4"] = t_ev
                            rd[f"otm{qsl}"] = t_ev

                        def stage_tr(qc):
                            qsl = qc % 2
                            qs4 = qc % 3
                            t_tr = p.op("sync", partial(
                                I_dmat,
                                out=ofm[qs4][:, :].rearrange("p (ic f) -> p ic f", ic=8),
                                in_=otm[qsl][:, :].rearrange("p (ic f) -> p ic f", ic=8)),
                                wait=dmax(rd[f"otm{qsl}"], fr[f"ofm{qs4}"]),
                                inc=[(f"of{qs4}", 16)])
                            fr[f"otm{qsl}"] = t_tr
                            rd[f"ofm{qs4}"] = t_tr

                        def stage_b1_pe(qc):
                            qs4 = qc % 3
                            bst = {}
                            for half in range(2):
                                bank = 6 + half
                                t_pe = {}
                                for ic in range(8):
                                    w = dmax(rd[f"ofm{qs4}"], rd["woT"],
                                             fr[f"ps{bank}"]) if ic == 0 else {}
                                    t_pe = p.op("tensor", partial(
                                        I_mm, out=ps[bank][:, :],
                                        lhsT=ofm[qs4][:, ic * 128:(ic + 1) * 128],
                                        rhs=woT[:, ic * D + half * 512:
                                                ic * D + half * 512 + 512],
                                        start=(ic == 0), stop=(ic == 7)),
                                        wait=w, inc=[("pe", 1)] if ic == 7 else None)
                                bst[half] = t_pe
                            fr[f"ofm{qs4}"] = {"pe": cnt["pe"]}
                            bctx[qc] = bst

                        def stage_b1_stt(qc):
                            sl = qc % 2
                            qs4 = qc % 3
                            bst = bctx[qc]
                            t_xb = p.op("vector", partial(
                                I_tt, out=xbf_[qs4][:, :], a=xbf_[qs4][:, :],
                                b=beff_rep[:, :], op=ALU.add),
                                wait=dmax(ctx.pop(qc), rd["beff"]), inc=[("dve", 1)])
                            t_stt = {}
                            for half in range(2):
                                bank = 6 + half
                                t_stt = p.op("vector", partial(
                                    I_tt, out=x1pre[sl][:, half * 512:(half + 1) * 512],
                                    a=ps[bank][:, :],
                                    b=xbf_[qs4][:, half * 512:(half + 1) * 512],
                                    op=ALU.add),
                                    wait=dmax(bst[half], t_xb,
                                              fr[f"x1pre{sl}"] if half == 0 else {}),
                                    inc=[("dve", 1)])
                                fr[f"ps{bank}"] = t_stt
                            fr[f"xb{qs4}"] = t_stt
                            bst["stt"] = t_stt

                        def stage_b1_stats(qc):
                            sl = qc % 2
                            bst = bctx[qc]
                            st_ = statsb[sl]
                            t = p.op("vector", partial(
                                I_red, out=st_[:, 0:1], in_=x1pre[sl][:, :],
                                op=ALU.add, negate=True),
                                wait=dmax(bst["stt"], fr[f"stats{sl}"]), inc=[("dve", 1)])
                            t = p.op("vector", partial(
                                I_ts, out=st_[:, 1:2], in_=st_[:, 0:1],
                                s1=1.0 / D, op0=ALU.mult), wait=t, inc=[("dve", 1)])
                            t_sq = p.op("vector", partial(
                                I_tt, out=sqv[:, :], a=x1pre[sl][:, :],
                                b=x1pre[sl][:, :], op=ALU.mult),
                                wait=dmax(bst["stt"], fr["sqv"]), inc=[("dve", 1)])
                            t_ss = p.op("vector", partial(
                                I_red, out=st_[:, 2:3], in_=sqv[:, :],
                                op=ALU.add), wait=t_sq, inc=[("dve", 1)])
                            fr["sqv"] = t_ss
                            t_m2 = p.op("vector", partial(
                                I_tt, out=st_[:, 3:4], a=st_[:, 1:2], b=st_[:, 1:2],
                                op=ALU.mult), wait=t, inc=[("dve", 1)])
                            t_var = p.op("vector", partial(
                                I_stt, out=st_[:, 4:5], in0=st_[:, 2:3], scal=1.0 / D,
                                in1=st_[:, 3:4], op0=ALU.mult, op1=ALU.subtract),
                                wait=dmax(t_ss, t_m2), inc=[("dve", 1)])
                            bst["var"] = t_var

                        def stage_b1_tail(qc):
                            sl = qc % 2
                            row0 = si * S + qc * 128
                            bst = bctx.pop(qc)
                            st_ = statsb[sl]
                            t_ln1 = p.op("scalar", partial(
                                I_act, out=st_[:, 5:6], in_=st_[:, 4:5], func=AF.Ln,
                                bias=eps_sb[:, :], scale=1.0),
                                wait=dmax(bst["var"], rd["eps"]), inc=[("act", 1)])
                            t_rstd = p.op("scalar", partial(
                                I_act, out=st_[:, 6:7], in_=st_[:, 5:6], func=AF.Exp,
                                bias=0.0, scale=-0.5), wait=t_ln1, inc=[("act", 1)])
                            t_bln = p.op("scalar", partial(
                                I_act, out=st_[:, 7:8], in_=st_[:, 1:2],
                                func=AF.Identity, bias=0.0,
                                scale=st_[:, 6:7]), wait=t_rstd, inc=[("act", 1)])
                            t_x1 = p.op("scalar", partial(
                                I_act, out=x1pre[sl][:, :], in_=x1pre[sl][:, :],
                                func=AF.Identity, bias=st_[:, 7:8],
                                scale=st_[:, 6:7]),
                                wait=t_bln, inc=[("act", 1)])
                            fr[f"stats{sl}"] = t_x1
                            t_x1b = p.op("vector", partial(
                                I_copy, out=x1btm[sl][:, :], in_=x1pre[sl][:, :]),
                                wait=dmax(t_x1, fr[f"x1btm{sl}"]), inc=[("dve", 1)])
                            t_park = p.op("scalar", partial(
                                I_dma, out=x1_d.ap()[row0:row0 + 128, :],
                                in_=x1pre[sl][:, :]), wait=t_x1,
                                inc=[(f"park{qc % 2}", 16)])
                            rd[f"x1park{qc}"] = t_park
                            fr[f"x1pre{sl}"] = dmax(t_x1b, t_park)
                            t_tr = {}
                            for ic in range(8):
                                t_tr = p.op("sync", partial(
                                    I_dmat,
                                    out=X1b[:, ic * S + qc * 128: ic * S + qc * 128 + 128],
                                    in_=x1btm[sl][:, ic * 128:(ic + 1) * 128]),
                                    wait=t_x1b, inc=[(f"x1t{sl}", 16)])
                            fr[f"x1btm{sl}"] = t_tr
                            rd[f"X1b{qc}"] = t_tr

                        def stage_b2_pe(qc):
                            t_pe = {}
                            for ic in range(8):
                                w = dmax(rd[f"X1b{qc}"], rd["rw"],
                                         fr["ps5"]) if ic == 0 else {}
                                t_pe = p.op("tensor", partial(
                                    I_mm, out=ps[5][:, 0:NE],
                                    lhsT=X1b[:, ic * S + qc * 128: ic * S + qc * 128 + 128],
                                    rhs=rw_sb[:, ic * NE:(ic + 1) * NE],
                                    start=(ic == 0), stop=(ic == 7)),
                                    wait=w, inc=[("pe", 1)] if ic == 7 else None)
                            b2ctx[qc] = t_pe

                        def stage_b2_rest(qc):
                            qsl = qc % 2
                            gs = gscr[qsl]
                            t_pe = b2ctx.pop(qc)
                            t = p.op("vector", partial(
                                I_tt, out=gs[:, 0:8], a=ps[5][:, 0:NE], b=rb_rep[:, :],
                                op=ALU.add),
                                wait=dmax(t_pe, rd["rb"], fr[f"gscr{qsl}"]),
                                inc=[("dve", 1)])
                            fr["ps5"] = t
                            t_nm = p.op("vector", partial(
                                I_red, out=gs[:, 8:9], in_=gs[:, 0:8], op=ALU.max,
                                negate=True), wait=t, inc=[("dve", 1)])
                            t_mk = p.op("vector", partial(
                                I_ts, out=gs[:, 16:24], in_=gs[:, 0:8],
                                s1=gs[:, 8:9], s2=0.0, op0=ALU.add, op1=ALU.is_ge),
                                wait=t_nm, inc=[("dve", 1)])
                            t_e = p.op("scalar", partial(
                                I_act, out=gs[:, 24:32], in_=gs[:, 0:8], func=AF.Exp,
                                bias=gs[:, 8:9], scale=1.0), wait=t_nm, inc=[("act", 1)])
                            t_se = p.op("vector", partial(
                                I_red, out=gs[:, 9:10], in_=gs[:, 24:32], op=ALU.add),
                                wait=t_e, inc=[("dve", 1)])
                            t_rse = p.op("vector", partial(
                                I_recip, out=gs[:, 10:11], in_=gs[:, 9:10]),
                                wait=t_se, inc=[("dve", 1)])
                            t_gm = p.op("vector", partial(
                                I_tt, out=gs[:, 32:40], a=gs[:, 24:32], b=gs[:, 16:24],
                                op=ALU.mult), wait=dmax(t_mk, t_e), inc=[("dve", 1)])
                            t_g = p.op("vector", partial(
                                I_ts, out=gates_tm[:, qc * NE:(qc + 1) * NE],
                                in_=gs[:, 32:40], s1=gs[:, 10:11], op0=ALU.mult),
                                wait=dmax(t_gm, t_rse), inc=[("dve", 1)])
                            rd[f"gates{qc}"] = t_g
                            gsl = qc % 2
                            t_gb = p.op("vector", partial(
                                I_copy, out=gpad[gsl][:, 0:NE],
                                in_=gates_tm[:, qc * NE:(qc + 1) * NE]),
                                wait=dmax(t_g, rd["gpad_z"], fr[f"gpad{gsl}"]),
                                inc=[("dve", 1)])
                            t_gt = p.op("sync", partial(
                                I_dmat, out=gfm[:, qc * 128:(qc + 1) * 128],
                                in_=gpad[gsl][:, :]),
                                wait=t_gb, inc=[(f"gf{gsl}", 16)])
                            fr[f"gpad{gsl}"] = t_gt
                            fr[f"gscr{qsl}"] = t_gt
                            rd[f"gfm{qc}"] = t_gt

                        sched_qprep(0)
                        sched_qprep(1)
                        sched_xload(0)
                        sched_xload(1)
                        stage_a1(0)
                        stage_a2_units(0)
                        stage_a2_evac(0)
                        stage_a1(1)
                        stage_a2_units(1)
                        stage_a2_evac(1)
                        sched_qprep(2)
                        stage_tr(0)
                        stage_tr(1)
                        for it in range(NQC + 3):
                            if 0 <= it - 1 < NQC:
                                stage_b1_pe(it - 1)
                            if 0 <= it - 2 < NQC:
                                stage_b1_stats(it - 2)
                            if it + 2 < NQC:
                                stage_a1(it + 2)
                            if 0 <= it - 1 < NQC:
                                stage_b1_stt(it - 1)
                            if 0 <= it - 2 < NQC:
                                stage_b1_tail(it - 2)
                            if 0 <= it - 3 < NQC:
                                stage_b2_pe(it - 3)
                                stage_b2_rest(it - 3)
                            if it + 2 < NQC:
                                stage_a2_units(it + 2)
                            if it + 3 < NQC:
                                sched_qprep(it + 3)
                            if it + 2 < NQC:
                                stage_a2_evac(it + 2)
                                sched_xload(it + 2)
                                stage_tr(it + 2)

                        p.drain()
                        p.emit()
                    nc.all_engine_barrier()

                with contextlib.ExitStack() as scf:
                    Ef = scf.enter_context
                    acc = Ef(nc.sbuf_tensor(f"acc_{si}", [128, NQC * D], F32))
                    w1b = [Ef(nc.sbuf_tensor(f"w1b{i}_{si}", [128, 8 * 1024], BF16))
                           for i in range(2)]
                    w2b = [Ef(nc.sbuf_tensor(f"w2b{i}_{si}", [128, 8 * 1024], BF16))
                           for i in range(2)]
                    hsc = [Ef(nc.sbuf_tensor(f"hsc{i}_{si}", [128, 8 * 512], BF16))
                           for i in range(2)]
                    outsb = [Ef(nc.sbuf_tensor(f"outsb{i}_{si}", [128, D], F32))
                             for i in range(2)]

                    p = Prog(nc, sems, cnt)
                    for i in range(2):
                        fr[f"w1b{i}"] = {}
                        fr[f"w2b{i}"] = {}
                        fr[f"hsc{i}"] = {}
                        fr[f"outsb{i}"] = {}

                    ginit = {}

                    def sched_accinit_group(g):
                        t_l = {}
                        for qc in range(g * 4, g * 4 + 4):
                            row0 = si * S + qc * 128
                            t_l = p.op("scalar", partial(
                                I_dma, out=acc[:, qc * D:(qc + 1) * D],
                                in_=x1_d.ap()[row0:row0 + 128, :]),
                                wait=rd[f"x1park{qc}"], inc=[("accl", 16)])
                        tg = {}
                        for qc in range(g * 4, g * 4 + 4):
                            for half in range(2):
                                bank = 6 + half
                                t_pe = p.op("tensor", partial(
                                    I_mm, out=ps[bank][:, :],
                                    lhsT=gfm[:, qc * 128:(qc + 1) * 128],
                                    rhs=b2pad[:, half * 512:(half + 1) * 512],
                                    start=True, stop=True),
                                    wait=dmax(rd[f"gfm{qc}"], rd["b2pad"],
                                              fr[f"ps{bank}"]),
                                    inc=[("pe", 1)])
                                tg = p.op("vector", partial(
                                    I_stt, out=acc[:, qc * D + half * 512:
                                                   qc * D + half * 512 + 512],
                                    in0=ps[bank][:, :], scal=1.0,
                                    in1=acc[:, qc * D + half * 512:
                                            qc * D + half * 512 + 512],
                                    op0=ALU.mult, op1=ALU.add),
                                    wait=dmax(t_pe, t_l), inc=[("dve", 1)])
                                fr[f"ps{bank}"] = tg
                        ginit[g] = tg

                    def sched_wload(e, b, wsl):
                        t1 = p.op("gpsimd", partial(
                            I_dma, out=w1b[wsl][:, :].rearrange("p (c f) -> p c f", c=8),
                            in_=w1_d.ap()[e, :, b * 1024:(b + 1) * 1024].rearrange(
                                "(c p) f -> p c f", p=128)),
                            wait=fr[f"w1b{wsl}"], inc=[(f"wa{wsl}", 16)])
                        t2 = p.op("gpsimd", partial(
                            I_dma, out=w2b[wsl][:, :].rearrange("p (m o) -> p m o", m=8),
                            in_=w2_d.ap()[e, b * 1024:(b + 1) * 1024, :].rearrange(
                                "(m p) o -> p m o", p=128)),
                            wait=fr[f"w2b{wsl}"], inc=[(f"wb{wsl}", 16)])
                        return t1, t2

                    eb_list = [(e, b) for e in range(NE) for b in range(NB)]
                    w_rd = {}
                    w_rd[0] = sched_wload(*eb_list[0], 0)
                    for idx, (e, b) in enumerate(eb_list):
                        wsl = idx % 2
                        if idx + 1 < len(eb_list):
                            w_rd[idx + 1] = sched_wload(*eb_list[idx + 1], (idx + 1) % 2)
                        t_w1, t_w2 = w_rd[idx]
                        for tt in range(4):
                            if idx == 0:
                                sched_accinit_group(tt)
                            hsl = tt % 2
                            h_done = {}
                            for m in range(8):
                                bank = m % 2
                                t_pe = {}
                                for dc in range(8):
                                    w = dmax(t_w1, fr[f"ps{bank}"],
                                             rd[f"X1b{tt * 4}"], rd[f"X1b{tt * 4 + 1}"],
                                             rd[f"X1b{tt * 4 + 2}"], rd[f"X1b{tt * 4 + 3}"],
                                             fr[f"hsc{hsl}"] if m == 0 else {}) \
                                        if dc == 0 else {}
                                    t_pe = p.op("tensor", partial(
                                        I_mm, out=ps[bank][:, :],
                                        lhsT=w1b[wsl][:, dc * 1024 + m * 128:
                                                      dc * 1024 + m * 128 + 128],
                                        rhs=X1b[:, dc * S + tt * 512:
                                                dc * S + tt * 512 + 512],
                                        start=(dc == 0), stop=(dc == 7)),
                                        wait=w, inc=[("pe", 1)] if dc == 7 else None)
                                t_h = p.op("scalar", partial(
                                    I_act, out=hsc[hsl][:, m * 512:(m + 1) * 512],
                                    in_=ps[bank][:, :], func=AF.Relu,
                                    bias=b1Tf[:, (b * 8 + m) * NE + e:
                                              (b * 8 + m) * NE + e + 1],
                                    scale=1.0),
                                    wait=dmax(t_pe, rd["b1T"]), inc=[("act", 1)])
                                fr[f"ps{bank}"] = t_h
                                h_done = t_h
                            rd[f"hsc{hsl}"] = h_done
                            last_y = {}
                            for ts_ in range(4):
                                qc = tt * 4 + ts_
                                for half in range(2):
                                    bank = 2 + (ts_ * 2 + half) % 3
                                    t_pe = {}
                                    for m in range(8):
                                        w = dmax(rd[f"hsc{hsl}"], t_w2,
                                                 fr[f"ps{bank}"]) if m == 0 else {}
                                        t_pe = p.op("tensor", partial(
                                            I_mm, out=ps[bank][:, :],
                                            lhsT=hsc[hsl][:, m * 512 + ts_ * 128:
                                                          m * 512 + ts_ * 128 + 128],
                                            rhs=w2b[wsl][:, m * 1024 + half * 512:
                                                         m * 1024 + half * 512 + 512],
                                            start=(m == 0), stop=(m == 7)),
                                            wait=w, inc=[("pe", 1)] if m == 7 else None)
                                    t_y = p.op("vector", partial(
                                        I_stt,
                                        out=acc[:, qc * D + half * 512:
                                                qc * D + half * 512 + 512],
                                        in0=ps[bank][:, :],
                                        scal=gates_tm[:, qc * NE + e:qc * NE + e + 1],
                                        in1=acc[:, qc * D + half * 512:
                                                qc * D + half * 512 + 512],
                                        op0=ALU.mult, op1=ALU.add),
                                        wait=dmax(t_pe, rd[f"gates{qc}"],
                                                  ginit[tt]),
                                        inc=[("dve", 1)])
                                    fr[f"ps{bank}"] = t_y
                                    last_y = t_y
                            fr[f"hsc{hsl}"] = last_y
                        fr[f"w1b{wsl}"] = last_y
                        fr[f"w2b{wsl}"] = last_y

                    acc_done = last_y
                    t_out = {}
                    for qc in range(NQC):
                        qsl = qc % 2
                        st_ = statsb[qsl]
                        a_ap = acc[:, qc * D:(qc + 1) * D]
                        t = p.op("vector", partial(
                            I_red, out=st_[:, 0:1], in_=a_ap, op=ALU.add, negate=True),
                            wait=dmax(acc_done, fr[f"stats{qsl}"]), inc=[("dve", 1)])
                        t = p.op("vector", partial(
                            I_ts, out=st_[:, 1:2], in_=st_[:, 0:1], s1=1.0 / D,
                            op0=ALU.mult), wait=t, inc=[("dve", 1)])
                        t_sq = p.op("vector", partial(
                            I_tt, out=outsb[qsl][:, :], a=a_ap, b=a_ap, op=ALU.mult),
                            wait=dmax(t, fr[f"outsb{qsl}"]), inc=[("dve", 1)])
                        t_ss = p.op("vector", partial(
                            I_red, out=st_[:, 2:3], in_=outsb[qsl][:, :], op=ALU.add),
                            wait=t_sq, inc=[("dve", 1)])
                        t_m2 = p.op("vector", partial(
                            I_tt, out=st_[:, 3:4], a=st_[:, 1:2], b=st_[:, 1:2],
                            op=ALU.mult), wait=t, inc=[("dve", 1)])
                        t_var = p.op("vector", partial(
                            I_stt, out=st_[:, 4:5], in0=st_[:, 2:3], scal=1.0 / D,
                            in1=st_[:, 3:4], op0=ALU.mult, op1=ALU.subtract),
                            wait=dmax(t_ss, t_m2), inc=[("dve", 1)])
                        t_ln = p.op("scalar", partial(
                            I_act, out=st_[:, 5:6], in_=st_[:, 4:5], func=AF.Ln,
                            bias=eps_sb[:, :], scale=1.0),
                            wait=dmax(t_var, rd["eps"]), inc=[("act", 1)])
                        t_rstd = p.op("scalar", partial(
                            I_act, out=st_[:, 6:7], in_=st_[:, 5:6], func=AF.Exp,
                            bias=0.0, scale=-0.5), wait=t_ln, inc=[("act", 1)])
                        t_bln = p.op("vector", partial(
                            I_tt, out=st_[:, 7:8], a=st_[:, 1:2], b=st_[:, 6:7],
                            op=ALU.mult), wait=t_rstd, inc=[("dve", 1)])
                        t_x2 = p.op("scalar", partial(
                            I_act, out=outsb[qsl][:, :], in_=a_ap, func=AF.Identity,
                            bias=st_[:, 7:8], scale=st_[:, 6:7]),
                            wait=t_bln, inc=[("act", 1)])
                        fr[f"stats{qsl}"] = t_x2
                        row0 = si * S + qc * 128
                        t_out = p.op("sync", partial(
                            I_dma, out=y_d.ap()[row0:row0 + 128, :],
                            in_=outsb[qsl][:, :]), wait=t_x2, inc=[("out", 16)])
                        fr[f"outsb{qsl}"] = t_out

                    if si == SEQS - 1:
                        p.op("gpsimd", None, wait={"out": cnt["out"]})
                    p.drain()
                    p.emit()
                nc.all_engine_barrier()

    return nc


_NC_CACHE = []


def kernel(**inputs):
    x = np.ascontiguousarray(np.asarray(inputs["L0_attn_x"], dtype=np.float32))
    full = {
        "in_proj_w": np.ascontiguousarray(np.asarray(inputs["L0_attn_in_proj_w"], np.float32)),
        "in_proj_b": np.ascontiguousarray(np.asarray(inputs["L0_attn_in_proj_b"], np.float32)),
        "out_proj_w": np.ascontiguousarray(np.asarray(inputs["L0_attn_out_proj_w"], np.float32)),
        "out_proj_b": np.ascontiguousarray(np.asarray(inputs["L0_attn_out_proj_b"], np.float32)),
        "router_W": np.ascontiguousarray(np.asarray(inputs["L0_ffn_router_W"], np.float32)),
        "router_b": np.ascontiguousarray(np.asarray(inputs["L0_ffn_router_b"], np.float32)),
        "W1": np.ascontiguousarray(np.asarray(inputs["L0_ffn_W1"], np.float32)),
        "b1": np.ascontiguousarray(np.asarray(inputs["L0_ffn_b1"], np.float32)),
        "W2": np.ascontiguousarray(np.asarray(inputs["L0_ffn_W2"], np.float32)),
        "b2": np.ascontiguousarray(np.asarray(inputs["L0_ffn_b2"], np.float32)),
    }
    if not _NC_CACHE:
        _NC_CACHE.append(build_program())
    nc = _NC_CACHE[0]
    in_maps = []
    for c in range(NCORES):
        m = {"x": x[SEQS * c:SEQS * (c + 1)].reshape(SEQS * S, D)}
        m.update(full)
        in_maps.append(m)
    res = run_bass_kernel_spmd(nc, in_maps, core_ids=list(range(NCORES)))
    out = np.concatenate(
        [res.results[c]["y"].reshape(SEQS, S, D) for c in range(NCORES)], axis=0)
    return out


if __name__ == "__main__":
    nc = build_program()
    print("instructions:", nc.n_instructions())


# version 17
# speedup vs baseline: 1.1530x; 1.0092x over previous
import contextlib
from functools import partial

import numpy as np

import concourse.bass as bass
import concourse.mybir as mybir
from concourse.bass import AP
from concourse.bass_utils import run_bass_kernel_spmd

F32 = mybir.dt.float32
BF16 = mybir.dt.bfloat16
AF = mybir.ActivationFunctionType
ALU = mybir.AluOpType
AXX = mybir.AxisListType.X

NCORES = 8
SEQS = 2
S = 2048
D = 1024
H = 16
NG = 4
GT = 512
NQC = 16
NE = 8
DFF = 4096
NB = 4
KSLOT = 6
EPS = 1e-5


def I_mm(eng, out=None, lhsT=None, rhs=None, start=True, stop=True):
    return eng.matmul(out, lhsT, rhs, start=start, stop=stop)

def I_act(eng, out=None, in_=None, func=None, bias=0.0, scale=1.0):
    return eng.activation(out, in_, func, bias=bias, scale=scale)

def I_dma(eng, out=None, in_=None):
    return eng.dma_start(out, in_)

def I_dmat(eng, out=None, in_=None):
    return eng.dma_start_transpose(out, in_)

def I_copy(eng, out=None, in_=None):
    return eng.tensor_copy(out, in_)

def I_tt(eng, out=None, a=None, b=None, op=None):
    return eng.tensor_tensor(out, a, b, op)

def I_ts(eng, out=None, in_=None, s1=None, s2=None, op0=None, op1=None):
    if op1 is None:
        return eng.tensor_single_scalar(out, in_, s1, op0)
    return eng.tensor_scalar(out, in_, s1, s2, op0, op1)

def I_stt(eng, out=None, in0=None, scal=None, in1=None, op0=None, op1=None):
    return eng.scalar_tensor_tensor(out, in0, scal, in1, op0, op1)

def I_red(eng, out=None, in_=None, op=None, negate=False):
    return eng.tensor_reduce(out, in_, AXX, op, negate=negate)

def I_recip(eng, out=None, in_=None):
    return eng.reciprocal(out, in_)

def I_memset(eng, ap=None, c=0.0):
    return eng.memset(ap, c)

def I_iota(eng, ap=None, pat=None, cm=0):
    return eng.iota(ap, pat, channel_multiplier=cm,
                    allow_small_or_imprecise_dtypes=True)


def dmax(*ds):
    out = {}
    for d in ds:
        if d:
            for k, v in d.items():
                if out.get(k, -1) < v:
                    out[k] = v
    return out


class Prog:

    ENGINES = ("tensor", "scalar", "vector", "gpsimd", "sync")

    def __init__(self, nc, sems, cnt):
        self.nc = nc
        self.sems = sems
        self.cnt = cnt
        self.q = {e: [] for e in self.ENGINES}

    COMPUTE_SEMS = ("pe", "act", "dve", "pool")

    def op(self, eng, fn, wait=None, inc=None):
        incs = inc or []
        w = dict(wait or {})
        for sname in list(w):
            if sname not in self.COMPUTE_SEMS:
                w[sname] = max(w[sname], self.cnt[sname])
        for sname, n in incs:
            self.cnt[sname] += n
        post = {sname: self.cnt[sname] for sname, _ in incs}
        self.q[eng].append((fn, w, incs))
        return post

    def drain(self):
        w = {s: self.cnt[s] for s in self.cnt
             if s not in self.COMPUTE_SEMS and self.cnt[s] > 0}
        self.op("gpsimd", None, wait=w)

    AUDIT = []

    def emit(self):
        Prog.AUDIT.append({e: list(self.q[e]) for e in self.ENGINES})
        with self.nc.Block() as block:
            for eng_name in self.ENGINES:
                ops = self.q[eng_name]
                if not ops:
                    continue

                def make_stream(ops=ops):
                    def stream(eng):
                        last_wait = {}
                        for fn, waits, incs in ops:
                            for sname, v in waits.items():
                                if last_wait.get(sname, -1) < v:
                                    eng.wait_ge(self.sems[sname], v)
                                    last_wait[sname] = v
                            ins = fn(eng) if fn is not None else None
                            for sname, n in incs:
                                ins.then_inc(self.sems[sname], n)
                    return stream

                getattr(block, eng_name)(make_stream())


def build_program():
    nc = bass.Bass("TRN2", target_bir_lowering=False)

    x_d = nc.dram_tensor("x", [SEQS * S, D], F32, kind="ExternalInput")
    ipw_d = nc.dram_tensor("in_proj_w", [3 * D, D], F32, kind="ExternalInput")
    ipb_d = nc.dram_tensor("in_proj_b", [3 * D], F32, kind="ExternalInput")
    opw_d = nc.dram_tensor("out_proj_w", [D, D], F32, kind="ExternalInput")
    opb_d = nc.dram_tensor("out_proj_b", [D], F32, kind="ExternalInput")
    rw_d = nc.dram_tensor("router_W", [D, NE], F32, kind="ExternalInput")
    rb_d = nc.dram_tensor("router_b", [NE], F32, kind="ExternalInput")
    w1_d = nc.dram_tensor("W1", [NE, D, DFF], F32, kind="ExternalInput")
    b1_d = nc.dram_tensor("b1", [NE, DFF], F32, kind="ExternalInput")
    w2_d = nc.dram_tensor("W2", [NE, DFF, D], F32, kind="ExternalInput")
    b2_d = nc.dram_tensor("b2", [NE, D], F32, kind="ExternalInput")
    y_d = nc.dram_tensor("y", [SEQS * S, D], F32, kind="ExternalOutput")

    wt_d = nc.dram_tensor("wt_scratch", [128, 8 * 4 * D], BF16, kind="Internal")
    x1_d = nc.dram_tensor("x1_scratch", [SEQS * S, D], F32, kind="Internal")
    beff_d = nc.dram_tensor("beff_scratch", [D], F32, kind="Internal")

    sem_names = ("pe", "act", "dve", "pool",
                 "su_b2", "su_rw", "su_rb", "su_bt", "su_btt", "su_b1",
                 "su_b1t", "su_bo", "su_be", "su_wts",
                 "ws0", "ws1", "wt0", "wt1",
                 "wq0", "wq1", "xg0", "xg1", "xt0", "xt1", "wot",
                 "of0", "of1", "of2", "x1t0", "x1t1", "gf0", "gf1",
                 "xf0", "xf1", "park0", "park1", "qp0", "qp1", "accl", "wa0", "wa1", "wb0", "wb1",
                 "out")
    cnt = {sname: 0 for sname in sem_names}
    fr = {}
    rd = {}
    for i in range(8):
        fr[f"ps{i}"] = {}

    with contextlib.ExitStack() as top:
        E = top.enter_context
        sems = {sname: E(nc.semaphore(sname)) for sname in sem_names}

        ps = [E(nc.psum_tensor(f"ps{i}", [128, 512], F32)) for i in range(8)]
        beff_rep = E(nc.sbuf_tensor("beff_rep", [128, D], F32))
        rb_rep = E(nc.sbuf_tensor("rb_rep", [128, NE], F32))
        rw_sb = E(nc.sbuf_tensor("rw_sb", [128, 8 * NE], BF16))
        b2pad = E(nc.sbuf_tensor("b2pad", [128, D], BF16))
        bTf = E(nc.sbuf_tensor("bTf", [128, 32], F32))
        b1Tf = E(nc.sbuf_tensor("b1Tf", [128, 32 * NE], F32))
        maskL = E(nc.sbuf_tensor("maskL", [128, 128], BF16))
        maskR = E(nc.sbuf_tensor("maskR", [128, 128], BF16))
        ones_col = E(nc.sbuf_tensor("ones_col", [128, 1], BF16))
        eps_sb = E(nc.sbuf_tensor("eps_sb", [128, 1], F32))
        gates_tm = E(nc.sbuf_tensor("gates_tm", [128, NQC * NE], F32))
        gfm = E(nc.sbuf_tensor("gfm", [128, NQC * 128], BF16))
        statsb = [E(nc.sbuf_tensor(f"stats{i}", [128, 16], F32)) for i in range(2)]
        gscr = [E(nc.sbuf_tensor(f"gscr{i}", [128, 40], F32)) for i in range(2)]
        gpad = [E(nc.sbuf_tensor(f"gpad{i}", [128, 128], BF16)) for i in range(2)]

        with contextlib.ExitStack() as sc0:
            E0 = sc0.enter_context
            wT_full = E0(nc.sbuf_tensor("wT_full", [128, 8 * 4 * D], BF16))
            wstag = [E0(nc.sbuf_tensor(f"wstag{i}", [128, D], BF16)) for i in range(2)]
            btmp = E0(nc.sbuf_tensor("btmp", [32, 128], BF16))
            b1tmp = E0(nc.sbuf_tensor("b1tmp", [16, DFF], BF16))
            b1T_bf = E0(nc.sbuf_tensor("b1T_bf", [128, 32 * 16], BF16))
            bT_bf = E0(nc.sbuf_tensor("bT_bf", [128, 32], BF16))
            iop = E0(nc.sbuf_tensor("iop", [128, 128], F32))
            ioq = E0(nc.sbuf_tensor("ioq", [128, 128], F32))
            mtmp = E0(nc.sbuf_tensor("mtmp", [128, 128], F32))
            bout_row = E0(nc.sbuf_tensor("bout_row", [1, D], F32))
            beff_row = E0(nc.sbuf_tensor("beff_row", [1, D], F32))

            p = Prog(nc, sems, cnt)

            p.op("gpsimd", partial(I_iota, ap=iop[:, :], pat=[[0, 128]], cm=1))
            t_io = p.op("gpsimd", partial(I_iota, ap=ioq[:, :], pat=[[1, 128]], cm=0),
                        inc=[("pool", 1)])
            t = p.op("vector", partial(I_tt, out=mtmp[:, :], a=iop[:, :], b=ioq[:, :],
                                       op=ALU.is_ge), wait=t_io, inc=[("dve", 1)])
            t = p.op("vector", partial(I_copy, out=maskL[:, :], in_=mtmp[:, :]),
                     wait=t, inc=[("dve", 1)])
            t = p.op("vector", partial(I_tt, out=mtmp[:, :], a=iop[:, :], b=ioq[:, :],
                                       op=ALU.is_le), wait=dmax(t_io, t), inc=[("dve", 1)])
            t = p.op("vector", partial(I_copy, out=maskR[:, :], in_=mtmp[:, :]),
                     wait=t, inc=[("dve", 1)])
            rd["masks"] = t
            rd["ones"] = p.op("vector", partial(I_memset, ap=ones_col[:, :], c=1.0),
                              inc=[("dve", 1)])
            rd["eps"] = p.op("vector", partial(I_memset, ap=eps_sb[:, :], c=float(EPS)),
                             inc=[("dve", 1)])
            t_b2z = p.op("vector", partial(I_memset, ap=b2pad[:, :], c=0.0), inc=[("dve", 1)])
            t_btz = p.op("vector", partial(I_memset, ap=btmp[:, :], c=0.0), inc=[("dve", 1)])
            t_b1z = p.op("vector", partial(I_memset, ap=b1tmp[:, :], c=0.0), inc=[("dve", 1)])
            t_gp = {}
            for gi in range(2):
                t_gp = p.op("vector", partial(I_memset, ap=gpad[gi][:, :], c=0.0),
                            inc=[("dve", 1)])
                fr[f"gpad{gi}"] = {}
            rd["gpad_z"] = t_gp

            rd["b2pad"] = p.op("gpsimd", partial(I_dma, out=b2pad[0:NE, :], in_=b2_d.ap()),
                               wait=t_b2z, inc=[("su_b2", 16)])
            rd["rw"] = p.op("gpsimd", partial(
                I_dma, out=rw_sb[:, :].rearrange("p (c e) -> p c e", c=8),
                in_=rw_d.ap().rearrange("(c p) e -> p c e", p=128)), inc=[("su_rw", 16)])
            rd["rb"] = p.op("sync", partial(
                I_dma, out=rb_rep[:, :], in_=AP(rb_d, 0, [[0, 128], [1, NE]])),
                inc=[("su_rb", 16)])
            t = p.op("gpsimd", partial(
                I_dma, out=btmp[0:24, :],
                in_=ipb_d.ap().rearrange("(c p) -> c p", p=128)),
                wait=t_btz, inc=[("su_bt", 16)])
            t = p.op("sync", partial(I_dmat, out=bT_bf[:, :], in_=btmp[:, :]),
                     wait=t, inc=[("su_btt", 16)])
            rd["bT"] = p.op("vector", partial(I_copy, out=bTf[:, :], in_=bT_bf[:, :]),
                            wait=t, inc=[("dve", 1)])
            t_b1 = p.op("gpsimd", partial(I_dma, out=b1tmp[0:NE, :], in_=b1_d.ap()),
                        wait=t_b1z, inc=[("su_b1", 16)])
            t = p.op("sync", partial(
                I_dmat,
                out=b1T_bf[:, :].rearrange("p (c e) -> p c e", c=32),
                in_=b1tmp[:, :].rearrange("p (c f) -> p c f", c=32)),
                wait=t_b1, inc=[("su_b1t", 16)])
            rd["b1T"] = p.op("vector", partial(
                I_copy,
                out=b1Tf[:, :].rearrange("p (c e) -> p c e", e=NE),
                in_=b1T_bf[:, :].rearrange("p (c e) -> p c e", e=16)[:, :, 0:NE]),
                wait=t, inc=[("dve", 1)])
            t_bout = p.op("sync", partial(I_dma, out=bout_row[:, :], in_=opb_d.ap()[None, :]),
                          inc=[("su_bo", 16)])

            wst_fr = [{}, {}]
            t_tr = {}
            for oc in range(32):
                sl = oc % 2
                src = (ipw_d.ap()[oc * 128:(oc + 1) * 128, :] if oc < 24 else
                       opw_d.ap()[(oc - 24) * 128:(oc - 24) * 128 + 128, :])
                t_ld = p.op("gpsimd", partial(I_dma, out=wstag[sl][:, :], in_=src),
                            wait=wst_fr[sl], inc=[(f"ws{sl}", 16)])
                t_tr = p.op("sync", partial(
                    I_dmat,
                    out=wT_full[:, oc * 1024:(oc + 1) * 1024].rearrange(
                        "p (ic f) -> p ic f", ic=8),
                    in_=wstag[sl][:, :].rearrange("p (ic f) -> p ic f", ic=8)),
                    wait=t_ld, inc=[(f"wt{oc % 2}", 16)])
                wst_fr[sl] = t_tr
            rd["wT_full"] = {"wt0": cnt["wt0"], "wt1": cnt["wt1"]}
            rd["wt_dram"] = p.op("sync", partial(
                I_dma, out=wt_d.ap(), in_=wT_full[:, :]),
                wait=rd["wT_full"], inc=[("su_wts", 16)])

            t_dve = {}
            for half in range(2):
                bank = 6 + half
                t_pe = {}
                for j in range(4):
                    oc = 24 + half * 4 + j
                    for ic in range(8):
                        w = dmax(rd["wT_full"], rd["bT"]) if (j == 0 and ic == 0) else {}
                        t_pe = p.op("tensor", partial(
                            I_mm, out=ps[bank][0:1, j * 128:(j + 1) * 128],
                            lhsT=bT_bf[:, 16 + ic:17 + ic],
                            rhs=wT_full[:, oc * 1024 + ic * 128:
                                        oc * 1024 + ic * 128 + 128],
                            start=(ic == 0), stop=(ic == 7)),
                            wait=w,
                            inc=[("pe", 1)] if (j == 3 and ic == 7) else None)
                t_dve = p.op("vector", partial(
                    I_tt, out=beff_row[:, half * 512:(half + 1) * 512],
                    a=ps[bank][0:1, 0:512],
                    b=bout_row[:, half * 512:(half + 1) * 512], op=ALU.add),
                    wait=dmax(t_pe, t_bout), inc=[("dve", 1)])
                fr[f"ps{bank}"] = t_dve
            t = p.op("sync", partial(I_dma, out=beff_d.ap()[None, :], in_=beff_row[:, :]),
                     wait=t_dve, inc=[("su_be", 16)])
            rd["beff"] = p.op("sync", partial(
                I_dma, out=beff_rep[:, :], in_=AP(beff_d, 0, [[0, 128], [1, D]])),
                wait=t, inc=[("su_be", 16)])

            p.drain()
            p.emit()
        nc.all_engine_barrier()

        fr["gates"] = {}
        fr["gfm"] = {}
        for si in range(SEQS):
            with contextlib.ExitStack() as scs:
                Es = scs.enter_context
                X1b = Es(nc.sbuf_tensor(f"X1b_{si}", [128, 8 * S], BF16))

                sc_att = contextlib.ExitStack()
                with sc_att:
                    kfm = sc_att.enter_context(
                        nc.sbuf_tensor(f"kfm_{si}", [128, 8 * S], BF16))
                    vsb = sc_att.enter_context(
                        nc.sbuf_tensor(f"vsb_{si}", [128, NQC * 1040], BF16))
                    qg = sc_att.enter_context(
                        nc.sbuf_tensor(f"qg_{si}", [128, 8 * S], BF16))

                    with contextlib.ExitStack() as scq:
                        Eq = scq.enter_context
                        wqb = [Eq(nc.sbuf_tensor(f"wqb{i}_{si}", [128, 8 * 512], BF16))
                               for i in range(2)]
                        xgbf = [Eq(nc.sbuf_tensor(f"xgbf{i}_{si}", [128, 4 * D], BF16))
                                for i in range(2)]
                        xfm = [Eq(nc.sbuf_tensor(f"xfm{i}_{si}", [128, 8 * GT], BF16))
                               for i in range(2)]

                        p = Prog(nc, sems, cnt)
                        for i in range(2):
                            fr[f"wqb{i}"] = {}
                            fr[f"xgbf{i}"] = {}
                            fr[f"xfm{i}"] = {}

                        def sched_group_loads(g):
                            sl = g % 2
                            row0 = si * S + g * GT
                            t_ld = {}
                            for tc in range(4):
                                t_ld = p.op("gpsimd", partial(
                                    I_dma,
                                    out=AP(xgbf[sl], tc * 128,
                                           [[4096, 128], [512, 8], [1, 128]]),
                                    in_=AP(x_d, (row0 + tc * 128) * D,
                                           [[D, 128], [128, 8], [1, 128]])),
                                    wait=fr[f"xgbf{sl}"] if tc == 0 else {},
                                    inc=[(f"xg{sl}", 16)])
                            t_tr = p.op("sync", partial(
                                I_dmat,
                                out=xfm[sl][:, :].rearrange("p (b f) -> p b f", b=32),
                                in_=xgbf[sl][:, :].rearrange("p (b f) -> p b f", b=32)),
                                wait=dmax(t_ld, fr[f"xfm{sl}"]), inc=[(f"xt{sl}", 16)])
                            fr[f"xgbf{sl}"] = t_tr
                            rd[f"xfm{sl}"] = t_tr

                        t_vi = p.op("vector", partial(I_memset, ap=vsb[:, :], c=1.0),
                                    inc=[("dve", 1)])
                        rd["vsb_ones"] = t_vi

                        sched_group_loads(0)

                        for g in range(NG):
                            sl = g % 2
                            xfm_rd = rd[f"xfm{sl}"]
                            for og in range(6):
                                wsl = og % 2
                                t_w = p.op("gpsimd", partial(
                                    I_dma, out=wqb[wsl][:, :],
                                    in_=wt_d.ap()[:, og * 4096:(og + 1) * 4096]),
                                    wait=dmax(rd["wt_dram"], fr[f"wqb{wsl}"]),
                                    inc=[(f"wq{wsl}", 16)])
                                if og < 4:
                                    for j in range(4):
                                        oc = og * 4 + j
                                        bank = 6 + (j % 2)
                                        t_pe = {}
                                        for ic in range(8):
                                            w = dmax(t_w, xfm_rd, fr[f"ps{bank}"]) if ic == 0 else {}
                                            t_pe = p.op("tensor", partial(
                                                I_mm, out=ps[bank][:, :],
                                                lhsT=wqb[wsl][:, j * 1024 + ic * 128:
                                                              j * 1024 + ic * 128 + 128],
                                                rhs=xfm[sl][:, ic * GT:(ic + 1) * GT],
                                                start=(ic == 0), stop=(ic == 7)),
                                                wait=w, inc=[("pe", 1)] if ic == 7 else None)
                                        fr[f"wqb{wsl}"] = dmax(fr[f"wqb{wsl}"], t_pe)
                                        if oc < 8:
                                            t = p.op("scalar", partial(
                                                I_act,
                                                out=qg[:, oc * S + g * GT:
                                                        oc * S + g * GT + GT],
                                                in_=ps[bank][:, :], func=AF.Identity,
                                                bias=bTf[:, oc:oc + 1], scale=1.0),
                                                wait=dmax(t_pe, rd["bT"]),
                                                inc=[("act", 1)])
                                            fr[f"ps{bank}"] = t
                                        else:
                                            koc = oc - 8
                                            t = p.op("scalar", partial(
                                                I_act,
                                                out=kfm[:, koc * S + g * GT:
                                                        koc * S + g * GT + GT],
                                                in_=ps[bank][:, :], func=AF.Identity,
                                                bias=bTf[:, 8 + koc:9 + koc], scale=1.0),
                                                wait=dmax(t_pe, rd["bT"]),
                                                inc=[("act", 1)])
                                            fr[f"ps{bank}"] = t
                                else:
                                    vh = og - 4
                                    for tc in range(4):
                                        slot = g * 4 + tc
                                        bank = 6 + (tc % 2)
                                        t_pe = {}
                                        for jv in range(4):
                                            for ic in range(8):
                                                w = dmax(t_w, xfm_rd, fr[f"ps{bank}"]) \
                                                    if (jv == 0 and ic == 0) else {}
                                                t_pe = p.op("tensor", partial(
                                                    I_mm,
                                                    out=ps[bank][:, jv * 128:(jv + 1) * 128],
                                                    lhsT=xfm[sl][:, ic * GT + tc * 128:
                                                                 ic * GT + tc * 128 + 128],
                                                    rhs=wqb[wsl][:, jv * 1024 + ic * 128:
                                                                 jv * 1024 + ic * 128 + 128],
                                                    start=(ic == 0), stop=(ic == 7)),
                                                    wait=w,
                                                    inc=[("pe", 1)]
                                                    if (jv == 3 and ic == 7) else None)
                                        fr[f"wqb{wsl}"] = dmax(fr[f"wqb{wsl}"], t_pe)
                                        t = p.op("scalar", partial(
                                            I_act,
                                            out=AP(vsb, slot * 1040 + vh * 520,
                                                   [[1040 * NQC, 128], [65, 8],
                                                    [1, 64]]),
                                            in_=ps[bank][:, :].rearrange(
                                                "p (h c) -> p h c", h=8),
                                            func=AF.Identity,
                                            bias=0.0, scale=1.0),
                                            wait=dmax(t_pe, rd["vsb_ones"]),
                                            inc=[("act", 1)])
                                        fr[f"ps{bank}"] = t

                            fr[f"xfm{sl}"] = {"pe": cnt["pe"]}
                            if g + 1 < NG:
                                sched_group_loads(g + 1)

                        p.drain()
                        p.emit()
                    nc.all_engine_barrier()

                    with contextlib.ExitStack() as sca:
                        Ea = sca.enter_context
                        woT = Ea(nc.sbuf_tensor(f"woT_{si}", [128, 8 * D], BF16))
                        qpadb = [Ea(nc.sbuf_tensor(f"qpad{i}_{si}", [128, H * 128], BF16))
                                 for i in range(2)]
                        atb = [Ea(nc.sbuf_tensor(f"atb{i}_{si}", [128, 512], BF16))
                               for i in range(4)]
                        fcols = Ea(nc.sbuf_tensor(f"fcols_{si}", [128, 16], F32))
                        otm = [Ea(nc.sbuf_tensor(f"otm{i}_{si}", [128, D], BF16))
                               for i in range(2)]
                        ofm = [Ea(nc.sbuf_tensor(f"ofm{i}_{si}", [128, D], BF16))
                               for i in range(3)]
                        xbf_ = [Ea(nc.sbuf_tensor(f"xb{i}_{si}", [128, D], F32))
                                for i in range(3)]
                        x1pre = [Ea(nc.sbuf_tensor(f"x1pre{i}_{si}", [128, D], F32))
                                 for i in range(2)]
                        sqv = Ea(nc.sbuf_tensor(f"sqv_{si}", [128, D], F32))
                        x1btm = [Ea(nc.sbuf_tensor(f"x1btm{i}_{si}", [128, D], BF16))
                                 for i in range(2)]

                        p = Prog(nc, sems, cnt)
                        for i in range(3):
                            for nm in ("ofm", "xb"):
                                fr.setdefault(f"{nm}{i}", {})
                        for i in range(2):
                            for nm in ("otm", "x1pre", "x1btm", "qpadb",
                                       "stats", "gscr"):
                                fr.setdefault(f"{nm}{i}", {})
                        for i in range(4):
                            fr[f"atb{i}"] = {}
                        fr["sqv"] = {}

                        for icw in range(8):
                            rd["woT"] = p.op("gpsimd", partial(
                                I_dma,
                                out=woT[:, icw * 1024:(icw + 1) * 1024],
                                in_=wt_d.ap()[:, 24 * 1024 + icw * 128:32 * 1024]
                                .rearrange("p (oc f) -> p oc f", f=128)[:, ::8, :]
                                if False else
                                AP(wt_d, 24 * 1024 + icw * 128,
                                   [[32768, 128], [1024, 8], [1, 128]])),
                                wait=rd["wt_dram"], inc=[("wot", 16)])
                        t_qz = {}
                        for i in range(2):
                            t_qz = p.op("vector", partial(I_memset, ap=qpadb[i][:, :], c=0.0),
                                        inc=[("dve", 1)])
                        rd["qpad_z"] = t_qz

                        ctx = {}

                        qprep_t = {}
                        ast = {}
                        bctx = {}
                        b2ctx = {}

                        def sched_qprep(qc):
                            if qc >= NQC:
                                return
                            qsl = qc % 2
                            t_qp = {}
                            for par in range(2):
                                r0 = 64 * par
                                t_qp = p.op("sync", partial(
                                    I_dma,
                                    out=AP(qpadb[qsl],
                                           r0 * (H * 128) + par * 128,
                                           [[H * 128, 64], [256, 8], [1, 128]]),
                                    in_=AP(qg, r0 * (8 * S) + qc * 128,
                                           [[8 * S, 64], [S, 8], [1, 128]])),
                                    wait=dmax(rd["qpad_z"], fr[f"qpadb{qsl}"])
                                    if par == 0 else {},
                                    inc=[(f"qp{qsl}", 16)])
                            qprep_t[qc] = t_qp

                        def sched_xload(qc):
                            qs4 = qc % 3
                            row0 = si * S + qc * 128
                            ctx[qc] = p.op("scalar", partial(
                                I_dma, out=xbf_[qs4][:, :],
                                in_=x_d.ap()[row0:row0 + 128, :]),
                                wait=fr[f"xb{qs4}"], inc=[(f"xf{qc % 2}", 16)])

                        LAG = 3

                        def stage_a1(qc):
                            qsl = qc % 2
                            qprep = qprep_t.pop(qc)

                            nkc = min(qc + 3, NQC)
                            kcs = list(range(nkc))
                            blocks = [kcs[i:i + 4] for i in range(0, nkc, 4)]
                            nblk = len(blocks)
                            has_mask = (qc + 2 <= NQC - 1)

                            units = [(h, bi) for h in range(H) for bi in range(nblk)]
                            st_post = {}
                            msk_post = {}
                            av_post = [None] * H

                            def sched_st(u):
                                h, bi = units[u]
                                blk = blocks[bi]
                                bank = u % 2
                                t_pe = {}
                                for j, kc in enumerate(blk):
                                    w = dmax(fr[f"ps{bank}"], qprep) if j == 0 else {}
                                    t_pe = p.op("tensor", partial(
                                        I_mm, out=ps[bank][:, j * 128:(j + 1) * 128],
                                        lhsT=kfm[:, (h // 2) * S + kc * 128:
                                                  (h // 2) * S + kc * 128 + 128],
                                        rhs=qpadb[qsl][:, h * 128:(h + 1) * 128],
                                        start=True, stop=True),
                                        wait=w,
                                        inc=[("pe", 1)] if j == len(blk) - 1 else None)
                                st_post[u] = t_pe

                            def sched_exp_mask(u):
                                h, bi = units[u]
                                blk = blocks[bi]
                                bank = u % 2
                                asl = u % 4
                                t = p.op("scalar", partial(
                                    I_act, out=atb[asl][:, 0:len(blk) * 128],
                                    in_=ps[bank][:, 0:len(blk) * 128], func=AF.Exp,
                                    bias=0.0, scale=0.125),
                                    wait=dmax(st_post[u], fr[f"atb{asl}"]),
                                    inc=[("act", 1)])
                                fr[f"ps{bank}"] = t
                                if has_mask and bi == nblk - 1:
                                    j = len(blk) - 1
                                    t = p.op("vector", partial(
                                        I_tt, out=atb[asl][:, j * 128:(j + 1) * 128],
                                        a=atb[asl][:, j * 128:(j + 1) * 128],
                                        b=maskR[:, :], op=ALU.mult),
                                        wait=dmax(t, rd["masks"]), inc=[("dve", 1)])
                                msk_post[u] = t

                            def sched_av(u):
                                h, bi = units[u]
                                blk = blocks[bi]
                                asl = u % 4
                                obank = 2 + (h // 7 if h < 14 else 2)
                                idx = h - (0, 7, 14)[obank - 2]
                                first = (bi == 0)
                                last = (bi == nblk - 1)
                                t_pe = {}
                                for j, kc in enumerate(blk):
                                    w = {}
                                    if j == 0:
                                        w = dmax(msk_post[u],
                                                 (fr[f"ps{obank}"] if (idx == 0 and first)
                                                  else {}))
                                    t_pe = p.op("tensor", partial(
                                        I_mm,
                                        out=ps[obank][:, idx * 65:idx * 65 + 65],
                                        lhsT=atb[asl][:, j * 128:(j + 1) * 128],
                                        rhs=vsb[:, kc * 1040 + h * 65:
                                                kc * 1040 + h * 65 + 65],
                                        start=(first and j == 0),
                                        stop=(last and j == len(blk) - 1)),
                                        wait=w,
                                        inc=[("pe", 1)] if j == len(blk) - 1 else None)
                                fr[f"atb{asl}"] = t_pe
                                if last:
                                    av_post[h] = t_pe

                            S_ = dict(units=units, st=sched_st, em=sched_exp_mask,
                                      av=sched_av, av_post=av_post)
                            ast[qc] = S_
                            K = min(8, len(units))
                            for u in range(K):
                                sched_st(u)
                                sched_exp_mask(u)
                                if u >= LAG:
                                    sched_av(u - LAG)
                            S_["next"] = K

                        def stage_a2_units(qc):
                            qsl = qc % 2
                            S_ = ast[qc]
                            units = S_["units"]
                            sched_st, sched_exp_mask, sched_av = \
                                S_["st"], S_["em"], S_["av"]
                            for u in range(S_["next"], len(units)):
                                sched_st(u)
                                sched_exp_mask(u)
                                if u >= LAG:
                                    sched_av(u - LAG)
                            for u in range(max(0, len(units) - LAG), len(units)):
                                sched_av(u)
                            fr[f"qpadb{qsl}"] = {"pe": cnt["pe"]}

                        def stage_a2_evac(qc):
                            qsl = qc % 2
                            S_ = ast.pop(qc)
                            av_post = S_["av_post"]
                            rr = [None] * 3
                            for b in range(3):
                                nh = (7, 7, 2)[b]
                                h0 = (0, 7, 14)[b]
                                rr[b] = p.op("vector", partial(
                                    I_recip, out=fcols[:, h0:h0 + nh],
                                    in_=AP(ps[2 + b], 64, [[512, 128], [65, nh]])),
                                    wait=av_post[h0 + nh - 1], inc=[("dve", 1)])
                            t_ev = {}
                            for h in range(H):
                                b = 0 if h < 7 else (1 if h < 14 else 2)
                                idx = h - (0, 7, 14)[b]
                                t_ev = p.op("vector", partial(
                                    I_ts, out=otm[qsl][:, h * 64:(h + 1) * 64],
                                    in_=ps[2 + b][:, idx * 65:idx * 65 + 64],
                                    s1=fcols[:, h:h + 1], op0=ALU.mult),
                                    wait=dmax(rr[b],
                                              fr[f"otm{qsl}"] if h == 0 else {}),
                                    inc=[("dve", 1)])
                            fr["ps2"] = t_ev
                            fr["ps3"] = t_ev
                            fr["ps4"] = t_ev
                            rd[f"otm{qsl}"] = t_ev

                        def stage_tr(qc):
                            qsl = qc % 2
                            qs4 = qc % 3
                            t_tr = p.op("sync", partial(
                                I_dmat,
                                out=ofm[qs4][:, :].rearrange("p (ic f) -> p ic f", ic=8),
                                in_=otm[qsl][:, :].rearrange("p (ic f) -> p ic f", ic=8)),
                                wait=dmax(rd[f"otm{qsl}"], fr[f"ofm{qs4}"]),
                                inc=[(f"of{qs4}", 16)])
                            fr[f"otm{qsl}"] = t_tr
                            rd[f"ofm{qs4}"] = t_tr

                        def stage_b1_pe(qc):
                            qs4 = qc % 3
                            bst = {}
                            for half in range(2):
                                bank = 6 + half
                                t_pe = {}
                                for ic in range(8):
                                    w = dmax(rd[f"ofm{qs4}"], rd["woT"],
                                             fr[f"ps{bank}"]) if ic == 0 else {}
                                    t_pe = p.op("tensor", partial(
                                        I_mm, out=ps[bank][:, :],
                                        lhsT=ofm[qs4][:, ic * 128:(ic + 1) * 128],
                                        rhs=woT[:, ic * D + half * 512:
                                                ic * D + half * 512 + 512],
                                        start=(ic == 0), stop=(ic == 7)),
                                        wait=w, inc=[("pe", 1)] if ic == 7 else None)
                                bst[half] = t_pe
                            fr[f"ofm{qs4}"] = {"pe": cnt["pe"]}
                            bctx[qc] = bst

                        def stage_b1_stt(qc):
                            sl = qc % 2
                            qs4 = qc % 3
                            bst = bctx[qc]
                            t_xb = p.op("vector", partial(
                                I_tt, out=xbf_[qs4][:, :], a=xbf_[qs4][:, :],
                                b=beff_rep[:, :], op=ALU.add),
                                wait=dmax(ctx.pop(qc), rd["beff"]), inc=[("dve", 1)])
                            t_stt = {}
                            for half in range(2):
                                bank = 6 + half
                                t_stt = p.op("vector", partial(
                                    I_tt, out=x1pre[sl][:, half * 512:(half + 1) * 512],
                                    a=ps[bank][:, :],
                                    b=xbf_[qs4][:, half * 512:(half + 1) * 512],
                                    op=ALU.add),
                                    wait=dmax(bst[half], t_xb,
                                              fr[f"x1pre{sl}"] if half == 0 else {}),
                                    inc=[("dve", 1)])
                                fr[f"ps{bank}"] = t_stt
                            fr[f"xb{qs4}"] = t_stt
                            bst["stt"] = t_stt

                        def stage_b1_stats(qc):
                            sl = qc % 2
                            bst = bctx[qc]
                            st_ = statsb[sl]
                            t = p.op("vector", partial(
                                I_red, out=st_[:, 0:1], in_=x1pre[sl][:, :],
                                op=ALU.add, negate=True),
                                wait=dmax(bst["stt"], fr[f"stats{sl}"]), inc=[("dve", 1)])
                            t = p.op("vector", partial(
                                I_ts, out=st_[:, 1:2], in_=st_[:, 0:1],
                                s1=1.0 / D, op0=ALU.mult), wait=t, inc=[("dve", 1)])
                            t_sq = p.op("vector", partial(
                                I_tt, out=sqv[:, :], a=x1pre[sl][:, :],
                                b=x1pre[sl][:, :], op=ALU.mult),
                                wait=dmax(bst["stt"], fr["sqv"]), inc=[("dve", 1)])
                            t_ss = p.op("vector", partial(
                                I_red, out=st_[:, 2:3], in_=sqv[:, :],
                                op=ALU.add), wait=t_sq, inc=[("dve", 1)])
                            fr["sqv"] = t_ss
                            t_m2 = p.op("vector", partial(
                                I_tt, out=st_[:, 3:4], a=st_[:, 1:2], b=st_[:, 1:2],
                                op=ALU.mult), wait=t, inc=[("dve", 1)])
                            t_var = p.op("vector", partial(
                                I_stt, out=st_[:, 4:5], in0=st_[:, 2:3], scal=1.0 / D,
                                in1=st_[:, 3:4], op0=ALU.mult, op1=ALU.subtract),
                                wait=dmax(t_ss, t_m2), inc=[("dve", 1)])
                            bst["var"] = t_var

                        def stage_b1_tail(qc):
                            sl = qc % 2
                            row0 = si * S + qc * 128
                            bst = bctx.pop(qc)
                            st_ = statsb[sl]
                            t_ln1 = p.op("scalar", partial(
                                I_act, out=st_[:, 5:6], in_=st_[:, 4:5], func=AF.Ln,
                                bias=eps_sb[:, :], scale=1.0),
                                wait=dmax(bst["var"], rd["eps"]), inc=[("act", 1)])
                            t_rstd = p.op("scalar", partial(
                                I_act, out=st_[:, 6:7], in_=st_[:, 5:6], func=AF.Exp,
                                bias=0.0, scale=-0.5), wait=t_ln1, inc=[("act", 1)])
                            t_bln = p.op("scalar", partial(
                                I_act, out=st_[:, 7:8], in_=st_[:, 1:2],
                                func=AF.Identity, bias=0.0,
                                scale=st_[:, 6:7]), wait=t_rstd, inc=[("act", 1)])
                            t_x1 = p.op("scalar", partial(
                                I_act, out=x1pre[sl][:, :], in_=x1pre[sl][:, :],
                                func=AF.Identity, bias=st_[:, 7:8],
                                scale=st_[:, 6:7]),
                                wait=t_bln, inc=[("act", 1)])
                            fr[f"stats{sl}"] = t_x1
                            t_x1b = p.op("vector", partial(
                                I_copy, out=x1btm[sl][:, :], in_=x1pre[sl][:, :]),
                                wait=dmax(t_x1, fr[f"x1btm{sl}"]), inc=[("dve", 1)])
                            t_park = p.op("scalar", partial(
                                I_dma, out=x1_d.ap()[row0:row0 + 128, :],
                                in_=x1pre[sl][:, :]), wait=t_x1,
                                inc=[(f"park{qc % 2}", 16)])
                            rd[f"x1park{qc}"] = t_park
                            fr[f"x1pre{sl}"] = dmax(t_x1b, t_park)
                            t_tr = {}
                            for ic in range(8):
                                t_tr = p.op("sync", partial(
                                    I_dmat,
                                    out=X1b[:, ic * S + qc * 128: ic * S + qc * 128 + 128],
                                    in_=x1btm[sl][:, ic * 128:(ic + 1) * 128]),
                                    wait=t_x1b, inc=[(f"x1t{sl}", 16)])
                            fr[f"x1btm{sl}"] = t_tr
                            rd[f"X1b{qc}"] = t_tr

                        def stage_b2_pe(qc):
                            t_pe = {}
                            for ic in range(8):
                                w = dmax(rd[f"X1b{qc}"], rd["rw"],
                                         fr["ps5"]) if ic == 0 else {}
                                t_pe = p.op("tensor", partial(
                                    I_mm, out=ps[5][:, 0:NE],
                                    lhsT=X1b[:, ic * S + qc * 128: ic * S + qc * 128 + 128],
                                    rhs=rw_sb[:, ic * NE:(ic + 1) * NE],
                                    start=(ic == 0), stop=(ic == 7)),
                                    wait=w, inc=[("pe", 1)] if ic == 7 else None)
                            b2ctx[qc] = t_pe

                        def stage_b2_rest(qc):
                            qsl = qc % 2
                            gs = gscr[qsl]
                            t_pe = b2ctx.pop(qc)
                            t = p.op("vector", partial(
                                I_tt, out=gs[:, 0:8], a=ps[5][:, 0:NE], b=rb_rep[:, :],
                                op=ALU.add),
                                wait=dmax(t_pe, rd["rb"], fr[f"gscr{qsl}"]),
                                inc=[("dve", 1)])
                            fr["ps5"] = t
                            t_nm = p.op("vector", partial(
                                I_red, out=gs[:, 8:9], in_=gs[:, 0:8], op=ALU.max,
                                negate=True), wait=t, inc=[("dve", 1)])
                            t_mk = p.op("vector", partial(
                                I_ts, out=gs[:, 16:24], in_=gs[:, 0:8],
                                s1=gs[:, 8:9], s2=0.0, op0=ALU.add, op1=ALU.is_ge),
                                wait=t_nm, inc=[("dve", 1)])
                            t_e = p.op("scalar", partial(
                                I_act, out=gs[:, 24:32], in_=gs[:, 0:8], func=AF.Exp,
                                bias=gs[:, 8:9], scale=1.0), wait=t_nm, inc=[("act", 1)])
                            t_se = p.op("vector", partial(
                                I_red, out=gs[:, 9:10], in_=gs[:, 24:32], op=ALU.add),
                                wait=t_e, inc=[("dve", 1)])
                            t_rse = p.op("vector", partial(
                                I_recip, out=gs[:, 10:11], in_=gs[:, 9:10]),
                                wait=t_se, inc=[("dve", 1)])
                            t_gm = p.op("vector", partial(
                                I_tt, out=gs[:, 32:40], a=gs[:, 24:32], b=gs[:, 16:24],
                                op=ALU.mult), wait=dmax(t_mk, t_e), inc=[("dve", 1)])
                            t_g = p.op("vector", partial(
                                I_ts, out=gates_tm[:, qc * NE:(qc + 1) * NE],
                                in_=gs[:, 32:40], s1=gs[:, 10:11], op0=ALU.mult),
                                wait=dmax(t_gm, t_rse), inc=[("dve", 1)])
                            rd[f"gates{qc}"] = t_g
                            gsl = qc % 2
                            t_gb = p.op("vector", partial(
                                I_copy, out=gpad[gsl][:, 0:NE],
                                in_=gates_tm[:, qc * NE:(qc + 1) * NE]),
                                wait=dmax(t_g, rd["gpad_z"], fr[f"gpad{gsl}"]),
                                inc=[("dve", 1)])
                            t_gt = p.op("sync", partial(
                                I_dmat, out=gfm[:, qc * 128:(qc + 1) * 128],
                                in_=gpad[gsl][:, :]),
                                wait=t_gb, inc=[(f"gf{gsl}", 16)])
                            fr[f"gpad{gsl}"] = t_gt
                            fr[f"gscr{qsl}"] = t_gt
                            rd[f"gfm{qc}"] = t_gt

                        sched_qprep(0)
                        sched_qprep(1)
                        sched_xload(0)
                        sched_xload(1)
                        stage_a1(0)
                        stage_a2_units(0)
                        stage_a2_evac(0)
                        stage_a1(1)
                        stage_a2_units(1)
                        stage_a2_evac(1)
                        sched_qprep(2)
                        stage_tr(0)
                        stage_tr(1)
                        for it in range(NQC + 3):
                            if 0 <= it - 1 < NQC:
                                stage_b1_pe(it - 1)
                            if 0 <= it - 2 < NQC:
                                stage_b1_stats(it - 2)
                            if it + 2 < NQC:
                                stage_a1(it + 2)
                            if 0 <= it - 1 < NQC:
                                stage_b1_stt(it - 1)
                            if 0 <= it - 2 < NQC:
                                stage_b1_tail(it - 2)
                            if 0 <= it - 3 < NQC:
                                stage_b2_pe(it - 3)
                                stage_b2_rest(it - 3)
                            if it + 2 < NQC:
                                stage_a2_units(it + 2)
                            if it + 3 < NQC:
                                sched_qprep(it + 3)
                            if it + 2 < NQC:
                                stage_a2_evac(it + 2)
                                sched_xload(it + 2)
                                stage_tr(it + 2)

                        p.drain()
                        p.emit()
                    nc.all_engine_barrier()

                with contextlib.ExitStack() as scf:
                    Ef = scf.enter_context
                    acc = Ef(nc.sbuf_tensor(f"acc_{si}", [128, NQC * D], F32))
                    w1b = [Ef(nc.sbuf_tensor(f"w1b{i}_{si}", [128, 8 * 1024], BF16))
                           for i in range(2)]
                    w2b = [Ef(nc.sbuf_tensor(f"w2b{i}_{si}", [128, 8 * 1024], BF16))
                           for i in range(2)]
                    hsc = [Ef(nc.sbuf_tensor(f"hsc{i}_{si}", [128, 8 * 512], BF16))
                           for i in range(2)]
                    outsb = [Ef(nc.sbuf_tensor(f"outsb{i}_{si}", [128, D], F32))
                             for i in range(2)]

                    p = Prog(nc, sems, cnt)
                    for i in range(2):
                        fr[f"w1b{i}"] = {}
                        fr[f"w2b{i}"] = {}
                        fr[f"hsc{i}"] = {}
                        fr[f"outsb{i}"] = {}

                    ginit = {}

                    def sched_accinit_group(g):
                        t_l = {}
                        for qc in range(g * 4, g * 4 + 4):
                            row0 = si * S + qc * 128
                            t_l = p.op("scalar", partial(
                                I_dma, out=acc[:, qc * D:(qc + 1) * D],
                                in_=x1_d.ap()[row0:row0 + 128, :]),
                                wait=rd[f"x1park{qc}"], inc=[("accl", 16)])
                        tg = {}
                        for qc in range(g * 4, g * 4 + 4):
                            for half in range(2):
                                bank = 6 + half
                                t_pe = p.op("tensor", partial(
                                    I_mm, out=ps[bank][:, :],
                                    lhsT=gfm[:, qc * 128:(qc + 1) * 128],
                                    rhs=b2pad[:, half * 512:(half + 1) * 512],
                                    start=True, stop=True),
                                    wait=dmax(rd[f"gfm{qc}"], rd["b2pad"],
                                              fr[f"ps{bank}"]),
                                    inc=[("pe", 1)])
                                tg = p.op("vector", partial(
                                    I_stt, out=acc[:, qc * D + half * 512:
                                                   qc * D + half * 512 + 512],
                                    in0=ps[bank][:, :], scal=1.0,
                                    in1=acc[:, qc * D + half * 512:
                                            qc * D + half * 512 + 512],
                                    op0=ALU.mult, op1=ALU.add),
                                    wait=dmax(t_pe, t_l), inc=[("dve", 1)])
                                fr[f"ps{bank}"] = tg
                        ginit[g] = tg

                    def sched_wload(e, b, wsl):
                        t1 = p.op("gpsimd", partial(
                            I_dma, out=w1b[wsl][:, :].rearrange("p (c f) -> p c f", c=8),
                            in_=w1_d.ap()[e, :, b * 1024:(b + 1) * 1024].rearrange(
                                "(c p) f -> p c f", p=128)),
                            wait=fr[f"w1b{wsl}"], inc=[(f"wa{wsl}", 16)])
                        t2 = p.op("gpsimd", partial(
                            I_dma, out=w2b[wsl][:, :].rearrange("p (m o) -> p m o", m=8),
                            in_=w2_d.ap()[e, b * 1024:(b + 1) * 1024, :].rearrange(
                                "(m p) o -> p m o", p=128)),
                            wait=fr[f"w2b{wsl}"], inc=[(f"wb{wsl}", 16)])
                        return t1, t2

                    eb_list = [(e, b) for e in range(NE) for b in range(NB)]
                    w_rd = {}
                    w_rd[0] = sched_wload(*eb_list[0], 0)
                    for idx, (e, b) in enumerate(eb_list):
                        wsl = idx % 2
                        if idx + 1 < len(eb_list):
                            w_rd[idx + 1] = sched_wload(*eb_list[idx + 1], (idx + 1) % 2)
                        t_w1, t_w2 = w_rd[idx]
                        for tt in range(4):
                            if idx == 0:
                                sched_accinit_group(tt)
                            hsl = tt % 2
                            h_done = {}
                            for m in range(8):
                                bank = m % 2
                                t_pe = {}
                                for dc in range(8):
                                    w = dmax(t_w1, fr[f"ps{bank}"],
                                             rd[f"X1b{tt * 4}"], rd[f"X1b{tt * 4 + 1}"],
                                             rd[f"X1b{tt * 4 + 2}"], rd[f"X1b{tt * 4 + 3}"],
                                             fr[f"hsc{hsl}"] if m == 0 else {}) \
                                        if dc == 0 else {}
                                    t_pe = p.op("tensor", partial(
                                        I_mm, out=ps[bank][:, :],
                                        lhsT=w1b[wsl][:, dc * 1024 + m * 128:
                                                      dc * 1024 + m * 128 + 128],
                                        rhs=X1b[:, dc * S + tt * 512:
                                                dc * S + tt * 512 + 512],
                                        start=(dc == 0), stop=(dc == 7)),
                                        wait=w, inc=[("pe", 1)] if dc == 7 else None)
                                t_h = p.op("scalar", partial(
                                    I_act, out=hsc[hsl][:, m * 512:(m + 1) * 512],
                                    in_=ps[bank][:, :], func=AF.Relu,
                                    bias=b1Tf[:, (b * 8 + m) * NE + e:
                                              (b * 8 + m) * NE + e + 1],
                                    scale=1.0),
                                    wait=dmax(t_pe, rd["b1T"]), inc=[("act", 1)])
                                fr[f"ps{bank}"] = t_h
                                h_done = t_h
                            rd[f"hsc{hsl}"] = h_done
                            last_y = {}
                            for ts_ in range(4):
                                qc = tt * 4 + ts_
                                for half in range(2):
                                    bank = 2 + (ts_ * 2 + half) % 3
                                    t_pe = {}
                                    for m in range(8):
                                        w = dmax(rd[f"hsc{hsl}"], t_w2,
                                                 fr[f"ps{bank}"]) if m == 0 else {}
                                        t_pe = p.op("tensor", partial(
                                            I_mm, out=ps[bank][:, :],
                                            lhsT=hsc[hsl][:, m * 512 + ts_ * 128:
                                                          m * 512 + ts_ * 128 + 128],
                                            rhs=w2b[wsl][:, m * 1024 + half * 512:
                                                         m * 1024 + half * 512 + 512],
                                            start=(m == 0), stop=(m == 7)),
                                            wait=w, inc=[("pe", 1)] if m == 7 else None)
                                    t_y = p.op("vector", partial(
                                        I_stt,
                                        out=acc[:, qc * D + half * 512:
                                                qc * D + half * 512 + 512],
                                        in0=ps[bank][:, :],
                                        scal=gates_tm[:, qc * NE + e:qc * NE + e + 1],
                                        in1=acc[:, qc * D + half * 512:
                                                qc * D + half * 512 + 512],
                                        op0=ALU.mult, op1=ALU.add),
                                        wait=dmax(t_pe, rd[f"gates{qc}"],
                                                  ginit[tt]),
                                        inc=[("dve", 1)])
                                    fr[f"ps{bank}"] = t_y
                                    last_y = t_y
                            fr[f"hsc{hsl}"] = last_y
                        fr[f"w1b{wsl}"] = last_y
                        fr[f"w2b{wsl}"] = last_y

                    acc_done = last_y
                    t_out = {}
                    for qc in range(NQC):
                        qsl = qc % 2
                        st_ = statsb[qsl]
                        a_ap = acc[:, qc * D:(qc + 1) * D]
                        t = p.op("vector", partial(
                            I_red, out=st_[:, 0:1], in_=a_ap, op=ALU.add, negate=True),
                            wait=dmax(acc_done, fr[f"stats{qsl}"]), inc=[("dve", 1)])
                        t = p.op("vector", partial(
                            I_ts, out=st_[:, 1:2], in_=st_[:, 0:1], s1=1.0 / D,
                            op0=ALU.mult), wait=t, inc=[("dve", 1)])
                        t_sq = p.op("vector", partial(
                            I_tt, out=outsb[qsl][:, :], a=a_ap, b=a_ap, op=ALU.mult),
                            wait=dmax(t, fr[f"outsb{qsl}"]), inc=[("dve", 1)])
                        t_ss = p.op("vector", partial(
                            I_red, out=st_[:, 2:3], in_=outsb[qsl][:, :], op=ALU.add),
                            wait=t_sq, inc=[("dve", 1)])
                        t_m2 = p.op("vector", partial(
                            I_tt, out=st_[:, 3:4], a=st_[:, 1:2], b=st_[:, 1:2],
                            op=ALU.mult), wait=t, inc=[("dve", 1)])
                        t_var = p.op("vector", partial(
                            I_stt, out=st_[:, 4:5], in0=st_[:, 2:3], scal=1.0 / D,
                            in1=st_[:, 3:4], op0=ALU.mult, op1=ALU.subtract),
                            wait=dmax(t_ss, t_m2), inc=[("dve", 1)])
                        t_ln = p.op("scalar", partial(
                            I_act, out=st_[:, 5:6], in_=st_[:, 4:5], func=AF.Ln,
                            bias=eps_sb[:, :], scale=1.0),
                            wait=dmax(t_var, rd["eps"]), inc=[("act", 1)])
                        t_rstd = p.op("scalar", partial(
                            I_act, out=st_[:, 6:7], in_=st_[:, 5:6], func=AF.Exp,
                            bias=0.0, scale=-0.5), wait=t_ln, inc=[("act", 1)])
                        t_bln = p.op("vector", partial(
                            I_tt, out=st_[:, 7:8], a=st_[:, 1:2], b=st_[:, 6:7],
                            op=ALU.mult), wait=t_rstd, inc=[("dve", 1)])
                        t_x2 = p.op("scalar", partial(
                            I_act, out=outsb[qsl][:, :], in_=a_ap, func=AF.Identity,
                            bias=st_[:, 7:8], scale=st_[:, 6:7]),
                            wait=t_bln, inc=[("act", 1)])
                        fr[f"stats{qsl}"] = t_x2
                        row0 = si * S + qc * 128
                        t_out = p.op("sync", partial(
                            I_dma, out=y_d.ap()[row0:row0 + 128, :],
                            in_=outsb[qsl][:, :]), wait=t_x2, inc=[("out", 16)])
                        fr[f"outsb{qsl}"] = t_out

                    if si == SEQS - 1:
                        p.op("gpsimd", None, wait={"out": cnt["out"]})
                    p.drain()
                    p.emit()
                nc.all_engine_barrier()

    return nc


_NC_CACHE = []


def kernel(**inputs):
    x = np.ascontiguousarray(np.asarray(inputs["L0_attn_x"], dtype=np.float32))
    full = {
        "in_proj_w": np.ascontiguousarray(np.asarray(inputs["L0_attn_in_proj_w"], np.float32)),
        "in_proj_b": np.ascontiguousarray(np.asarray(inputs["L0_attn_in_proj_b"], np.float32)),
        "out_proj_w": np.ascontiguousarray(np.asarray(inputs["L0_attn_out_proj_w"], np.float32)),
        "out_proj_b": np.ascontiguousarray(np.asarray(inputs["L0_attn_out_proj_b"], np.float32)),
        "router_W": np.ascontiguousarray(np.asarray(inputs["L0_ffn_router_W"], np.float32)),
        "router_b": np.ascontiguousarray(np.asarray(inputs["L0_ffn_router_b"], np.float32)),
        "W1": np.ascontiguousarray(np.asarray(inputs["L0_ffn_W1"], np.float32)),
        "b1": np.ascontiguousarray(np.asarray(inputs["L0_ffn_b1"], np.float32)),
        "W2": np.ascontiguousarray(np.asarray(inputs["L0_ffn_W2"], np.float32)),
        "b2": np.ascontiguousarray(np.asarray(inputs["L0_ffn_b2"], np.float32)),
    }
    if not _NC_CACHE:
        _NC_CACHE.append(build_program())
    nc = _NC_CACHE[0]
    in_maps = []
    for c in range(NCORES):
        m = {"x": x[SEQS * c:SEQS * (c + 1)].reshape(SEQS * S, D)}
        m.update(full)
        in_maps.append(m)
    res = run_bass_kernel_spmd(nc, in_maps, core_ids=list(range(NCORES)))
    out = np.concatenate(
        [res.results[c]["y"].reshape(SEQS, S, D) for c in range(NCORES)], axis=0)
    return out


if __name__ == "__main__":
    nc = build_program()
    print("instructions:", nc.n_instructions())
